# version 01
# baseline (speedup 1.0000x reference)
import numpy as np
from contextlib import ExitStack
import concourse.bass as bass
import concourse.mybir as mybir
from concourse.bass_utils import run_bass_kernel_spmd

F32 = mybir.dt.float32
BF16 = mybir.dt.bfloat16
AF = mybir.ActivationFunctionType
ALU = mybir.AluOpType
AX = mybir.AxisListType

T, D, H, DH = 4096, 2048, 16, 128
E, MD = 16, 1408
NKB = D // 128
NMB = MD // 128
NT = T // 128
NN = T // 512
TANH_SC = float((1.0 / np.sqrt(128.0)) / 50.0)
EXP_SC = 50.0


def build(phases=6, dump=(), p2sub=2, npr=8, ntt=NT, p3sub=3, nt3=NT, p3x=0):
    nc = bass.Bass("TRN2", target_bir_lowering=False)
    dtn = nc.dram_tensor
    x_h = dtn("x", [T, D], F32, kind="ExternalInput").ap()
    wqT_h = dtn("wqT", [D, D], F32, kind="ExternalInput").ap()
    wkT_h = dtn("wkT", [D, D], F32, kind="ExternalInput").ap()
    wvT_h = dtn("wvT", [D, D], F32, kind="ExternalInput").ap()
    woT_h = dtn("woT", [D, D], F32, kind="ExternalInput").ap()
    gwT_h = dtn("gwT", [D, E], F32, kind="ExternalInput").ap()
    wgT_h = dtn("wgT", [E, D, MD], F32, kind="ExternalInput").ap()
    wuT_h = dtn("wuT", [E, D, MD], F32, kind="ExternalInput").ap()
    wdT_h = dtn("wdT", [E, MD, D], F32, kind="ExternalInput").ap()
    rows_h = dtn("rows", [8, D], F32, kind="ExternalInput").ap()
    masks_h = dtn("masks", [128, 2048], BF16, kind="ExternalInput").ap()
    identb_h = dtn("identb", [128, 128], BF16, kind="ExternalInput").ap()
    identf_h = dtn("identf", [128, 128], F32, kind="ExternalInput").ap()
    onescb_h = dtn("onescb", [128, 1], BF16, kind="ExternalInput").ap()
    consts_h = dtn("consts", [128, 4], F32, kind="ExternalInput").ap()

    xT_h = dtn("xTs", [D, T], BF16, kind="Internal").ap()
    vtok_h = dtn("vtoks", [T, D], BF16, kind="Internal").ap()
    aoT_h = dtn("aoTs", [D, T], BF16, kind="Internal").ap()
    h1_h = dtn("h1s", [T, D], F32, kind="Internal").ap()
    hT_h = dtn("hTs", [D, T], BF16, kind="Internal").ap()
    wqB_h = dtn("wqB", [D, D], BF16, kind="Internal").ap()
    wkB_h = dtn("wkB", [D, D], BF16, kind="Internal").ap()
    woB_h = dtn("woB", [D, D], BF16, kind="Internal").ap()
    wgB_h = dtn("wgB", [E, NMB, 128, D], BF16, kind="Internal").ap()
    wuB_h = dtn("wuB", [E, NMB, 128, D], BF16, kind="Internal").ap()
    wdB_h = dtn("wdB", [E, MD, D], BF16, kind="Internal").ap()
    yac_h = dtn("yac", [T, D], F32, kind="Internal").ap()
    rsc_h = dtn("ropescratch", [4, D], F32, kind="Internal").ap()
    out_h = dtn("out", [T, D], F32, kind="ExternalOutput").ap()

    dump_h = {}
    for name, shp, dty in [
        ("d_xT", [D, T], BF16), ("d_vtok", [T, D], BF16),
        ("d_aoT", [D, T], BF16), ("d_h1", [T, D], F32),
        ("d_hT", [D, T], BF16), ("d_cw", [128, NT * E], F32),
        ("d_yac", [T, D], F32), ("d_wgB", [NMB, 128, D], BF16),
        ("d_wvsb", [128, NKB * D], BF16),
        ("d_qT", [128, 2 * T], BF16), ("d_kT", [128, 2 * T], BF16),
        ("d_pqs", [128, 512], F32), ("d_qro", [128, 256], BF16),
        ("d_wqB", [D, D], BF16), ("d_xtc2", [128, D], BF16), ("d_wqsl", [128, NKB * 256], BF16),
        ("d_msd", [128, 16], F32),
    ]:
        if name in dump:
            dump_h[name] = dtn(name, shp, dty, kind="ExternalOutput").ap()

    es = ExitStack()
    sb = lambda name, shape, dty: es.enter_context(nc.sbuf_tensor(name, shape, dty))

    cst = sb("cst", [128, 4], F32)
    idb = sb("idb", [128, 128], BF16)
    idf = sb("idf", [128, 128], F32)
    maskt = sb("maskt", [128, 2048], BF16)
    onescb = sb("s_onescb", [128, 1], BF16)
    onesrf = sb("onesrf", [1, 128], F32)
    ropeq = sb("ropeq", [128, 2 * D], BF16)
    ropek = sb("ropek", [128, 2 * D], BF16)
    ghi = sb("ghi", [128, NKB * E], BF16)
    glo = sb("glo", [128, NKB * E], BF16)
    gwf = sb("gwf", [128, NKB * E], F32)
    glof = sb("glof", [128, NKB * E], F32)
    cwsb = sb("cwsb", [128, NT * E], F32)
    setup_ps = ExitStack()
    rowsb = setup_ps.enter_context(nc.sbuf_tensor("rowsb", [1, 8 * D], F32))
    rwork = setup_ps.enter_context(nc.sbuf_tensor("rwork", [1, 5 * D], F32))

    sems = {}
    dma_sems = set()
    def S(name):
        if name not in sems:
            sems[name] = [es.enter_context(nc.semaphore(name)), 0]
        return sems[name]
    def inc(name, instr, n=1):
        s = S(name)
        instr.then_inc(s[0], n)
        s[1] += n
        if n >= 16:
            dma_sems.add(name)
        return instr
    def cur(name):
        return S(name)[1]
    def dinc(name, instr):
        return inc(name, instr, 16)
    waits_log = []
    def W(eng, name, val):
        waits_log.append((name, val))
        eng.wait_ge(S(name)[0], val)
    def fence(eng, instr, name="FEN"):
        inc(name, instr)
        W(eng, name, cur(name))
        return instr

    block = es.enter_context(nc.Block())
    nc.const_aps.register(F32, TANH_SC, cst[:, 0:1])
    nc.const_aps.register(F32, EXP_SC, cst[:, 1:2])
    nc.const_aps.register(F32, 128.0, cst[:, 2:3])
    nc.const_aps.register(F32, float(D), cst[:, 3:4])

    bar_n = [0]
    ENGINES = ["sync", "scalar", "vector", "tensor", "gpsimd"]
    def barrier():
        @block.sync
        def _(sy):
            for nm in sorted(dma_sems):
                W(sy, nm, cur(nm))
        bar_n[0] += 1
        tgt = 5 * 16 * bar_n[0]
        for en in ENGINES:
            @getattr(block, en)
            def _(eng):
                inc("BAR", eng.nop(), 16)
        for en in ENGINES:
            @getattr(block, en)
            def _(eng):
                W(eng, "BAR", tgt)

    @block.sync
    def _(sy):
        for dst, src in [
            (cst[:], consts_h), (idb[:], identb_h), (idf[:], identf_h),
            (maskt[:], masks_h), (onescb[:], onescb_h),
            (onesrf[0:1, :], rows_h[7:8, 0:128]),
            (rowsb[0:1, :], rows_h.rearrange("r d -> (r d)")[None, :]),
            (gwf[:].rearrange("p (kb e) -> p kb e", kb=NKB),
             gwT_h.rearrange("(kb p) e -> p kb e", p=128)),
        ]:
            dinc("SETL", sy.dma_start(out=dst, in_=src))
    @block.vector
    def _(v):
        W(v, "SETL", cur("SETL"))
        CE, SE, SG, WQ, WQS, WK, WKS = [rowsb[0:1, i * D:(i + 1) * D] for i in range(7)]
        TMP = rwork[0:1, 4 * D:5 * D]
        v.tensor_mul(rwork[0:1, 0:D], CE, WQ)
        fence(v, v.tensor_mul(TMP, SE, SG))
        v.tensor_mul(rwork[0:1, D:2 * D], TMP, WQS)
        v.tensor_mul(rwork[0:1, 2 * D:3 * D], CE, WK)
        v.tensor_mul(rwork[0:1, 3 * D:4 * D], TMP, WKS)
        fence(v, v.tensor_copy(ghi[:], gwf[:]))
        fence(v, v.tensor_sub(glof[:], gwf[:], ghi[:]))
        inc("SETV", v.tensor_copy(glo[:], glof[:]))
    @block.gpsimd
    def _(g):
        W(g, "SETV", cur("SETV"))
        dinc("SETG0", g.dma_start(out=rsc_h.rearrange("r d -> (r d)")[None, :], in_=rwork[0:1, 0:4 * D]))
        W(g, "SETG0", 16)
        for i, dst in enumerate([ropeq[:, 0:D], ropeq[:, D:2 * D], ropek[:, 0:D], ropek[:, D:2 * D]]):
            dinc("SETG", g.dma_start(out=dst, in_=rsc_h[i:i + 1, :].to_broadcast([128, D])))
    setup_ps.close()

    if phases >= 1:
        with ExitStack() as ps:
            psb = lambda name, shape, dty: ps.enter_context(nc.sbuf_tensor(name, shape, dty))
            wvsb = psb("wvsb", [128, NKB * D], BF16)
            xf = [psb(f"p1xf{i}", [128, D], F32) for i in range(2)]
            xb = [psb(f"p1xb{i}", [128, D], BF16) for i in range(2)]
            xtc = [psb(f"p1xtc{i}", [128, D], BF16) for i in range(2)]
            vst = [psb(f"p1vst{i}", [128, D], BF16) for i in range(2)]
            ptr = ps.enter_context(nc.psum_tensor("p1ptr", [128, 2048], BF16))
            pv = ps.enter_context(nc.psum_tensor("p1pv", [128, 2048], F32))

            @block.gpsimd
            def _(g):
                W(g, "SETL", cur("SETL"))
                for c in range(4):
                    dinc("WVL", g.dma_start(
                        out=wvsb[:].rearrange("p (kb n) -> p kb n", kb=NKB)[:, c * 4:(c + 1) * 4, :],
                        in_=wvT_h.rearrange("(kb p) n -> p kb n", p=128)[:, c * 4:(c + 1) * 4, :]))
                if "d_wvsb" in dump_h:
                    W(g, "P1SV", 16 * NT)
                    dinc("DMPW", g.dma_start(out=dump_h["d_wvsb"], in_=wvsb[:]))
                    W(g, "DMPW", 16)
            @block.sync
            def _(sy):
                W(sy, "SETG", 16 * 4)
                for t in range(NT):
                    if t >= 2:
                        W(sy, "P1XF", t - 1)
                    dinc("P1LX", sy.dma_start(out=xf[t % 2][:], in_=x_h[t * 128:(t + 1) * 128, :]))
            @block.vector
            def _(v):
                for t in range(NT):
                    b = t % 2
                    W(v, "P1LX", 16 * (t + 1))
                    if t >= 2:
                        W(v, "P1TR", t - 1)
                    inc("P1XF", v.tensor_copy(xb[b][:], xf[b][:]))
                    W(v, "P1TR", t + 1)
                    if t >= 2:
                        W(v, "P1SX", 16 * (t - 1))
                    inc("P1EV", v.tensor_copy(xtc[b][:], ptr[:]))
            @block.tensor
            def _(te):
                W(te, "WVL", 16 * 4)
                for t in range(NT):
                    b = t % 2
                    W(te, "P1XF", t + 1)
                    if t >= 1:
                        W(te, "P1EV", t)
                    for k in range(NKB):
                        instr = te.transpose(ptr[:, k * 128:(k + 1) * 128], xb[b][:, k * 128:(k + 1) * 128], idb[:])
                    inc("P1TR", instr)
                    W(te, "P1EV", t + 1)
                    if t >= 1:
                        W(te, "P1VE", t)
                    for d in range(4):
                        for k in range(NKB):
                            instr = te.matmul(pv[:, d * 512:(d + 1) * 512],
                                              xtc[b][:, k * 128:(k + 1) * 128],
                                              wvsb[:, k * D + d * 512:k * D + d * 512 + 512],
                                              start=(k == 0), stop=(k == NKB - 1),
                                              skip_group_check=True)
                    inc("P1MM", instr)
            @block.scalar
            def _(sc):
                for t in range(NT):
                    b = t % 2
                    W(sc, "P1MM", t + 1)
                    if t >= 2:
                        W(sc, "P1SV", 16 * (t - 1))
                    inc("P1VE", sc.copy(vst[b][:], pv[:]))
                    W(sc, "P1VE", t + 1)
                    dinc("P1SV", sc.dma_start(out=vtok_h[t * 128:(t + 1) * 128, :], in_=vst[b][:]))
                    W(sc, "P1EV", t + 1)
                    dinc("P1SX", sc.dma_start(
                        out=xT_h[:, t * 128:(t + 1) * 128].rearrange("(kb p) c -> p kb c", p=128),
                        in_=xtc[b][:].rearrange("p (kb c) -> p kb c", kb=NKB)))
            barrier()

    if phases >= 2:
        pre = [es.enter_context(nc.sbuf_tensor(f"preb{i}", [128, 2 * MD], BF16)) for i in range(2)]
        @block.gpsimd
        def _(g):
            W(g, "SETL", cur("SETL"))
            nbuf = [0]
            def stage(dst_ap, src_ap, nelem, view=None):
                b = nbuf[0] % 2
                if nbuf[0] >= 2:
                    W(g, "PRES", 16 * (nbuf[0] - 1))
                dinc("PREL", g.dma_start(out=pre[b][:, 0:nelem], in_=src_ap))
                W(g, "PREL", cur("PREL"))
                buf = pre[b][:, 0:nelem] if view is None else view(pre[b][:, 0:nelem])
                dinc("PRES", g.dma_start(out=dst_ap, in_=buf))
                nbuf[0] += 1
            for W_h, WB_h in [(wqT_h, wqB_h), (wkT_h, wkB_h), (woT_h, woB_h)]:
                for c in range(16):
                    stage(WB_h.rearrange("(kb p) n -> p kb n", p=128)[:, c:c + 1, :],
                          W_h.rearrange("(kb p) n -> p kb n", p=128)[:, c:c + 1, :],
                          D)
            for W_h, WB_h in [(wgT_h, wgB_h), (wuT_h, wuB_h)]:
                for e in range(E):
                    for kb in range(NKB):
                        stage(WB_h[e, :, :, kb * 128:(kb + 1) * 128].rearrange("m p c -> p m c"),
                              W_h[e, kb * 128:(kb + 1) * 128, :],
                              MD,
                              view=lambda bb: bb.rearrange("p (m c) -> p m c", m=NMB))
            for e in range(E):
                for m in range(NMB):
                    stage(wdB_h[e, m * 128:(m + 1) * 128, :],
                          wdT_h[e, m * 128:(m + 1) * 128, :],
                          D)

    if phases >= 2 and p2sub >= 1:
        with ExitStack() as ps:
            psb = lambda name, shape, dty: ps.enter_context(nc.sbuf_tensor(name, shape, dty))
            wq_sl = [psb(f"wqsl{i}", [128, NKB * 256], BF16) for i in range(2)]
            wk_sl = [psb(f"wksl{i}", [128, NKB * 256], BF16) for i in range(2)]
            xtc2 = [psb(f"p2xtc{i}", [128, D], BF16) for i in range(3)]
            qTp = [psb(f"qTp{i}", [128, 2 * T], BF16) for i in range(2)]
            kTp = [psb(f"kTp{i}", [128, 2 * T], BF16) for i in range(2)]
            vhb = [psb(f"vhb{i}", [128, T], BF16) for i in range(2)]
            ostg = [psb(f"ostg{i}", [128, T], BF16) for i in range(2)]
            pqs = [psb(f"pqs{i}", [128, 512], F32) for i in range(2)]
            qrs = psb("qrs", [128, 256], F32)
            qro = psb("qro", [128, 256], F32)
            qbb = psb("qbb", [128, 256], BF16)
            msd = psb("msd", [128, 16], F32)
            jsq = psb("jsq", [128, 256], F32)
            jro = psb("jro", [128, 256], F32)
            jro2 = psb("jro2", [128, 256], F32)
            ts1 = [psb(f"ts1{i}", [128, 512], F32) for i in range(2)]
            ebuf = [psb(f"ebuf{i}", [128, 512], BF16) for i in range(3)]
            etmp = psb("etmp", [128, 512], BF16)
            rdb = psb("rdb", [1, 512], F32)
            rbb = psb("rbb", [128, 512], F32)

            pq_full = ps.enter_context(nc.psum_tensor("p2q", [128, 512], F32))
            pq = pq_full[:, 0:256]
            pk_full = ps.enter_context(nc.psum_tensor("p2k", [128, 512], F32))
            pk = pk_full[:, 0:256]
            ptr2_full = ps.enter_context(nc.psum_tensor("p2tr", [128, 1024], BF16))
            ptr2 = ptr2_full[:, 0:512]
            pss = [ps.enter_context(nc.psum_tensor(f"p2s{i}", [128, 512], F32)) for i in range(2)]
            pso = [ps.enter_context(nc.psum_tensor(f"p2o{i}", [128, 512], F32)) for i in range(2)]
            psd = ps.enter_context(nc.psum_tensor("p2d", [128, 512], F32))

            gci = [0]
            for pr in range(npr):
                pb = pr % 2
                @block.sync
                def _(sy):
                    if pr == 0:
                        W(sy, "PRES", 16 * 32)
                        W(sy, "P1SX", 16 * NT)
                        W(sy, "P1SV", 16 * NT)
                    if pr >= 2:
                        W(sy, "P2TE", 2 * ntt * (pr - 1))
                    dinc("WQL", sy.dma_start(
                        out=wq_sl[pb][:].rearrange("p (kb c) -> p kb c", kb=NKB),
                        in_=wqB_h[:, pr * 256:(pr + 1) * 256].rearrange("(kb p) c -> p kb c", p=128)))
                    dinc("WQL", sy.dma_start(
                        out=wk_sl[pb][:].rearrange("p (kb c) -> p kb c", kb=NKB),
                        in_=wkB_h[:, pr * 256:(pr + 1) * 256].rearrange("(kb p) c -> p kb c", p=128)))
                    for t in range(ntt):
                        need = pr * ntt + t
                        if need >= 3:
                            W(sy, "P2MMQ", need - 2)
                        dinc("P2XL", sy.dma_start(
                            out=xtc2[need % 3][:].rearrange("p (kb c) -> p kb c", kb=NKB),
                            in_=xT_h[:, t * 128:(t + 1) * 128].rearrange("(kb p) c -> p kb c", p=128)))

                for t in range(ntt):
                    gi = pr * ntt + t
                    xb3 = gi % 3
                    @block.tensor
                    def _(te):
                        W(te, "P2XL", 16 * (gi + 1))
                        W(te, "WQL", 32 * (pr + 1))
                        if gi >= 1:
                            W(te, "P2CP", 2 * gi)
                        for k in range(NKB):
                            te.matmul(pq, xtc2[xb3][:, k * 128:(k + 1) * 128],
                                      wq_sl[pb][:, k * 256:(k + 1) * 256],
                                      start=(k == 0), stop=(k == NKB - 1),
                                      skip_group_check=True)
                            instr = te.matmul(pk, xtc2[xb3][:, k * 128:(k + 1) * 128],
                                              wk_sl[pb][:, k * 256:(k + 1) * 256],
                                              start=(k == 0), stop=(k == NKB - 1),
                                              skip_group_check=True)
                        inc("P2MMQ", instr)
                    @block.vector
                    def _(v):
                        W(v, "P2MMQ", gi + 1)
                        if gi == 2 and "d_pqs" in dump_h:
                            W(v, "DMPP", 16)
                        if gi == 1 and "d_msd" in dump_h:
                            W(v, "DMPM", 16)
                        if gi >= 2:
                            W(v, "P2SQ", gi - 1)
                        inc("P2CP", v.tensor_copy(pqs[gi % 2][:, 0:256], pq))
                        inc("P2CP", v.tensor_copy(pqs[gi % 2][:, 256:512], pk))
                    @block.scalar
                    def _(sc):
                        W(sc, "P2CP", 2 * (gi + 1))
                        if gi == 0 and "d_pqs" in dump_h:
                            dinc("DMPP", sc.dma_start(out=dump_h["d_pqs"], in_=pqs[0][:]))
                        if gi >= 1:
                            W(sc, "P2MS", 2 * gi)
                        for s2 in range(2):
                            sc.activation(jsq[:, 0:128], pqs[gi % 2][:, s2 * 128:(s2 + 1) * 128], AF.Square,
                                          accum_out=msd[:, s2:s2 + 1])
                            instr = sc.activation(jsq[:, 128:256], pqs[gi % 2][:, 256 + s2 * 128:256 + (s2 + 1) * 128],
                                                  AF.Square, accum_out=msd[:, 2 + s2:3 + s2])
                        inc("P2SQ", instr)
                    @block.vector
                    def _(v):
                        W(v, "P2SQ", gi + 1)
                        fence(v, v.tensor_scalar(msd[:, 4:8], msd[:, 0:4], 128.0 * 1e-6, None, ALU.add))
                        inc("P2MS", v.reciprocal(msd[:, 8:12], msd[:, 4:8]), 2)
                    @block.scalar
                    def _(sc):
                        W(sc, "P2MS", 2 * (gi + 1))
                        if gi >= 1:
                            W(sc, "P2RC", 2 * gi)
                        inc("P2RS", sc.activation(msd[:, 12:16], msd[:, 8:12], AF.Sqrt, scale=128.0))
                        if gi == 0 and "d_msd" in dump_h:
                            dinc("DMPM", sc.dma_start(out=dump_h["d_msd"], in_=msd[:]))
                    for qk in range(2):
                        ptensor = pqs[gi % 2][:, 0:256] if qk == 0 else pqs[gi % 2][:, 256:512]
                        rope = ropeq if qk == 0 else ropek
                        dstg = qTp if qk == 0 else kTp
                        tqk = 2 * gi + qk
                        @block.vector
                        def _(v):
                            W(v, "P2RS", gi + 1)
                            if tqk == 0:
                                W(v, "SETG", 16 * 4)
                            if tqk == 1 and "d_qro" in dump_h:
                                W(v, "DMPR", 16)
                            if tqk >= 1:
                                W(v, "P2TR", tqk)
                            v.tensor_scalar(qrs[:, 0:128],
                                            ptensor[:, 0:128],
                                            msd[:, 12 + 2 * qk:13 + 2 * qk], None, ALU.mult)
                            fence(v, v.tensor_scalar(qrs[:, 128:256],
                                                     ptensor[:, 128:256],
                                                     msd[:, 13 + 2 * qk:14 + 2 * qk], None, ALU.mult))
                            CW = rope[:, pr * 256:(pr + 1) * 256]
                            SW = rope[:, D + pr * 256:D + (pr + 1) * 256]
                            q3 = qrs[:].rearrange("p (n two) -> p n two", two=2)
                            j3 = jro2[:].rearrange("p (n two) -> p n two", two=2)
                            s3 = SW.rearrange("p (n two) -> p n two", two=2)
                            v.tensor_mul(jro[:], qrs[:], CW)
                            v.tensor_mul(j3[:, :, 0:1], q3[:, :, 1:2], s3[:, :, 0:1])
                            fence(v, v.tensor_mul(j3[:, :, 1:2], q3[:, :, 0:1], s3[:, :, 1:2]))
                            instr = v.tensor_add(qbb[:], jro[:], jro2[:])
                            inc("P2RC", instr)
                        if tqk == 0 and "d_qro" in dump_h:
                            @block.scalar
                            def _(sc):
                                W(sc, "P2RC", 1)
                                dinc("DMPR", sc.dma_start(out=dump_h["d_qro"], in_=qbb[:]))
                        @block.tensor
                        def _(te):
                            W(te, "P2RC", tqk + 1)
                            if tqk >= 1:
                                W(te, "P2TE", tqk)
                            for s2 in range(2):
                                instr = te.transpose(ptr2[:, s2 * 128:(s2 + 1) * 128],
                                                     qbb[:, s2 * 128:(s2 + 1) * 128], idb[:])
                            inc("P2TR", instr)
                        @block.vector
                        def _(v):
                            W(v, "P2TR", tqk + 1)
                            if t == 0 and qk == 0 and pr >= 2 and p2sub >= 2:
                                W(v, "TAV", (2 * pr - 2) * 144)
                            instr = v.tensor_copy(
                                dstg[pb][:].rearrange("p (hl tt) -> p hl tt", hl=2)[:, :, t * 128:(t + 1) * 128],
                                ptr2[:, 0:256].rearrange("p (hl c) -> p hl c", hl=2))
                            inc("P2TE", instr)

                if pr == 0 and ("d_wqB" in dump_h or "d_xtc2" in dump_h or "d_wqsl" in dump_h):
                    @block.gpsimd
                    def _(g):
                        W(g, "P2MMQ", 1)
                        if "d_wqB" in dump_h:
                            dinc("DMPW2", g.dma_start(out=dump_h["d_wqB"], in_=wqB_h))
                        if "d_xtc2" in dump_h:
                            dinc("DMPW2", g.dma_start(out=dump_h["d_xtc2"], in_=xtc2[0][:]))
                        if "d_wqsl" in dump_h:
                            dinc("DMPW2", g.dma_start(out=dump_h["d_wqsl"], in_=wq_sl[0][:]))
                        W(g, "DMPW2", cur("DMPW2"))
                if pr == 0 and ("d_qT" in dump_h or "d_kT" in dump_h):
                    @block.gpsimd
                    def _(g):
                        W(g, "P2TE", 2 * ntt)
                        if "d_qT" in dump_h:
                            dinc("DMPQ", g.dma_start(out=dump_h["d_qT"], in_=qTp[0][:]))
                        if "d_kT" in dump_h:
                            dinc("DMPQ", g.dma_start(out=dump_h["d_kT"], in_=kTp[0][:]))
                        W(g, "DMPQ", cur("DMPQ"))
                for hl in (range(2) if p2sub >= 2 else []):
                    h = 2 * pr + hl
                    @block.sync
                    def _(sy):
                        if h >= 2:
                            W(sy, "VHU", h - 1)
                        dinc("VHL", sy.dma_start(
                            out=vhb[h % 2][:].rearrange("p (s c) -> p s c", s=32),
                            in_=vtok_h[:, h * 128:(h + 1) * 128].rearrange("(s p) c -> p s c", p=128)))
                    for n in range(NN):
                        ni = 4 * n + 4
                        for i in range(ni):
                            ci = gci[0]
                            sslot = ci % 2
                            eslot = ci % 3
                            diag = i >= 4 * n
                            @block.tensor
                            def _(te):
                                if i == 0:
                                    W(te, "P2TE", 2 * ntt * (pr + 1))
                                    W(te, "VHL", 16 * (h + 1))
                                if i <= 1 and (h * NN + n) >= 1:
                                    W(te, "OE", h * NN + n)
                                if ci >= 2:
                                    W(te, "TH", ci - 1)
                                instr = te.matmul(pss[sslot][:],
                                                  kTp[pb][:, hl * T + i * 128:hl * T + (i + 1) * 128],
                                                  qTp[pb][:, hl * T + n * 512:hl * T + (n + 1) * 512],
                                                  start=True, stop=True, skip_group_check=True)
                                inc("TS", instr)
                            @block.scalar
                            def _(sc):
                                W(sc, "TS", ci + 1)
                                if ci >= 3:
                                    W(sc, "TAV", ci - 2)
                                if diag:
                                    W(sc, "VM", cur("VM"))
                                fence(sc, sc.activation(ts1[sslot][:], pss[sslot][:], AF.Tanh, scale=TANH_SC))
                                instr = sc.activation(etmp[:] if diag else ebuf[eslot][:],
                                                      ts1[sslot][:], AF.Exp, scale=EXP_SC)
                                inc("TH", instr)
                            if diag:
                                off = 128 * i - 512 * n
                                @block.vector
                                def _(v):
                                    W(v, "TH", ci + 1)
                                    inc("VM", v.tensor_mul(ebuf[eslot][:], etmp[:],
                                                           maskt[:, (off // 128) * 512:(off // 128 + 1) * 512]))
                            @block.tensor
                            def _(te):
                                W(te, "TH", ci + 1)
                                if diag:
                                    W(te, "VM", cur("VM"))
                                te.matmul(psd[0:1, :], onescb[:], ebuf[eslot][:],
                                          start=(i == 0), stop=(i == ni - 1), skip_group_check=True)
                                instr = te.matmul(pso[n % 2][:],
                                                  vhb[h % 2][:, i * 128:(i + 1) * 128],
                                                  ebuf[eslot][:],
                                                  start=(i == 0), stop=(i == ni - 1), skip_group_check=True)
                                inc("TAV", instr)
                            gci[0] += 1
                        @block.vector
                        def _(v):
                            W(v, "TAV", gci[0])
                            inc("RD", v.reciprocal(rdb[0:1, :], psd[0:1, :]))
                        @block.tensor
                        def _(te):
                            W(te, "RD", h * NN + n + 1)
                            instr = te.matmul(pss[n % 2][:], onesrf[0:1, :], rdb[0:1, :],
                                              start=True, stop=True, skip_group_check=True)
                            inc("TB", instr)
                        @block.vector
                        def _(v):
                            W(v, "TB", h * NN + n + 1)
                            if n == 0 and h >= 2:
                                W(v, "AOS", 16 * (h - 1))
                            fence(v, v.tensor_copy(rbb[:], pss[n % 2][:]))
                            instr = v.tensor_mul(ostg[hl][:, n * 512:(n + 1) * 512], pso[n % 2][:], rbb[:])
                            inc("OE", instr)
                    @block.tensor
                    def _(te):
                        W(te, "OE", (h + 1) * NN)
                        inc("VHU", te.nop())
                    @block.scalar
                    def _(sc):
                        W(sc, "OE", (h + 1) * NN)
                        dinc("AOS", sc.dma_start(out=aoT_h[h * 128:(h + 1) * 128, :], in_=ostg[hl][:]))
            barrier()

    if phases >= 3:
        with ExitStack() as ps:
            psb = lambda name, shape, dty: ps.enter_context(nc.sbuf_tensor(name, shape, dty))
            wosb = psb("wosb", [128, NKB * D], BF16)
            aot = [psb(f"aot{i}", [128, D], BF16) for i in range(2)]
            xres = [psb(f"xres{i}", [128, D], F32) for i in range(2)]
            sres = psb("sres", [128, D], F32)
            junk3 = sres
            stats = psb("stats", [128, 8], F32)
            dscr = psb("dscr", [128, D], F32)
            h1t = [psb(f"h1t{i}", [128, D], F32) for i in range(2)]
            hib = psb("hib", [128, D], BF16)
            lob = psb("lob", [128, D], BF16)
            hTst0 = psb("hTst0", [128, NKB * 512], BF16)
            hTst = [hTst0, hTst0]
            loTst = psb("loTst", [128, NKB * 512], BF16)
            lgsb = psb("lgsb", [128, 512], F32)
            ltok = psb("ltok", [128, 16], F32)
            gsc = psb("gsc", [128, 64], F32)

            po3 = [ps.enter_context(nc.psum_tensor(f"p3o{i}", [128, 512], F32)) for i in range(2)]
            ptrh = ps.enter_context(nc.psum_tensor("p3trh", [128, 2048], BF16))
            ptrl = ps.enter_context(nc.psum_tensor("p3trl", [128, 2048], BF16))
            plg = ps.enter_context(nc.psum_tensor("p3lg", [128, 512], F32))
            ptt = ps.enter_context(nc.psum_tensor("p3tt", [128, 128], F32))

            @block.vector
            def _(v):
                inc("LGZ", v.memset(lgsb[:], 0.0))
            @block.sync
            def _(sy):
                W(sy, "PRES", 16 * 48)
                W(sy, "AOS", 16 * H)
                for c in range(4):
                    dinc("WOL", sy.dma_start(
                        out=wosb[:].rearrange("p (kb n) -> p kb n", kb=NKB)[:, c * 4:(c + 1) * 4, :],
                        in_=woB_h.rearrange("(kb p) n -> p kb n", p=128)[:, c * 4:(c + 1) * 4, :]))
                for t in range(nt3):
                    b = t % 2
                    if t >= 2:
                        W(sy, "P3MM", 4 * (t - 1))
                        W(sy, "P3SE", 4 * (t - 1))
                    dinc("P3AL", sy.dma_start(
                        out=aot[b][:].rearrange("p (kb c) -> p kb c", kb=NKB),
                        in_=aoT_h[:, t * 128:(t + 1) * 128].rearrange("(kb p) c -> p kb c", p=128)))
                    dinc("P3XL", sy.dma_start(out=xres[b][:], in_=x_h[t * 128:(t + 1) * 128, :]))

            for t in range(nt3):
                b = t % 2
                n = t // 4
                nb = n % 2
                @block.tensor
                def _(te):
                    W(te, "P3AL", 16 * (t + 1))
                    if t == 0:
                        W(te, "WOL", 16 * 4)
                    for d in range(4):
                        if 4 * t + d >= 2:
                            W(te, "P3SE", 4 * t + d - 1)
                        if p3x >= 2:
                            inc("P3MM", te.nop())
                            continue
                        for k in range(NKB):
                            instr = te.matmul(po3[d % 2][:],
                                              aot[b][:, k * 128:(k + 1) * 128],
                                              wosb[:, k * D + d * 512:k * D + d * 512 + 512],
                                              start=(k == 0), stop=(k == NKB - 1))
                        inc("P3MM", instr)
                @block.vector
                def _(v):
                    W(v, "P3XL", 16 * (t + 1))
                    for d in range(4):
                        W(v, "P3MM", 4 * t + d + 1)
                        if p3x >= 2:
                            inc("P3SE", v.tensor_copy(sres[:, d * 512:(d + 1) * 512],
                                                      xres[b][:, d * 512:(d + 1) * 512]))
                        else:
                            inc("P3SE", v.tensor_add(sres[:, d * 512:(d + 1) * 512],
                                                     xres[b][:, d * 512:(d + 1) * 512], po3[d % 2][:]))
                    if p3x >= 1:
                        inc("P3LN", v.tensor_copy(dscr[:], sres[:]))
                    else:
                        fence(v, v.tensor_reduce(stats[:, 0:1], sres[:], AX.X, ALU.add))
                        fence(v, v.tensor_scalar(stats[:, 1:2], stats[:, 0:1], 1.0 / D, None, ALU.mult))
                        inc("P3LN", v.tensor_scalar(dscr[:], sres[:], stats[:, 1:2], None, ALU.subtract))
                @block.scalar
                def _(sc):
                    W(sc, "P3LN", t + 1)
                    if p3x >= 1:
                        inc("P3SQ", sc.nop())
                    else:
                        inc("P3SQ", sc.activation(junk3[:], dscr[:], AF.Square, accum_out=stats[:, 2:3]))
                @block.vector
                def _(v):
                    W(v, "P3SQ", t + 1)
                    if p3x >= 1:
                        inc("P3RC", v.engine_nop())
                    else:
                        fence(v, v.tensor_scalar(stats[:, 3:4], stats[:, 2:3], float(D) * 1e-5, None, ALU.add))
                        inc("P3RC", v.reciprocal(stats[:, 4:5], stats[:, 3:4]))
                @block.scalar
                def _(sc):
                    W(sc, "P3RC", t + 1)
                    if p3x >= 1:
                        inc("P3SR", sc.nop())
                    else:
                        inc("P3SR", sc.activation(stats[:, 5:6], stats[:, 4:5], AF.Sqrt, scale=float(D)))
                @block.vector
                def _(v):
                    W(v, "P3SR", t + 1)
                    if t >= 2:
                        W(v, "P3H1S", 16 * (t - 1))
                    if p3x >= 1:
                        inc("P3HD", v.tensor_copy(h1t[b][:], dscr[:]))
                    else:
                        inc("P3HD", v.tensor_scalar(h1t[b][:], dscr[:], stats[:, 5:6], None, ALU.mult))
                    W(v, "P3HD", cur("P3HD"))
                    if t >= 1 and p3sub >= 2:
                        W(v, "P3TR", t)
                    fence(v, v.tensor_copy(hib[:], h1t[b][:]))
                    inc("P3HL", v.tensor_sub(lob[:], h1t[b][:], hib[:]))
                @block.scalar
                def _(sc):
                    W(sc, "P3HD", t + 1)
                    dinc("P3H1S", sc.dma_start(out=h1_h[t * 128:(t + 1) * 128, :], in_=h1t[b][:]))
                if p3sub < 2:
                    continue
                @block.tensor
                def _(te):
                    W(te, "P3HL", t + 1)
                    if t >= 1:
                        W(te, "P3TE", 2 * t)
                    for k in range(NKB):
                        te.transpose(ptrh[:, k * 128:(k + 1) * 128], hib[:, k * 128:(k + 1) * 128], idb[:])
                        instr = te.transpose(ptrl[:, k * 128:(k + 1) * 128], lob[:, k * 128:(k + 1) * 128], idb[:])
                    inc("P3TR", instr)
                @block.vector
                def _(v):
                    W(v, "P3TR", t + 1)
                    if t % 4 == 0 and n >= 1 and p3sub >= 3:
                        W(v, "P3GD", n)
                        W(v, "P3HTS", 16 * n)
                        W(v, "P3GM", n)
                    v.tensor_copy(
                        hTst[nb][:].rearrange("p (kb c) -> p kb c", kb=NKB)[:, :, (t % 4) * 128:(t % 4 + 1) * 128],
                        ptrh[:].rearrange("p (kb c) -> p kb c", kb=NKB))
                    instr = v.tensor_copy(
                        loTst[:].rearrange("p (kb c) -> p kb c", kb=NKB)[:, :, (t % 4) * 128:(t % 4 + 1) * 128],
                        ptrl[:].rearrange("p (kb c) -> p kb c", kb=NKB))
                    inc("P3TE", instr, 2)
                if t % 4 == 3 and p3sub >= 3:
                    @block.tensor
                    def _(te):
                        W(te, "P3TE", 2 * (t + 1))
                        if n == 0:
                            W(te, "SETV", cur("SETV"))
                        if n >= 1:
                            W(te, "P3GE", n)
                        nmm = 0
                        for k in range(NKB):
                            for (lw, rh) in [(ghi, hTst[nb]), (glo, hTst[nb]), (ghi, loTst)]:
                                nmm += 1
                                instr = te.matmul(plg[0:16, :], lw[:, k * 16:(k + 1) * 16],
                                                  rh[:, k * 512:(k + 1) * 512],
                                                  start=(nmm == 1), stop=(nmm == 3 * NKB))
                        inc("P3GM", instr)
                    @block.vector
                    def _(v):
                        W(v, "P3GM", n + 1)
                        W(v, "LGZ", 1)
                        if n >= 1:
                            W(v, "P3TT", 4 * n)
                        inc("P3GE", v.tensor_copy(lgsb[0:16, :], plg[0:16, :]))
                    for c in range(4):
                        @block.tensor
                        def _(te):
                            W(te, "P3GE", n + 1)
                            if 4 * n + c >= 1:
                                W(te, "P3TTE", 4 * n + c)
                            inc("P3TT", te.transpose(ptt[:], lgsb[:, c * 128:(c + 1) * 128], idf[:]))
                        @block.vector
                        def _(v):
                            W(v, "P3TT", 4 * n + c + 1)
                            tt = n * 4 + c
                            if tt >= 1:
                                W(v, "P3EX", tt)
                            inc("P3TTE", v.tensor_copy(ltok[:], ptt[:, 0:16]))
                            W(v, "P3TTE", cur("P3TTE"))
                            fence(v, v.max(gsc[:, 0:8], ltok[:]))
                            inc("P3M8", v.tensor_scalar(gsc[:, 8:24], ltok[:], gsc[:, 5:6], None, ALU.is_ge))
                        @block.scalar
                        def _(sc):
                            W(sc, "P3M8", 4 * n + c + 1)
                            inc("P3EX", sc.activation(gsc[:, 24:40], ltok[:], AF.Exp))
                        @block.vector
                        def _(v):
                            W(v, "P3EX", 4 * n + c + 1)
                            tt = n * 4 + c
                            fence(v, v.tensor_reduce(gsc[:, 40:41], gsc[:, 24:40], AX.X, ALU.add))
                            fence(v, v.reciprocal(gsc[:, 41:42], gsc[:, 40:41]))
                            fence(v, v.tensor_scalar(gsc[:, 42:58], gsc[:, 24:40], gsc[:, 41:42], None, ALU.mult))
                            instr = v.tensor_mul(cwsb[:, tt * 16:(tt + 1) * 16], gsc[:, 42:58], gsc[:, 8:24])
                            inc("P3CW", instr)
                            if c == 3:
                                inc("P3GD", v.engine_nop())
                    @block.scalar
                    def _(sc):
                        W(sc, "P3TE", 2 * (t + 1))
                        dinc("P3HTS", sc.dma_start(
                            out=hT_h[:, n * 512:(n + 1) * 512].rearrange("(kb p) c -> p kb c", p=128),
                            in_=hTst[nb][:].rearrange("p (kb c) -> p kb c", kb=NKB)))
            barrier()

    if phases >= 5:
        with ExitStack() as ps:
            psb = lambda name, shape, dty: ps.enter_context(nc.sbuf_tensor(name, shape, dty))
            wdsb = psb("wdsb", [128, NMB * D], BF16)
            wgsl = [psb(f"wgsl{i}", [128, D], BF16) for i in range(2)]
            wusl = [psb(f"wusl{i}", [128, D], BF16) for i in range(2)]
            hTn = [psb(f"hTn{i}", [128, NKB * 512], BF16) for i in range(2)]
            aTn = psb("aTn", [128, NMB * 512], BF16)
            sg5 = psb("sg5", [128, 512], F32)
            ysb = [psb(f"ysb{i}", [128, D], F32) for i in range(2)]

            pg5 = [ps.enter_context(nc.psum_tensor(f"p5g{i}", [128, 512], F32)) for i in range(2)]
            pu5 = [ps.enter_context(nc.psum_tensor(f"p5u{i}", [128, 512], F32)) for i in range(2)]
            py5 = [ps.enter_context(nc.psum_tensor(f"p5y{i}", [128, 512], F32)) for i in range(2)]

            nws = [0]
            nyt = [0]
            for e in range(E):
                @block.sync
                def _(sy):
                    if e == 0:
                        W(sy, "PRES", cur("PRES"))
                        W(sy, "P3HTS", 16 * NN)
                        W(sy, "P3GD", NN)
                        W(sy, "P3H1S", 16 * NT)
                    if e >= 1:
                        W(sy, "P5YM", e * NN * 16)
                    dinc("P5WD", sy.dma_start(
                        out=wdsb[:].rearrange("p (mb c) -> p mb c", mb=NMB),
                        in_=wdB_h[e].rearrange("(mb p) c -> p mb c", p=128)))
                for n in range(NN):
                    gn = e * NN + n
                    hb = gn % 2
                    @block.sync
                    def _(sy):
                        if gn >= 2:
                            W(sy, "P5GU", (gn - 1) * NMB)
                        dinc("P5HL", sy.dma_start(
                            out=hTn[hb][:].rearrange("p (kb c) -> p kb c", kb=NKB),
                            in_=hT_h[:, n * 512:(n + 1) * 512].rearrange("(kb p) c -> p kb c", p=128)))
                        for m in range(NMB):
                            wb = nws[0] % 2
                            if nws[0] >= 2:
                                W(sy, "P5GU", nws[0] - 1)
                            dinc("P5WL", sy.dma_start(out=wgsl[wb][:], in_=wgB_h[e, m]))
                            dinc("P5WL", sy.dma_start(out=wusl[wb][:], in_=wuB_h[e, m]))
                            nws[0] += 1
                    for m in range(NMB):
                        gm = gn * NMB + m
                        wb = gm % 2
                        gb = gm % 2
                        @block.tensor
                        def _(te):
                            W(te, "P5HL", 16 * (gn + 1))
                            W(te, "P5WL", 32 * (gm + 1))
                            if gm >= 2:
                                W(te, "P5AE", gm - 1)
                            for k in range(NKB):
                                te.matmul(pg5[gb][:], wgsl[wb][:, k * 128:(k + 1) * 128],
                                          hTn[hb][:, k * 512:(k + 1) * 512],
                                          start=(k == 0), stop=(k == NKB - 1),
                                          skip_group_check=True)
                                instr = te.matmul(pu5[gb][:], wusl[wb][:, k * 128:(k + 1) * 128],
                                                  hTn[hb][:, k * 512:(k + 1) * 512],
                                                  start=(k == 0), stop=(k == NKB - 1),
                                                  skip_group_check=True)
                            inc("P5GU", instr)
                        @block.scalar
                        def _(sc):
                            W(sc, "P5GU", gm + 1)
                            if gm >= 1:
                                W(sc, "P5AE", gm)
                            inc("P5SI", sc.activation(sg5[:], pg5[gb][:], AF.Silu))
                        @block.vector
                        def _(v):
                            W(v, "P5SI", gm + 1)
                            if gn >= 1 and m == 0:
                                W(v, "P5YM", gn * 16)
                            inc("P5AE", v.tensor_mul(aTn[:, m * 512:(m + 1) * 512], sg5[:], pu5[gb][:]))
                    for tj in range(4):
                        yb = (4 * gn + tj) % 2
                        for dd in range(4):
                            yp = nyt[0] % 2
                            @block.tensor
                            def _(te):
                                W(te, "P5AE", (gn + 1) * NMB)
                                if n == 0 and tj == 0 and dd == 0:
                                    W(te, "P5WD", 16 * (e + 1))
                                if nyt[0] >= 2:
                                    W(te, "P5YE", nyt[0] - 1)
                                for m in range(NMB):
                                    instr = te.matmul(py5[yp][:],
                                                      aTn[:, m * 512 + tj * 128:m * 512 + (tj + 1) * 128],
                                                      wdsb[:, m * D + dd * 512:m * D + dd * 512 + 512],
                                                      start=(m == 0), stop=(m == NMB - 1))
                                inc("P5YM", instr)
                            @block.vector
                            def _(v):
                                W(v, "P5YM", nyt[0] + 1)
                                if nyt[0] >= 8:
                                    W(v, "P5YS", 16 * (nyt[0] // 4 - 1))
                                tt = n * 4 + tj
                                inc("P5YE", v.tensor_scalar(ysb[yb][:, dd * 512:(dd + 1) * 512], py5[yp][:],
                                                            cwsb[:, tt * 16 + e:tt * 16 + e + 1], None, ALU.mult))
                            nyt[0] += 1
                        @block.gpsimd
                        def _(g):
                            W(g, "P5YE", nyt[0])
                            tt = n * 4 + tj
                            if e == 0:
                                instr = g.dma_start(out=yac_h[tt * 128:(tt + 1) * 128, :], in_=ysb[yb][:])
                            else:
                                instr = g.dma_start(out=yac_h[tt * 128:(tt + 1) * 128, :], in_=ysb[yb][:],
                                                    accum_op=ALU.add)
                            dinc("P5YS", instr)
            barrier()

    if phases >= 6:
        with ExitStack() as ps:
            psb = lambda name, shape, dty: ps.enter_context(nc.sbuf_tensor(name, shape, dty))
            yld = [psb(f"yld{i}", [128, D], F32) for i in range(2)]
            hld = [psb(f"hld{i}", [128, D], F32) for i in range(2)]
            s6 = psb("s6", [128, D], F32)
            junk6 = psb("junk6", [128, D], F32)
            st6 = psb("st6", [128, 8], F32)
            dscr6 = psb("dscr6", [128, D], F32)
            ot6 = [psb(f"ot6{i}", [128, D], F32) for i in range(2)]
            @block.sync
            def _(sy):
                W(sy, "P5YS", 16 * NT * E)
                W(sy, "P3H1S", 16 * NT)
                for t in range(NT):
                    b = t % 2
                    if t >= 2:
                        W(sy, "P6A", t - 1)
                    dinc("P6L", sy.dma_start(out=yld[b][:], in_=yac_h[t * 128:(t + 1) * 128, :]))
                    dinc("P6L", sy.dma_start(out=hld[b][:], in_=h1_h[t * 128:(t + 1) * 128, :]))
            for t in range(NT):
                b = t % 2
                @block.vector
                def _(v):
                    W(v, "P6L", 32 * (t + 1))
                    inc("P6A", v.tensor_add(s6[:], yld[b][:], hld[b][:]))
                    W(v, "P6A", cur("P6A"))
                    fence(v, v.tensor_reduce(st6[:, 0:1], s6[:], AX.X, ALU.add))
                    fence(v, v.tensor_scalar(st6[:, 1:2], st6[:, 0:1], 1.0 / D, None, ALU.mult))
                    inc("P6D", v.tensor_scalar(dscr6[:], s6[:], st6[:, 1:2], None, ALU.subtract))
                @block.scalar
                def _(sc):
                    W(sc, "P6D", t + 1)
                    inc("P6SQ", sc.activation(junk6[:], dscr6[:], AF.Square, accum_out=st6[:, 2:3]))
                @block.vector
                def _(v):
                    W(v, "P6SQ", t + 1)
                    fence(v, v.tensor_scalar(st6[:, 3:4], st6[:, 2:3], float(D) * 1e-5, None, ALU.add))
                    inc("P6RC", v.reciprocal(st6[:, 4:5], st6[:, 3:4]))
                @block.scalar
                def _(sc):
                    W(sc, "P6RC", t + 1)
                    inc("P6SR", sc.activation(st6[:, 5:6], st6[:, 4:5], AF.Sqrt, scale=float(D)))
                @block.vector
                def _(v):
                    W(v, "P6SR", t + 1)
                    if t >= 2:
                        W(v, "P6S", 16 * (t - 1))
                    inc("P6N", v.tensor_scalar(ot6[b][:], dscr6[:], st6[:, 5:6], None, ALU.mult))
                @block.scalar
                def _(sc):
                    W(sc, "P6N", t + 1)
                    dinc("P6S", sc.dma_start(out=out_h[t * 128:(t + 1) * 128, :], in_=ot6[b][:]))
            @block.gpsimd
            def _(g):
                W(g, "P6S", 16 * NT)

    if dump:
        barrier()
        @block.sync
        def _(sy):
            pairs = {"d_xT": xT_h, "d_vtok": vtok_h, "d_aoT": aoT_h, "d_h1": h1_h,
                     "d_hT": hT_h, "d_yac": yac_h}
            for name, ap in dump_h.items():
                if name in pairs:
                    dinc("DMP", sy.dma_start(out=ap, in_=pairs[name]))
                elif name == "d_wgB":
                    dinc("DMP", sy.dma_start(out=ap, in_=wgB_h[0]))
        if "d_cw" in dump_h:
            @block.scalar
            def _(sc):
                dinc("DMP", sc.dma_start(out=dump_h["d_cw"], in_=cwsb[:]))
        @block.gpsimd
        def _(g):
            W(g, "DMP", cur("DMP"))

    for nm, val in waits_log:
        if val > cur(nm):
            raise RuntimeError(f"wait on {nm} for {val} but only {cur(nm)} increments emitted")
    es.close()
    return nc


def _host_prep(inp):
    f32 = np.float32
    x = np.ascontiguousarray(np.asarray(inp["L0_attn_x"], f32))
    wqT = np.ascontiguousarray(np.asarray(inp["L0_attn_wq"], f32).T)
    wkT = np.ascontiguousarray(np.asarray(inp["L0_attn_wk"], f32).T)
    wvT = np.ascontiguousarray(np.asarray(inp["L0_attn_wv"], f32).T)
    woT = np.ascontiguousarray(np.asarray(inp["L0_attn_wo"], f32).T)
    gwT = np.ascontiguousarray(np.asarray(inp["L0_ffn_gate_w"], f32).T)
    wgT = np.ascontiguousarray(np.transpose(np.asarray(inp["L0_ffn_w_gate"], f32), (0, 2, 1)))
    wuT = np.ascontiguousarray(np.transpose(np.asarray(inp["L0_ffn_w_up"], f32), (0, 2, 1)))
    wdT = np.ascontiguousarray(np.transpose(np.asarray(inp["L0_ffn_w_down"], f32), (0, 2, 1)))
    cos16 = np.asarray(inp["L0_attn_rope_cos"], f32)[:H]
    sin16 = np.asarray(inp["L0_attn_rope_sin"], f32)[:H]
    qnw = np.asarray(inp["L0_attn_q_norm_w"], f32)
    knw = np.asarray(inp["L0_attn_k_norm_w"], f32)

    cosE = np.repeat(cos16, 2, axis=1).reshape(-1)
    sinE = np.repeat(sin16, 2, axis=1).reshape(-1)
    sign = np.tile(np.array([-1.0, 1.0], f32), D // 2)
    swap = lambda w: np.ascontiguousarray(w.reshape(-1, 2)[:, ::-1]).reshape(-1)
    rows = np.stack([
        cosE, sinE, sign,
        np.tile(qnw, H), np.tile(swap(qnw), H),
        np.tile(knw, H), np.tile(swap(knw), H),
        np.ones(D, f32),
    ]).astype(f32)

    import ml_dtypes
    masks = np.zeros((128, 2048), np.float32)
    rr = np.arange(128)[:, None]
    cc = np.arange(512)[None, :]
    for oi, off in enumerate((0, 128, 256, 384)):
        masks[:, oi * 512:(oi + 1) * 512] = (cc >= rr + off)
    masks = masks.astype(ml_dtypes.bfloat16)
    identb = np.eye(128, dtype=np.float32).astype(ml_dtypes.bfloat16)
    identf = np.eye(128, dtype=np.float32)
    onescb = np.ones((128, 1), np.float32).astype(ml_dtypes.bfloat16)
    consts = np.tile(np.array([TANH_SC, EXP_SC, 128.0, float(D)], f32)[None, :], (128, 1))

    shared = dict(wqT=wqT, wkT=wkT, wvT=wvT, woT=woT, gwT=gwT,
                  wgT=wgT, wuT=wuT, wdT=wdT, rows=rows, masks=masks,
                  identb=identb, identf=identf, onescb=onescb, consts=consts)
    return x, shared


def kernel(**inp):
    x, shared = _host_prep(inp)
    nc = build(phases=6)
    in_maps = [dict(shared, x=np.ascontiguousarray(x[b])) for b in range(8)]
    res = run_bass_kernel_spmd(nc, in_maps, core_ids=list(range(8)))
    return np.stack([np.asarray(res.results[b]["out"]) for b in range(8)]).astype(np.float32)


# version 02
# speedup vs baseline: 1.0232x; 1.0232x over previous
import numpy as np
from contextlib import ExitStack
import concourse.bass as bass
import concourse.mybir as mybir
from concourse.bass_utils import run_bass_kernel_spmd

F32 = mybir.dt.float32
BF16 = mybir.dt.bfloat16
AF = mybir.ActivationFunctionType
ALU = mybir.AluOpType
AX = mybir.AxisListType

T, D, H, DH = 4096, 2048, 16, 128
E, MD = 16, 1408
NKB = D // 128
NMB = MD // 128
NT = T // 128
NN = T // 512
TANH_SC = float((1.0 / np.sqrt(128.0)) / 50.0)
EXP_SC = 50.0


def build(phases=6, dump=(), p2sub=2, npr=8, ntt=NT, p3sub=3, nt3=NT, p3x=0):
    nc = bass.Bass("TRN2", target_bir_lowering=False)
    dtn = nc.dram_tensor
    x_h = dtn("x", [T, D], F32, kind="ExternalInput").ap()
    wqT_h = dtn("wqT", [D, D], F32, kind="ExternalInput").ap()
    wkT_h = dtn("wkT", [D, D], F32, kind="ExternalInput").ap()
    wvT_h = dtn("wvT", [D, D], F32, kind="ExternalInput").ap()
    woT_h = dtn("woT", [D, D], F32, kind="ExternalInput").ap()
    gwT_h = dtn("gwT", [D, E], F32, kind="ExternalInput").ap()
    wgT_h = dtn("wgT", [E, D, MD], F32, kind="ExternalInput").ap()
    wuT_h = dtn("wuT", [E, D, MD], F32, kind="ExternalInput").ap()
    wdT_h = dtn("wdT", [E, MD, D], F32, kind="ExternalInput").ap()
    rows_h = dtn("rows", [8, D], F32, kind="ExternalInput").ap()
    masks_h = dtn("masks", [128, 2048], BF16, kind="ExternalInput").ap()
    identb_h = dtn("identb", [128, 128], BF16, kind="ExternalInput").ap()
    identf_h = dtn("identf", [128, 128], F32, kind="ExternalInput").ap()
    onescb_h = dtn("onescb", [128, 1], BF16, kind="ExternalInput").ap()
    consts_h = dtn("consts", [128, 4], F32, kind="ExternalInput").ap()

    xT_h = dtn("xTs", [D, T], BF16, kind="Internal").ap()
    vtok_h = dtn("vtoks", [T, D], BF16, kind="Internal").ap()
    aoT_h = dtn("aoTs", [D, T], BF16, kind="Internal").ap()
    h1_h = dtn("h1s", [T, D], F32, kind="Internal").ap()
    hT_h = dtn("hTs", [D, T], BF16, kind="Internal").ap()
    wqB_h = dtn("wqB", [D, D], BF16, kind="Internal").ap()
    wkB_h = dtn("wkB", [D, D], BF16, kind="Internal").ap()
    woB_h = dtn("woB", [D, D], BF16, kind="Internal").ap()
    wgB_h = dtn("wgB", [E, NMB, 128, D], BF16, kind="Internal").ap()
    wuB_h = dtn("wuB", [E, NMB, 128, D], BF16, kind="Internal").ap()
    wdB_h = dtn("wdB", [E, MD, D], BF16, kind="Internal").ap()
    yac_h = dtn("yac", [T, D], F32, kind="Internal").ap()
    rsc_h = dtn("ropescratch", [4, D], F32, kind="Internal").ap()
    out_h = dtn("out", [T, D], F32, kind="ExternalOutput").ap()

    dump_h = {}
    for name, shp, dty in [
        ("d_xT", [D, T], BF16), ("d_vtok", [T, D], BF16),
        ("d_aoT", [D, T], BF16), ("d_h1", [T, D], F32),
        ("d_hT", [D, T], BF16), ("d_cw", [128, NT * E], F32),
        ("d_yac", [T, D], F32), ("d_wgB", [NMB, 128, D], BF16),
        ("d_wvsb", [128, NKB * D], BF16),
        ("d_qT", [128, 2 * T], BF16), ("d_kT", [128, 2 * T], BF16),
        ("d_pqs", [128, 512], F32), ("d_qro", [128, 256], BF16),
        ("d_wqB", [D, D], BF16), ("d_xtc2", [128, D], BF16), ("d_wqsl", [128, NKB * 256], BF16),
        ("d_msd", [128, 16], F32),
    ]:
        if name in dump:
            dump_h[name] = dtn(name, shp, dty, kind="ExternalOutput").ap()

    es = ExitStack()
    sb = lambda name, shape, dty: es.enter_context(nc.sbuf_tensor(name, shape, dty))

    cst = sb("cst", [128, 4], F32)
    idb = sb("idb", [128, 128], BF16)
    idf = sb("idf", [128, 128], F32)
    maskt = sb("maskt", [128, 2048], BF16)
    onescb = sb("s_onescb", [128, 1], BF16)
    onesrf = sb("onesrf", [1, 128], F32)
    ropeq = sb("ropeq", [128, 2 * D], BF16)
    ropek = sb("ropek", [128, 2 * D], BF16)
    ghi = sb("ghi", [128, NKB * E], BF16)
    glo = sb("glo", [128, NKB * E], BF16)
    gwf = sb("gwf", [128, NKB * E], F32)
    glof = sb("glof", [128, NKB * E], F32)
    cwsb = sb("cwsb", [128, NT * E], F32)
    setup_ps = ExitStack()
    rowsb = setup_ps.enter_context(nc.sbuf_tensor("rowsb", [1, 8 * D], F32))
    rwork = setup_ps.enter_context(nc.sbuf_tensor("rwork", [1, 5 * D], F32))

    sems = {}
    dma_sems = set()
    def S(name):
        if name not in sems:
            sems[name] = [es.enter_context(nc.semaphore(name)), 0]
        return sems[name]
    def inc(name, instr, n=1):
        s = S(name)
        instr.then_inc(s[0], n)
        s[1] += n
        if n >= 16:
            dma_sems.add(name)
        return instr
    def cur(name):
        return S(name)[1]
    def dinc(name, instr):
        return inc(name, instr, 16)
    waits_log = []
    def W(eng, name, val):
        waits_log.append((name, val))
        eng.wait_ge(S(name)[0], val)
    def fence(eng, instr, name=None):
        if name is None:
            name = "FEN_" + type(eng).__name__
        inc(name, instr)
        W(eng, name, cur(name))
        return instr

    block = es.enter_context(nc.Block())
    nc.const_aps.register(F32, TANH_SC, cst[:, 0:1])
    nc.const_aps.register(F32, EXP_SC, cst[:, 1:2])
    nc.const_aps.register(F32, 128.0, cst[:, 2:3])
    nc.const_aps.register(F32, float(D), cst[:, 3:4])

    bar_n = [0]
    ENGINES = ["sync", "scalar", "vector", "tensor", "gpsimd"]
    def barrier():
        @block.sync
        def _(sy):
            for nm in sorted(dma_sems):
                W(sy, nm, cur(nm))
        bar_n[0] += 1
        tgt = 5 * 16 * bar_n[0]
        for en in ENGINES:
            @getattr(block, en)
            def _(eng):
                inc("BAR", eng.nop(), 16)
        for en in ENGINES:
            @getattr(block, en)
            def _(eng):
                W(eng, "BAR", tgt)

    @block.sync
    def _(sy):
        for dst, src in [
            (cst[:], consts_h), (idb[:], identb_h), (idf[:], identf_h),
            (maskt[:], masks_h), (onescb[:], onescb_h),
            (onesrf[0:1, :], rows_h[7:8, 0:128]),
            (rowsb[0:1, :], rows_h.rearrange("r d -> (r d)")[None, :]),
            (gwf[:].rearrange("p (kb e) -> p kb e", kb=NKB),
             gwT_h.rearrange("(kb p) e -> p kb e", p=128)),
        ]:
            dinc("SETL", sy.dma_start(out=dst, in_=src))
    @block.vector
    def _(v):
        W(v, "SETL", cur("SETL"))
        CE, SE, SG, WQ, WQS, WK, WKS = [rowsb[0:1, i * D:(i + 1) * D] for i in range(7)]
        TMP = rwork[0:1, 4 * D:5 * D]
        v.tensor_mul(rwork[0:1, 0:D], CE, WQ)
        fence(v, v.tensor_mul(TMP, SE, SG))
        v.tensor_mul(rwork[0:1, D:2 * D], TMP, WQS)
        v.tensor_mul(rwork[0:1, 2 * D:3 * D], CE, WK)
        v.tensor_mul(rwork[0:1, 3 * D:4 * D], TMP, WKS)
        fence(v, v.tensor_copy(ghi[:], gwf[:]))
        fence(v, v.tensor_sub(glof[:], gwf[:], ghi[:]))
        inc("SETV", v.tensor_copy(glo[:], glof[:]))
    @block.gpsimd
    def _(g):
        W(g, "SETV", cur("SETV"))
        dinc("SETG0", g.dma_start(out=rsc_h.rearrange("r d -> (r d)")[None, :], in_=rwork[0:1, 0:4 * D]))
        W(g, "SETG0", 16)
        for i, dst in enumerate([ropeq[:, 0:D], ropeq[:, D:2 * D], ropek[:, 0:D], ropek[:, D:2 * D]]):
            dinc("SETG", g.dma_start(out=dst, in_=rsc_h[i:i + 1, :].to_broadcast([128, D])))
    setup_ps.close()

    if phases >= 1:
        with ExitStack() as ps:
            psb = lambda name, shape, dty: ps.enter_context(nc.sbuf_tensor(name, shape, dty))
            wvsb = psb("wvsb", [128, NKB * D], BF16)
            xf = [psb(f"p1xf{i}", [128, D], F32) for i in range(2)]
            xb = [psb(f"p1xb{i}", [128, D], BF16) for i in range(2)]
            xtc = [psb(f"p1xtc{i}", [128, D], BF16) for i in range(2)]
            vst = [psb(f"p1vst{i}", [128, D], BF16) for i in range(2)]
            ptr = ps.enter_context(nc.psum_tensor("p1ptr", [128, 2048], BF16))
            pv = ps.enter_context(nc.psum_tensor("p1pv", [128, 2048], F32))

            @block.gpsimd
            def _(g):
                W(g, "SETL", cur("SETL"))
                for c in range(4):
                    dinc("WVL", g.dma_start(
                        out=wvsb[:].rearrange("p (kb n) -> p kb n", kb=NKB)[:, c * 4:(c + 1) * 4, :],
                        in_=wvT_h.rearrange("(kb p) n -> p kb n", p=128)[:, c * 4:(c + 1) * 4, :]))
                if "d_wvsb" in dump_h:
                    W(g, "P1SV", 16 * NT)
                    dinc("DMPW", g.dma_start(out=dump_h["d_wvsb"], in_=wvsb[:]))
                    W(g, "DMPW", 16)
            @block.sync
            def _(sy):
                W(sy, "SETG", 16 * 4)
                for t in range(NT):
                    if t >= 2:
                        W(sy, "P1XF", t - 1)
                    dinc("P1LX", sy.dma_start(out=xf[t % 2][:], in_=x_h[t * 128:(t + 1) * 128, :]))
            @block.vector
            def _(v):
                for t in range(NT):
                    b = t % 2
                    W(v, "P1LX", 16 * (t + 1))
                    if t >= 2:
                        W(v, "P1TR", t - 1)
                    inc("P1XF", v.tensor_copy(xb[b][:], xf[b][:]))
                    W(v, "P1TR", t + 1)
                    if t >= 2:
                        W(v, "P1SX", 16 * (t - 1))
                    inc("P1EV", v.tensor_copy(xtc[b][:], ptr[:]))
            @block.tensor
            def _(te):
                W(te, "WVL", 16 * 4)
                for t in range(NT):
                    b = t % 2
                    W(te, "P1XF", t + 1)
                    if t >= 1:
                        W(te, "P1EV", t)
                    for k in range(NKB):
                        instr = te.transpose(ptr[:, k * 128:(k + 1) * 128], xb[b][:, k * 128:(k + 1) * 128], idb[:])
                    inc("P1TR", instr)
                    W(te, "P1EV", t + 1)
                    if t >= 1:
                        W(te, "P1VE", t)
                    for d in range(4):
                        for k in range(NKB):
                            instr = te.matmul(pv[:, d * 512:(d + 1) * 512],
                                              xtc[b][:, k * 128:(k + 1) * 128],
                                              wvsb[:, k * D + d * 512:k * D + d * 512 + 512],
                                              start=(k == 0), stop=(k == NKB - 1),
                                              skip_group_check=True)
                    inc("P1MM", instr)
            @block.scalar
            def _(sc):
                for t in range(NT):
                    b = t % 2
                    W(sc, "P1MM", t + 1)
                    if t >= 2:
                        W(sc, "P1SV", 16 * (t - 1))
                    inc("P1VE", sc.copy(vst[b][:], pv[:]))
                    W(sc, "P1VE", t + 1)
                    dinc("P1SV", sc.dma_start(out=vtok_h[t * 128:(t + 1) * 128, :], in_=vst[b][:]))
                    W(sc, "P1EV", t + 1)
                    dinc("P1SX", sc.dma_start(
                        out=xT_h[:, t * 128:(t + 1) * 128].rearrange("(kb p) c -> p kb c", p=128),
                        in_=xtc[b][:].rearrange("p (kb c) -> p kb c", kb=NKB)))
            barrier()

    if phases >= 2:
        pre = [es.enter_context(nc.sbuf_tensor(f"preb{i}", [128, 2 * MD], BF16)) for i in range(2)]
        @block.gpsimd
        def _(g):
            W(g, "SETL", cur("SETL"))
            nbuf = [0]
            def stage(dst_ap, src_ap, nelem, view=None):
                b = nbuf[0] % 2
                if nbuf[0] >= 2:
                    W(g, "PRES", 16 * (nbuf[0] - 1))
                dinc("PREL", g.dma_start(out=pre[b][:, 0:nelem], in_=src_ap))
                W(g, "PREL", cur("PREL"))
                buf = pre[b][:, 0:nelem] if view is None else view(pre[b][:, 0:nelem])
                dinc("PRES", g.dma_start(out=dst_ap, in_=buf))
                nbuf[0] += 1
            for W_h, WB_h in [(wqT_h, wqB_h), (wkT_h, wkB_h), (woT_h, woB_h)]:
                for c in range(16):
                    stage(WB_h.rearrange("(kb p) n -> p kb n", p=128)[:, c:c + 1, :],
                          W_h.rearrange("(kb p) n -> p kb n", p=128)[:, c:c + 1, :],
                          D)
            for W_h, WB_h in [(wgT_h, wgB_h), (wuT_h, wuB_h)]:
                for e in range(E):
                    for kb in range(NKB):
                        stage(WB_h[e, :, :, kb * 128:(kb + 1) * 128].rearrange("m p c -> p m c"),
                              W_h[e, kb * 128:(kb + 1) * 128, :],
                              MD,
                              view=lambda bb: bb.rearrange("p (m c) -> p m c", m=NMB))
            for e in range(E):
                for m in range(NMB):
                    stage(wdB_h[e, m * 128:(m + 1) * 128, :],
                          wdT_h[e, m * 128:(m + 1) * 128, :],
                          D)

    if phases >= 2 and p2sub >= 1:
        with ExitStack() as ps:
            psb = lambda name, shape, dty: ps.enter_context(nc.sbuf_tensor(name, shape, dty))
            wq_sl = [psb(f"wqsl{i}", [128, NKB * 256], BF16) for i in range(2)]
            wk_sl = [psb(f"wksl{i}", [128, NKB * 256], BF16) for i in range(2)]
            xtc2 = [psb(f"p2xtc{i}", [128, D], BF16) for i in range(3)]
            qTp = [psb(f"qTp{i}", [128, 2 * T], BF16) for i in range(2)]
            kTp = [psb(f"kTp{i}", [128, 2 * T], BF16) for i in range(2)]
            vhb = [psb(f"vhb{i}", [128, T], BF16) for i in range(2)]
            ostg = [psb(f"ostg{i}", [128, T], BF16) for i in range(2)]
            pqs = [psb(f"pqs{i}", [128, 512], F32) for i in range(2)]
            qrs = psb("qrs", [128, 256], F32)
            qro = psb("qro", [128, 256], F32)
            qbb = psb("qbb", [128, 256], BF16)
            msd = psb("msd", [128, 16], F32)
            jsq = psb("jsq", [128, 256], F32)
            jro = psb("jro", [128, 256], F32)
            jro2 = psb("jro2", [128, 256], F32)
            ts1P = [psb(f"ts1{i}", [128, 1024], F32) for i in range(2)]
            ebufP = [psb(f"ebuf{i}", [128, 1024], BF16) for i in range(3)]
            etmpP = psb("etmp", [128, 1024], BF16)
            rdb = psb("rdb", [1, 512], F32)
            rbb = psb("rbb", [128, 512], F32)

            pqk = ps.enter_context(nc.psum_tensor("p2qk", [128, 512], F32))
            pq = pqk[:, 0:256]
            pk = pqk[:, 256:512]
            ptr2_full = ps.enter_context(nc.psum_tensor("p2tr", [128, 1024], BF16))
            ptr2 = ptr2_full[:, 0:512]
            pssP = [ps.enter_context(nc.psum_tensor(f"p2sp{i}", [128, 1024], F32)) for i in range(2)]
            pso = ps.enter_context(nc.psum_tensor("p2o", [128, 512], F32))
            psd = ps.enter_context(nc.psum_tensor("p2d", [128, 512], F32))

            gpi = [0]
            for pr in range(npr):
                pb = pr % 2
                @block.sync
                def _(sy):
                    if pr == 0:
                        W(sy, "PRES", 16 * 32)
                        W(sy, "P1SX", 16 * NT)
                        W(sy, "P1SV", 16 * NT)
                    if pr >= 2:
                        W(sy, "P2TE", 2 * ntt * (pr - 1))
                    dinc("WQL", sy.dma_start(
                        out=wq_sl[pb][:].rearrange("p (kb c) -> p kb c", kb=NKB),
                        in_=wqB_h[:, pr * 256:(pr + 1) * 256].rearrange("(kb p) c -> p kb c", p=128)))
                    dinc("WQL", sy.dma_start(
                        out=wk_sl[pb][:].rearrange("p (kb c) -> p kb c", kb=NKB),
                        in_=wkB_h[:, pr * 256:(pr + 1) * 256].rearrange("(kb p) c -> p kb c", p=128)))
                    for t in range(ntt):
                        need = pr * ntt + t
                        if need >= 3:
                            W(sy, "P2MMQ", need - 2)
                        dinc("P2XL", sy.dma_start(
                            out=xtc2[need % 3][:].rearrange("p (kb c) -> p kb c", kb=NKB),
                            in_=xT_h[:, t * 128:(t + 1) * 128].rearrange("(kb p) c -> p kb c", p=128)))

                for t in range(ntt):
                    gi = pr * ntt + t
                    xb3 = gi % 3
                    @block.tensor
                    def _(te):
                        W(te, "P2XL", 16 * (gi + 1))
                        W(te, "WQL", 32 * (pr + 1))
                        if gi >= 1:
                            W(te, "P2CP", gi)
                        for k in range(NKB):
                            te.matmul(pq, xtc2[xb3][:, k * 128:(k + 1) * 128],
                                      wq_sl[pb][:, k * 256:(k + 1) * 256],
                                      start=(k == 0), stop=(k == NKB - 1),
                                      skip_group_check=True)
                        for k in range(NKB):
                            instr = te.matmul(pk, xtc2[xb3][:, k * 128:(k + 1) * 128],
                                              wk_sl[pb][:, k * 256:(k + 1) * 256],
                                              start=(k == 0), stop=(k == NKB - 1),
                                              skip_group_check=True)
                        inc("P2MMQ", instr)
                    @block.vector
                    def _(v):
                        W(v, "P2MMQ", gi + 1)
                        if gi == 2 and "d_pqs" in dump_h:
                            W(v, "DMPP", 16)
                        if gi == 1 and "d_msd" in dump_h:
                            W(v, "DMPM", 16)
                        if gi >= 2:
                            W(v, "P2SQ", gi - 1)
                        inc("P2CP", v.tensor_copy(pqs[gi % 2][:], pqk[:]))
                    @block.scalar
                    def _(sc):
                        W(sc, "P2CP", gi + 1)
                        if gi == 0 and "d_pqs" in dump_h:
                            dinc("DMPP", sc.dma_start(out=dump_h["d_pqs"], in_=pqs[0][:]))
                        if gi >= 1:
                            W(sc, "P2MS", 2 * gi)
                        for s2 in range(2):
                            sc.activation(jsq[:, 0:128], pqs[gi % 2][:, s2 * 128:(s2 + 1) * 128], AF.Square,
                                          accum_out=msd[:, s2:s2 + 1])
                            instr = sc.activation(jsq[:, 128:256], pqs[gi % 2][:, 256 + s2 * 128:256 + (s2 + 1) * 128],
                                                  AF.Square, accum_out=msd[:, 2 + s2:3 + s2])
                        inc("P2SQ", instr)
                    @block.vector
                    def _(v):
                        W(v, "P2SQ", gi + 1)
                        fence(v, v.tensor_scalar(msd[:, 4:8], msd[:, 0:4], 128.0 * 1e-6, None, ALU.add))
                        inc("P2MS", v.reciprocal(msd[:, 8:12], msd[:, 4:8]), 2)
                    @block.scalar
                    def _(sc):
                        W(sc, "P2MS", 2 * (gi + 1))
                        if gi >= 1:
                            W(sc, "P2RC", 2 * gi)
                        inc("P2RS", sc.activation(msd[:, 12:16], msd[:, 8:12], AF.Sqrt, scale=128.0))
                        if gi == 0 and "d_msd" in dump_h:
                            dinc("DMPM", sc.dma_start(out=dump_h["d_msd"], in_=msd[:]))
                    for qk in range(2):
                        ptensor = pqs[gi % 2][:, 0:256] if qk == 0 else pqs[gi % 2][:, 256:512]
                        rope = ropeq if qk == 0 else ropek
                        dstg = qTp if qk == 0 else kTp
                        tqk = 2 * gi + qk
                        @block.vector
                        def _(v):
                            W(v, "P2RS", gi + 1)
                            if tqk == 0:
                                W(v, "SETG", 16 * 4)
                            if tqk == 1 and "d_qro" in dump_h:
                                W(v, "DMPR", 16)
                            if tqk >= 1:
                                W(v, "P2TR", tqk)
                            v.tensor_scalar(qrs[:, 0:128],
                                            ptensor[:, 0:128],
                                            msd[:, 12 + 2 * qk:13 + 2 * qk], None, ALU.mult)
                            fence(v, v.tensor_scalar(qrs[:, 128:256],
                                                     ptensor[:, 128:256],
                                                     msd[:, 13 + 2 * qk:14 + 2 * qk], None, ALU.mult))
                            CW = rope[:, pr * 256:(pr + 1) * 256]
                            SW = rope[:, D + pr * 256:D + (pr + 1) * 256]
                            q3 = qrs[:].rearrange("p (n two) -> p n two", two=2)
                            j3 = jro2[:].rearrange("p (n two) -> p n two", two=2)
                            s3 = SW.rearrange("p (n two) -> p n two", two=2)
                            v.tensor_mul(jro[:], qrs[:], CW)
                            v.tensor_mul(j3[:, :, 0:1], q3[:, :, 1:2], s3[:, :, 0:1])
                            fence(v, v.tensor_mul(j3[:, :, 1:2], q3[:, :, 0:1], s3[:, :, 1:2]))
                            instr = v.tensor_add(qbb[:], jro[:], jro2[:])
                            inc("P2RC", instr)
                        if tqk == 0 and "d_qro" in dump_h:
                            @block.scalar
                            def _(sc):
                                W(sc, "P2RC", 1)
                                dinc("DMPR", sc.dma_start(out=dump_h["d_qro"], in_=qbb[:]))
                        @block.tensor
                        def _(te):
                            W(te, "P2RC", tqk + 1)
                            if tqk >= 1:
                                W(te, "P2TE", tqk)
                            for s2 in range(2):
                                instr = te.transpose(ptr2[:, s2 * 128:(s2 + 1) * 128],
                                                     qbb[:, s2 * 128:(s2 + 1) * 128], idb[:])
                            inc("P2TR", instr)
                        @block.vector
                        def _(v):
                            W(v, "P2TR", tqk + 1)
                            if t == 0 and qk == 0 and pr >= 2 and p2sub >= 2:
                                W(v, "TAV", (2 * pr - 2) * 72)
                            instr = v.tensor_copy(
                                dstg[pb][:].rearrange("p (hl tt) -> p hl tt", hl=2)[:, :, t * 128:(t + 1) * 128],
                                ptr2[:, 0:256].rearrange("p (hl c) -> p hl c", hl=2))
                            inc("P2TE", instr)

                if pr == 0 and ("d_wqB" in dump_h or "d_xtc2" in dump_h or "d_wqsl" in dump_h):
                    @block.gpsimd
                    def _(g):
                        W(g, "P2MMQ", 1)
                        if "d_wqB" in dump_h:
                            dinc("DMPW2", g.dma_start(out=dump_h["d_wqB"], in_=wqB_h))
                        if "d_xtc2" in dump_h:
                            dinc("DMPW2", g.dma_start(out=dump_h["d_xtc2"], in_=xtc2[0][:]))
                        if "d_wqsl" in dump_h:
                            dinc("DMPW2", g.dma_start(out=dump_h["d_wqsl"], in_=wq_sl[0][:]))
                        W(g, "DMPW2", cur("DMPW2"))
                if pr == 0 and ("d_qT" in dump_h or "d_kT" in dump_h):
                    @block.gpsimd
                    def _(g):
                        W(g, "P2TE", 2 * ntt)
                        if "d_qT" in dump_h:
                            dinc("DMPQ", g.dma_start(out=dump_h["d_qT"], in_=qTp[0][:]))
                        if "d_kT" in dump_h:
                            dinc("DMPQ", g.dma_start(out=dump_h["d_kT"], in_=kTp[0][:]))
                        W(g, "DMPQ", cur("DMPQ"))
                for hl in (range(2) if p2sub >= 2 else []):
                    h = 2 * pr + hl
                    @block.sync
                    def _(sy):
                        if h >= 2:
                            W(sy, "VHU", h - 1)
                        dinc("VHL", sy.dma_start(
                            out=vhb[h % 2][:].rearrange("p (s c) -> p s c", s=32),
                            in_=vtok_h[:, h * 128:(h + 1) * 128].rearrange("(s p) c -> p s c", p=128)))
                    for n in range(NN):
                        ni = 4 * n + 4
                        np2 = ni // 2
                        hn = h * NN + n
                        ap0 = gpi[0]
                        tanh_fence = {}

                        def emit_mms(pi):
                            ap = ap0 + pi
                            @block.tensor
                            def _(te):
                                if pi == 0:
                                    W(te, "P2TE", 2 * ntt * (pr + 1))
                                    W(te, "VHL", 16 * (h + 1))
                                    if hn >= 1:
                                        W(te, "OE", hn)
                                if ap >= 2:
                                    W(te, "TH", ap - 1)
                                te.matmul(pssP[ap % 2][:, 0:512],
                                          kTp[pb][:, hl * T + 2 * pi * 128:hl * T + (2 * pi + 1) * 128],
                                          qTp[pb][:, hl * T + n * 512:hl * T + (n + 1) * 512],
                                          start=True, stop=True, skip_group_check=True)
                                instr = te.matmul(pssP[ap % 2][:, 512:1024],
                                                  kTp[pb][:, hl * T + (2 * pi + 1) * 128:hl * T + (2 * pi + 2) * 128],
                                                  qTp[pb][:, hl * T + n * 512:hl * T + (n + 1) * 512],
                                                  start=True, stop=True, skip_group_check=True)
                                inc("TS", instr)

                        def emit_tanh(pi):
                            ap = ap0 + pi
                            @block.scalar
                            def _(sc):
                                W(sc, "TS", ap + 1)
                                instr = sc.activation(ts1P[ap % 2][:], pssP[ap % 2][:], AF.Tanh, scale=TANH_SC)
                                inc("FEN_SC_TANH", instr)
                                tanh_fence[pi] = cur("FEN_SC_TANH")

                        def emit_exp(pi):
                            ap = ap0 + pi
                            diag = pi >= 2 * n
                            @block.scalar
                            def _(sc):
                                W(sc, "FEN_SC_TANH", tanh_fence[pi])
                                if ap >= 3:
                                    W(sc, "TAV", ap - 2)
                                if diag:
                                    W(sc, "VM", cur("VM"))
                                instr = sc.activation(etmpP[:] if diag else ebufP[ap % 3][:],
                                                      ts1P[ap % 2][:], AF.Exp, scale=EXP_SC)
                                inc("TH", instr)
                            if diag:
                                @block.vector
                                def _(v):
                                    W(v, "TH", ap + 1)
                                    for half in range(2):
                                        off_idx = 2 * pi - 4 * n + half
                                        inc("VM", v.tensor_mul(
                                            ebufP[ap % 3][:, half * 512:(half + 1) * 512],
                                            etmpP[:, half * 512:(half + 1) * 512],
                                            maskt[:, off_idx * 512:(off_idx + 1) * 512]))

                        def emit_dav(pi):
                            ap = ap0 + pi
                            diag = pi >= 2 * n
                            @block.tensor
                            def _(te):
                                W(te, "TH", ap + 1)
                                if diag:
                                    W(te, "VM", cur("VM"))
                                for half in range(2):
                                    i = 2 * pi + half
                                    te.matmul(psd[0:1, :], onescb[:],
                                              ebufP[ap % 3][:, half * 512:(half + 1) * 512],
                                              start=(i == 0), stop=(i == ni - 1), skip_group_check=True)
                                    instr = te.matmul(pso[:],
                                                      vhb[h % 2][:, i * 128:(i + 1) * 128],
                                                      ebufP[ap % 3][:, half * 512:(half + 1) * 512],
                                                      start=(i == 0), stop=(i == ni - 1), skip_group_check=True)
                                inc("TAV", instr)

                        for pi in range(np2):
                            emit_mms(pi)
                            emit_tanh(pi)
                            if pi >= 1:
                                emit_exp(pi - 1)
                                emit_dav(pi - 1)
                        emit_exp(np2 - 1)
                        emit_dav(np2 - 1)
                        gpi[0] += np2

                        @block.vector
                        def _(v):
                            W(v, "TAV", gpi[0])
                            inc("RD", v.reciprocal(rdb[0:1, :], psd[0:1, :]))
                        @block.tensor
                        def _(te):
                            W(te, "RD", hn + 1)
                            instr = te.matmul(pssP[gpi[0] % 2][:, 0:512], onesrf[0:1, :], rdb[0:1, :],
                                              start=True, stop=True, skip_group_check=True)
                            inc("TB", instr)
                        @block.vector
                        def _(v):
                            W(v, "TB", hn + 1)
                            if n == 0 and h >= 2:
                                W(v, "AOS", 16 * (h - 1))
                            fence(v, v.tensor_copy(rbb[:], pssP[gpi[0] % 2][:, 0:512]))
                            instr = v.tensor_mul(ostg[hl][:, n * 512:(n + 1) * 512], pso[:], rbb[:])
                            inc("OE", instr)
                    @block.tensor
                    def _(te):
                        W(te, "OE", (h + 1) * NN)
                        inc("VHU", te.nop())
                    @block.scalar
                    def _(sc):
                        W(sc, "OE", (h + 1) * NN)
                        dinc("AOS", sc.dma_start(out=aoT_h[h * 128:(h + 1) * 128, :], in_=ostg[hl][:]))
            barrier()

    if phases >= 3:
        with ExitStack() as ps:
            psb = lambda name, shape, dty: ps.enter_context(nc.sbuf_tensor(name, shape, dty))
            wosb = psb("wosb", [128, NKB * D], BF16)
            aot = [psb(f"aot{i}", [128, D], BF16) for i in range(2)]
            xres = [psb(f"xres{i}", [128, D], F32) for i in range(2)]
            sres = psb("sres", [128, D], F32)
            junk3 = sres
            stats = psb("stats", [128, 8], F32)
            dscr = psb("dscr", [128, D], F32)
            h1t = [psb(f"h1t{i}", [128, D], F32) for i in range(2)]
            hib = psb("hib", [128, D], BF16)
            lob = psb("lob", [128, D], BF16)
            hTst0 = psb("hTst0", [128, NKB * 512], BF16)
            hTst = [hTst0, hTst0]
            loTst = psb("loTst", [128, NKB * 512], BF16)
            lgsb = psb("lgsb", [128, 512], F32)
            ltok = psb("ltok", [128, 16], F32)
            gsc = psb("gsc", [128, 64], F32)

            po3 = [ps.enter_context(nc.psum_tensor(f"p3o{i}", [128, 512], F32)) for i in range(2)]
            ptrh = ps.enter_context(nc.psum_tensor("p3trh", [128, 2048], BF16))
            ptrl = ps.enter_context(nc.psum_tensor("p3trl", [128, 2048], BF16))
            plg = ps.enter_context(nc.psum_tensor("p3lg", [128, 512], F32))
            ptt = ps.enter_context(nc.psum_tensor("p3tt", [128, 128], F32))

            @block.vector
            def _(v):
                inc("LGZ", v.memset(lgsb[:], 0.0))
            @block.sync
            def _(sy):
                W(sy, "PRES", 16 * 48)
                W(sy, "AOS", 16 * H)
                for c in range(4):
                    dinc("WOL", sy.dma_start(
                        out=wosb[:].rearrange("p (kb n) -> p kb n", kb=NKB)[:, c * 4:(c + 1) * 4, :],
                        in_=woB_h.rearrange("(kb p) n -> p kb n", p=128)[:, c * 4:(c + 1) * 4, :]))
                for t in range(nt3):
                    b = t % 2
                    if t >= 2:
                        W(sy, "P3MM", 4 * (t - 1))
                        W(sy, "P3SE", 4 * (t - 1))
                    dinc("P3AL", sy.dma_start(
                        out=aot[b][:].rearrange("p (kb c) -> p kb c", kb=NKB),
                        in_=aoT_h[:, t * 128:(t + 1) * 128].rearrange("(kb p) c -> p kb c", p=128)))
                    dinc("P3XL", sy.dma_start(out=xres[b][:], in_=x_h[t * 128:(t + 1) * 128, :]))

            for t in range(nt3):
                b = t % 2
                n = t // 4
                nb = n % 2
                @block.tensor
                def _(te):
                    W(te, "P3AL", 16 * (t + 1))
                    if t == 0:
                        W(te, "WOL", 16 * 4)
                    for d in range(4):
                        if 4 * t + d >= 2:
                            W(te, "P3SE", 4 * t + d - 1)
                        if p3x >= 2:
                            inc("P3MM", te.nop())
                            continue
                        for k in range(NKB):
                            instr = te.matmul(po3[d % 2][:],
                                              aot[b][:, k * 128:(k + 1) * 128],
                                              wosb[:, k * D + d * 512:k * D + d * 512 + 512],
                                              start=(k == 0), stop=(k == NKB - 1))
                        inc("P3MM", instr)
                @block.vector
                def _(v):
                    W(v, "P3XL", 16 * (t + 1))
                    for d in range(4):
                        W(v, "P3MM", 4 * t + d + 1)
                        if p3x >= 2:
                            inc("P3SE", v.tensor_copy(sres[:, d * 512:(d + 1) * 512],
                                                      xres[b][:, d * 512:(d + 1) * 512]))
                        else:
                            inc("P3SE", v.tensor_add(sres[:, d * 512:(d + 1) * 512],
                                                     xres[b][:, d * 512:(d + 1) * 512], po3[d % 2][:]))
                    if p3x >= 1:
                        inc("P3LN", v.tensor_copy(dscr[:], sres[:]))
                    else:
                        fence(v, v.tensor_reduce(stats[:, 0:1], sres[:], AX.X, ALU.add))
                        fence(v, v.tensor_scalar(stats[:, 1:2], stats[:, 0:1], 1.0 / D, None, ALU.mult))
                        inc("P3LN", v.tensor_scalar(dscr[:], sres[:], stats[:, 1:2], None, ALU.subtract))
                @block.scalar
                def _(sc):
                    W(sc, "P3LN", t + 1)
                    if p3x >= 1:
                        inc("P3SQ", sc.nop())
                    else:
                        inc("P3SQ", sc.activation(junk3[:], dscr[:], AF.Square, accum_out=stats[:, 2:3]))
                @block.vector
                def _(v):
                    W(v, "P3SQ", t + 1)
                    if p3x >= 1:
                        inc("P3RC", v.engine_nop())
                    else:
                        fence(v, v.tensor_scalar(stats[:, 3:4], stats[:, 2:3], float(D) * 1e-5, None, ALU.add))
                        inc("P3RC", v.reciprocal(stats[:, 4:5], stats[:, 3:4]))
                @block.scalar
                def _(sc):
                    W(sc, "P3RC", t + 1)
                    if p3x >= 1:
                        inc("P3SR", sc.nop())
                    else:
                        inc("P3SR", sc.activation(stats[:, 5:6], stats[:, 4:5], AF.Sqrt, scale=float(D)))
                @block.vector
                def _(v):
                    W(v, "P3SR", t + 1)
                    if t >= 2:
                        W(v, "P3H1S", 16 * (t - 1))
                    if p3x >= 1:
                        inc("P3HD", v.tensor_copy(h1t[b][:], dscr[:]))
                    else:
                        inc("P3HD", v.tensor_scalar(h1t[b][:], dscr[:], stats[:, 5:6], None, ALU.mult))
                    W(v, "P3HD", cur("P3HD"))
                    if t >= 1 and p3sub >= 2:
                        W(v, "P3TR", t)
                    fence(v, v.tensor_copy(hib[:], h1t[b][:]))
                    inc("P3HL", v.tensor_sub(lob[:], h1t[b][:], hib[:]))
                @block.scalar
                def _(sc):
                    W(sc, "P3HD", t + 1)
                    dinc("P3H1S", sc.dma_start(out=h1_h[t * 128:(t + 1) * 128, :], in_=h1t[b][:]))
                if p3sub < 2:
                    continue
                @block.tensor
                def _(te):
                    W(te, "P3HL", t + 1)
                    if t >= 1:
                        W(te, "P3TE", 2 * t)
                    for k in range(NKB):
                        te.transpose(ptrh[:, k * 128:(k + 1) * 128], hib[:, k * 128:(k + 1) * 128], idb[:])
                        instr = te.transpose(ptrl[:, k * 128:(k + 1) * 128], lob[:, k * 128:(k + 1) * 128], idb[:])
                    inc("P3TR", instr)
                @block.vector
                def _(v):
                    W(v, "P3TR", t + 1)
                    if t % 4 == 0 and n >= 1 and p3sub >= 3:
                        W(v, "P3GD", n)
                        W(v, "P3HTS", 16 * n)
                        W(v, "P3GM", n)
                    v.tensor_copy(
                        hTst[nb][:].rearrange("p (kb c) -> p kb c", kb=NKB)[:, :, (t % 4) * 128:(t % 4 + 1) * 128],
                        ptrh[:].rearrange("p (kb c) -> p kb c", kb=NKB))
                    instr = v.tensor_copy(
                        loTst[:].rearrange("p (kb c) -> p kb c", kb=NKB)[:, :, (t % 4) * 128:(t % 4 + 1) * 128],
                        ptrl[:].rearrange("p (kb c) -> p kb c", kb=NKB))
                    inc("P3TE", instr, 2)
                if t % 4 == 3 and p3sub >= 3:
                    @block.tensor
                    def _(te):
                        W(te, "P3TE", 2 * (t + 1))
                        if n == 0:
                            W(te, "SETV", cur("SETV"))
                        if n >= 1:
                            W(te, "P3GE", n)
                        nmm = 0
                        for k in range(NKB):
                            for (lw, rh) in [(ghi, hTst[nb]), (glo, hTst[nb]), (ghi, loTst)]:
                                nmm += 1
                                instr = te.matmul(plg[0:16, :], lw[:, k * 16:(k + 1) * 16],
                                                  rh[:, k * 512:(k + 1) * 512],
                                                  start=(nmm == 1), stop=(nmm == 3 * NKB))
                        inc("P3GM", instr)
                    @block.vector
                    def _(v):
                        W(v, "P3GM", n + 1)
                        W(v, "LGZ", 1)
                        if n >= 1:
                            W(v, "P3TT", 4 * n)
                        inc("P3GE", v.tensor_copy(lgsb[0:16, :], plg[0:16, :]))
                    for c in range(4):
                        @block.tensor
                        def _(te):
                            W(te, "P3GE", n + 1)
                            if 4 * n + c >= 1:
                                W(te, "P3TTE", 4 * n + c)
                            inc("P3TT", te.transpose(ptt[:], lgsb[:, c * 128:(c + 1) * 128], idf[:]))
                        @block.vector
                        def _(v):
                            W(v, "P3TT", 4 * n + c + 1)
                            tt = n * 4 + c
                            if tt >= 1:
                                W(v, "P3EX", tt)
                            inc("P3TTE", v.tensor_copy(ltok[:], ptt[:, 0:16]))
                            W(v, "P3TTE", cur("P3TTE"))
                            fence(v, v.max(gsc[:, 0:8], ltok[:]))
                            inc("P3M8", v.tensor_scalar(gsc[:, 8:24], ltok[:], gsc[:, 5:6], None, ALU.is_ge))
                        @block.scalar
                        def _(sc):
                            W(sc, "P3M8", 4 * n + c + 1)
                            inc("P3EX", sc.activation(gsc[:, 24:40], ltok[:], AF.Exp))
                        @block.vector
                        def _(v):
                            W(v, "P3EX", 4 * n + c + 1)
                            tt = n * 4 + c
                            fence(v, v.tensor_reduce(gsc[:, 40:41], gsc[:, 24:40], AX.X, ALU.add))
                            fence(v, v.reciprocal(gsc[:, 41:42], gsc[:, 40:41]))
                            fence(v, v.tensor_scalar(gsc[:, 42:58], gsc[:, 24:40], gsc[:, 41:42], None, ALU.mult))
                            instr = v.tensor_mul(cwsb[:, tt * 16:(tt + 1) * 16], gsc[:, 42:58], gsc[:, 8:24])
                            inc("P3CW", instr)
                            if c == 3:
                                inc("P3GD", v.engine_nop())
                    @block.scalar
                    def _(sc):
                        W(sc, "P3TE", 2 * (t + 1))
                        dinc("P3HTS", sc.dma_start(
                            out=hT_h[:, n * 512:(n + 1) * 512].rearrange("(kb p) c -> p kb c", p=128),
                            in_=hTst[nb][:].rearrange("p (kb c) -> p kb c", kb=NKB)))
            barrier()

    if phases >= 5:
        with ExitStack() as ps:
            psb = lambda name, shape, dty: ps.enter_context(nc.sbuf_tensor(name, shape, dty))
            wdsb = psb("wdsb", [128, NMB * D], BF16)
            wgsl = [psb(f"wgsl{i}", [128, D], BF16) for i in range(2)]
            wusl = [psb(f"wusl{i}", [128, D], BF16) for i in range(2)]
            hTn = [psb(f"hTn{i}", [128, NKB * 512], BF16) for i in range(2)]
            aTn = psb("aTn", [128, NMB * 512], BF16)
            sg5 = psb("sg5", [128, 512], F32)
            ysb = [psb(f"ysb{i}", [128, D], F32) for i in range(2)]

            pg5 = [ps.enter_context(nc.psum_tensor(f"p5g{i}", [128, 512], F32)) for i in range(2)]
            pu5 = [ps.enter_context(nc.psum_tensor(f"p5u{i}", [128, 512], F32)) for i in range(2)]
            py5 = [ps.enter_context(nc.psum_tensor(f"p5y{i}", [128, 512], F32)) for i in range(2)]

            nws = [0]
            nyt = [0]
            for e in range(E):
                @block.sync
                def _(sy):
                    if e == 0:
                        W(sy, "PRES", cur("PRES"))
                        W(sy, "P3HTS", 16 * NN)
                        W(sy, "P3GD", NN)
                        W(sy, "P3H1S", 16 * NT)
                    if e >= 1:
                        W(sy, "P5YM", e * NN * 16)
                    dinc("P5WD", sy.dma_start(
                        out=wdsb[:].rearrange("p (mb c) -> p mb c", mb=NMB),
                        in_=wdB_h[e].rearrange("(mb p) c -> p mb c", p=128)))
                for n in range(NN):
                    gn = e * NN + n
                    hb = gn % 2
                    @block.sync
                    def _(sy):
                        if gn >= 2:
                            W(sy, "P5GU", (gn - 1) * NMB)
                        dinc("P5HL", sy.dma_start(
                            out=hTn[hb][:].rearrange("p (kb c) -> p kb c", kb=NKB),
                            in_=hT_h[:, n * 512:(n + 1) * 512].rearrange("(kb p) c -> p kb c", p=128)))
                        for m in range(NMB):
                            wb = nws[0] % 2
                            if nws[0] >= 2:
                                W(sy, "P5GU", nws[0] - 1)
                            dinc("P5WL", sy.dma_start(out=wgsl[wb][:], in_=wgB_h[e, m]))
                            dinc("P5WL", sy.dma_start(out=wusl[wb][:], in_=wuB_h[e, m]))
                            nws[0] += 1
                    for m in range(NMB):
                        gm = gn * NMB + m
                        wb = gm % 2
                        gb = gm % 2
                        @block.tensor
                        def _(te):
                            W(te, "P5HL", 16 * (gn + 1))
                            W(te, "P5WL", 32 * (gm + 1))
                            if gm >= 2:
                                W(te, "P5AE", gm - 1)
                            for k in range(NKB):
                                te.matmul(pg5[gb][:], wgsl[wb][:, k * 128:(k + 1) * 128],
                                          hTn[hb][:, k * 512:(k + 1) * 512],
                                          start=(k == 0), stop=(k == NKB - 1),
                                          skip_group_check=True)
                                instr = te.matmul(pu5[gb][:], wusl[wb][:, k * 128:(k + 1) * 128],
                                                  hTn[hb][:, k * 512:(k + 1) * 512],
                                                  start=(k == 0), stop=(k == NKB - 1),
                                                  skip_group_check=True)
                            inc("P5GU", instr)
                        @block.scalar
                        def _(sc):
                            W(sc, "P5GU", gm + 1)
                            if gm >= 1:
                                W(sc, "P5AE", gm)
                            inc("P5SI", sc.activation(sg5[:], pg5[gb][:], AF.Silu))
                        @block.vector
                        def _(v):
                            W(v, "P5SI", gm + 1)
                            if gn >= 1 and m == 0:
                                W(v, "P5YM", gn * 16)
                            inc("P5AE", v.tensor_mul(aTn[:, m * 512:(m + 1) * 512], sg5[:], pu5[gb][:]))
                    for tj in range(4):
                        yb = (4 * gn + tj) % 2
                        for dd in range(4):
                            yp = nyt[0] % 2
                            @block.tensor
                            def _(te):
                                W(te, "P5AE", (gn + 1) * NMB)
                                if n == 0 and tj == 0 and dd == 0:
                                    W(te, "P5WD", 16 * (e + 1))
                                if nyt[0] >= 2:
                                    W(te, "P5YE", nyt[0] - 1)
                                for m in range(NMB):
                                    instr = te.matmul(py5[yp][:],
                                                      aTn[:, m * 512 + tj * 128:m * 512 + (tj + 1) * 128],
                                                      wdsb[:, m * D + dd * 512:m * D + dd * 512 + 512],
                                                      start=(m == 0), stop=(m == NMB - 1))
                                inc("P5YM", instr)
                            @block.vector
                            def _(v):
                                W(v, "P5YM", nyt[0] + 1)
                                if nyt[0] >= 8:
                                    W(v, "P5YS", 16 * (nyt[0] // 4 - 1))
                                tt = n * 4 + tj
                                inc("P5YE", v.tensor_scalar(ysb[yb][:, dd * 512:(dd + 1) * 512], py5[yp][:],
                                                            cwsb[:, tt * 16 + e:tt * 16 + e + 1], None, ALU.mult))
                            nyt[0] += 1
                        @block.gpsimd
                        def _(g):
                            W(g, "P5YE", nyt[0])
                            tt = n * 4 + tj
                            if e == 0:
                                instr = g.dma_start(out=yac_h[tt * 128:(tt + 1) * 128, :], in_=ysb[yb][:])
                            else:
                                instr = g.dma_start(out=yac_h[tt * 128:(tt + 1) * 128, :], in_=ysb[yb][:],
                                                    accum_op=ALU.add)
                            dinc("P5YS", instr)
            barrier()

    if phases >= 6:
        with ExitStack() as ps:
            psb = lambda name, shape, dty: ps.enter_context(nc.sbuf_tensor(name, shape, dty))
            yld = [psb(f"yld{i}", [128, D], F32) for i in range(2)]
            hld = [psb(f"hld{i}", [128, D], F32) for i in range(2)]
            s6 = psb("s6", [128, D], F32)
            junk6 = psb("junk6", [128, D], F32)
            st6 = psb("st6", [128, 8], F32)
            dscr6 = psb("dscr6", [128, D], F32)
            ot6 = [psb(f"ot6{i}", [128, D], F32) for i in range(2)]
            @block.sync
            def _(sy):
                W(sy, "P5YS", 16 * NT * E)
                W(sy, "P3H1S", 16 * NT)
                for t in range(NT):
                    b = t % 2
                    if t >= 2:
                        W(sy, "P6A", t - 1)
                    dinc("P6L", sy.dma_start(out=yld[b][:], in_=yac_h[t * 128:(t + 1) * 128, :]))
                    dinc("P6L", sy.dma_start(out=hld[b][:], in_=h1_h[t * 128:(t + 1) * 128, :]))
            for t in range(NT):
                b = t % 2
                @block.vector
                def _(v):
                    W(v, "P6L", 32 * (t + 1))
                    inc("P6A", v.tensor_add(s6[:], yld[b][:], hld[b][:]))
                    W(v, "P6A", cur("P6A"))
                    fence(v, v.tensor_reduce(st6[:, 0:1], s6[:], AX.X, ALU.add))
                    fence(v, v.tensor_scalar(st6[:, 1:2], st6[:, 0:1], 1.0 / D, None, ALU.mult))
                    inc("P6D", v.tensor_scalar(dscr6[:], s6[:], st6[:, 1:2], None, ALU.subtract))
                @block.scalar
                def _(sc):
                    W(sc, "P6D", t + 1)
                    inc("P6SQ", sc.activation(junk6[:], dscr6[:], AF.Square, accum_out=st6[:, 2:3]))
                @block.vector
                def _(v):
                    W(v, "P6SQ", t + 1)
                    fence(v, v.tensor_scalar(st6[:, 3:4], st6[:, 2:3], float(D) * 1e-5, None, ALU.add))
                    inc("P6RC", v.reciprocal(st6[:, 4:5], st6[:, 3:4]))
                @block.scalar
                def _(sc):
                    W(sc, "P6RC", t + 1)
                    inc("P6SR", sc.activation(st6[:, 5:6], st6[:, 4:5], AF.Sqrt, scale=float(D)))
                @block.vector
                def _(v):
                    W(v, "P6SR", t + 1)
                    if t >= 2:
                        W(v, "P6S", 16 * (t - 1))
                    inc("P6N", v.tensor_scalar(ot6[b][:], dscr6[:], st6[:, 5:6], None, ALU.mult))
                @block.scalar
                def _(sc):
                    W(sc, "P6N", t + 1)
                    dinc("P6S", sc.dma_start(out=out_h[t * 128:(t + 1) * 128, :], in_=ot6[b][:]))
            @block.gpsimd
            def _(g):
                W(g, "P6S", 16 * NT)

    if dump:
        barrier()
        @block.sync
        def _(sy):
            pairs = {"d_xT": xT_h, "d_vtok": vtok_h, "d_aoT": aoT_h, "d_h1": h1_h,
                     "d_hT": hT_h, "d_yac": yac_h}
            for name, ap in dump_h.items():
                if name in pairs:
                    dinc("DMP", sy.dma_start(out=ap, in_=pairs[name]))
                elif name == "d_wgB":
                    dinc("DMP", sy.dma_start(out=ap, in_=wgB_h[0]))
        if "d_cw" in dump_h:
            @block.scalar
            def _(sc):
                dinc("DMP", sc.dma_start(out=dump_h["d_cw"], in_=cwsb[:]))
        @block.gpsimd
        def _(g):
            W(g, "DMP", cur("DMP"))

    for nm, val in waits_log:
        if val > cur(nm):
            raise RuntimeError(f"wait on {nm} for {val} but only {cur(nm)} increments emitted")
    es.close()
    return nc


def _host_prep(inp):
    f32 = np.float32
    x = np.ascontiguousarray(np.asarray(inp["L0_attn_x"], f32))
    wqT = np.ascontiguousarray(np.asarray(inp["L0_attn_wq"], f32).T)
    wkT = np.ascontiguousarray(np.asarray(inp["L0_attn_wk"], f32).T)
    wvT = np.ascontiguousarray(np.asarray(inp["L0_attn_wv"], f32).T)
    woT = np.ascontiguousarray(np.asarray(inp["L0_attn_wo"], f32).T)
    gwT = np.ascontiguousarray(np.asarray(inp["L0_ffn_gate_w"], f32).T)
    wgT = np.ascontiguousarray(np.transpose(np.asarray(inp["L0_ffn_w_gate"], f32), (0, 2, 1)))
    wuT = np.ascontiguousarray(np.transpose(np.asarray(inp["L0_ffn_w_up"], f32), (0, 2, 1)))
    wdT = np.ascontiguousarray(np.transpose(np.asarray(inp["L0_ffn_w_down"], f32), (0, 2, 1)))
    cos16 = np.asarray(inp["L0_attn_rope_cos"], f32)[:H]
    sin16 = np.asarray(inp["L0_attn_rope_sin"], f32)[:H]
    qnw = np.asarray(inp["L0_attn_q_norm_w"], f32)
    knw = np.asarray(inp["L0_attn_k_norm_w"], f32)

    cosE = np.repeat(cos16, 2, axis=1).reshape(-1)
    sinE = np.repeat(sin16, 2, axis=1).reshape(-1)
    sign = np.tile(np.array([-1.0, 1.0], f32), D // 2)
    swap = lambda w: np.ascontiguousarray(w.reshape(-1, 2)[:, ::-1]).reshape(-1)
    rows = np.stack([
        cosE, sinE, sign,
        np.tile(qnw, H), np.tile(swap(qnw), H),
        np.tile(knw, H), np.tile(swap(knw), H),
        np.ones(D, f32),
    ]).astype(f32)

    import ml_dtypes
    masks = np.zeros((128, 2048), np.float32)
    rr = np.arange(128)[:, None]
    cc = np.arange(512)[None, :]
    for oi, off in enumerate((0, 128, 256, 384)):
        masks[:, oi * 512:(oi + 1) * 512] = (cc >= rr + off)
    masks = masks.astype(ml_dtypes.bfloat16)
    identb = np.eye(128, dtype=np.float32).astype(ml_dtypes.bfloat16)
    identf = np.eye(128, dtype=np.float32)
    onescb = np.ones((128, 1), np.float32).astype(ml_dtypes.bfloat16)
    consts = np.tile(np.array([TANH_SC, EXP_SC, 128.0, float(D)], f32)[None, :], (128, 1))

    shared = dict(wqT=wqT, wkT=wkT, wvT=wvT, woT=woT, gwT=gwT,
                  wgT=wgT, wuT=wuT, wdT=wdT, rows=rows, masks=masks,
                  identb=identb, identf=identf, onescb=onescb, consts=consts)
    return x, shared


def kernel(**inp):
    x, shared = _host_prep(inp)
    nc = build(phases=6)
    in_maps = [dict(shared, x=np.ascontiguousarray(x[b])) for b in range(8)]
    res = run_bass_kernel_spmd(nc, in_maps, core_ids=list(range(8)))
    return np.stack([np.asarray(res.results[b]["out"]) for b in range(8)]).astype(np.float32)


# version 03
# speedup vs baseline: 1.0710x; 1.0468x over previous
import numpy as np
from contextlib import ExitStack
import concourse.bass as bass
import concourse.mybir as mybir
from concourse.bass_utils import run_bass_kernel_spmd

F32 = mybir.dt.float32
BF16 = mybir.dt.bfloat16
AF = mybir.ActivationFunctionType
ALU = mybir.AluOpType
AX = mybir.AxisListType

T, D, H, DH = 4096, 2048, 16, 128
E, MD = 16, 1408
NKB = D // 128
NMB = MD // 128
NT = T // 128
NN = T // 512
TANH_SC = float((1.0 / np.sqrt(128.0)) / 50.0)
EXP_SC = 50.0


def build(phases=6, dump=(), p2sub=2, npr=8, ntt=NT, p3sub=3, nt3=NT, p3x=0):
    nc = bass.Bass("TRN2", target_bir_lowering=False)
    dtn = nc.dram_tensor
    x_h = dtn("x", [T, D], F32, kind="ExternalInput").ap()
    wqT_h = dtn("wqT", [D, D], F32, kind="ExternalInput").ap()
    wkT_h = dtn("wkT", [D, D], F32, kind="ExternalInput").ap()
    wvT_h = dtn("wvT", [D, D], F32, kind="ExternalInput").ap()
    woT_h = dtn("woT", [D, D], F32, kind="ExternalInput").ap()
    gwT_h = dtn("gwT", [D, E], F32, kind="ExternalInput").ap()
    wgT_h = dtn("wgT", [E, D, MD], F32, kind="ExternalInput").ap()
    wuT_h = dtn("wuT", [E, D, MD], F32, kind="ExternalInput").ap()
    wdT_h = dtn("wdT", [E, MD, D], F32, kind="ExternalInput").ap()
    rows_h = dtn("rows", [8, D], F32, kind="ExternalInput").ap()
    masks_h = dtn("masks", [128, 2048], BF16, kind="ExternalInput").ap()
    identb_h = dtn("identb", [128, 128], BF16, kind="ExternalInput").ap()
    identf_h = dtn("identf", [128, 128], F32, kind="ExternalInput").ap()
    onescb_h = dtn("onescb", [128, 1], BF16, kind="ExternalInput").ap()
    consts_h = dtn("consts", [128, 4], F32, kind="ExternalInput").ap()

    xT_h = dtn("xTs", [D, T], BF16, kind="Internal").ap()
    vtok_h = dtn("vtoks", [T, D], BF16, kind="Internal").ap()
    aoT_h = dtn("aoTs", [D, T], BF16, kind="Internal").ap()
    h1_h = dtn("h1s", [T, D], F32, kind="Internal").ap()
    hT_h = dtn("hTs", [D, T], BF16, kind="Internal").ap()
    wqB_h = dtn("wqB", [D, D], BF16, kind="Internal").ap()
    wkB_h = dtn("wkB", [D, D], BF16, kind="Internal").ap()
    woB_h = dtn("woB", [D, D], BF16, kind="Internal").ap()
    wgB_h = dtn("wgB", [E, NMB, 128, D], BF16, kind="Internal").ap()
    wuB_h = dtn("wuB", [E, NMB, 128, D], BF16, kind="Internal").ap()
    wdB_h = dtn("wdB", [E, MD, D], BF16, kind="Internal").ap()
    yac_h = dtn("yac", [T, D], F32, kind="Internal").ap()
    rsc_h = dtn("ropescratch", [4, D], F32, kind="Internal").ap()
    out_h = dtn("out", [T, D], F32, kind="ExternalOutput").ap()

    dump_h = {}
    for name, shp, dty in [
        ("d_xT", [D, T], BF16), ("d_vtok", [T, D], BF16),
        ("d_aoT", [D, T], BF16), ("d_h1", [T, D], F32),
        ("d_hT", [D, T], BF16), ("d_cw", [128, NT * E], F32),
        ("d_yac", [T, D], F32), ("d_wgB", [NMB, 128, D], BF16),
        ("d_wvsb", [128, NKB * D], BF16),
        ("d_qT", [128, 2 * T], BF16), ("d_kT", [128, 2 * T], BF16),
        ("d_pqs", [128, 512], F32), ("d_qro", [128, 256], BF16),
        ("d_wqB", [D, D], BF16), ("d_xtc2", [128, D], BF16), ("d_wqsl", [128, NKB * 256], BF16),
        ("d_msd", [128, 16], F32),
    ]:
        if name in dump:
            dump_h[name] = dtn(name, shp, dty, kind="ExternalOutput").ap()

    es = ExitStack()
    sb = lambda name, shape, dty: es.enter_context(nc.sbuf_tensor(name, shape, dty))

    cst = sb("cst", [128, 4], F32)
    idb = sb("idb", [128, 128], BF16)
    idf = sb("idf", [128, 128], F32)
    maskt = sb("maskt", [128, 2048], BF16)
    onescb = sb("s_onescb", [128, 1], BF16)
    onesrf = sb("onesrf", [1, 128], F32)
    ropeCW = sb("ropeCW", [128, 2 * D], BF16)
    ropeSW = sb("ropeSW", [128, 2 * D], BF16)
    ghi = sb("ghi", [128, NKB * E], BF16)
    glo = sb("glo", [128, NKB * E], BF16)
    gwf = sb("gwf", [128, NKB * E], F32)
    glof = sb("glof", [128, NKB * E], F32)
    cwsb = sb("cwsb", [128, NT * E], F32)
    setup_ps = ExitStack()
    rowsb = setup_ps.enter_context(nc.sbuf_tensor("rowsb", [1, 8 * D], F32))
    rwork = setup_ps.enter_context(nc.sbuf_tensor("rwork", [1, 5 * D], F32))

    sems = {}
    dma_sems = set()
    def S(name):
        if name not in sems:
            sems[name] = [es.enter_context(nc.semaphore(name)), 0]
        return sems[name]
    def inc(name, instr, n=1):
        s = S(name)
        instr.then_inc(s[0], n)
        s[1] += n
        if n >= 16:
            dma_sems.add(name)
        return instr
    def cur(name):
        return S(name)[1]
    def dinc(name, instr):
        return inc(name, instr, 16)
    waits_log = []
    def W(eng, name, val):
        waits_log.append((name, val))
        eng.wait_ge(S(name)[0], val)
    def fence(eng, instr, name=None):
        if name is None:
            name = "FEN_" + type(eng).__name__
        inc(name, instr)
        W(eng, name, cur(name))
        return instr

    block = es.enter_context(nc.Block())
    nc.const_aps.register(F32, TANH_SC, cst[:, 0:1])
    nc.const_aps.register(F32, EXP_SC, cst[:, 1:2])
    nc.const_aps.register(F32, 128.0, cst[:, 2:3])
    nc.const_aps.register(F32, float(D), cst[:, 3:4])

    bar_n = [0]
    ENGINES = ["sync", "scalar", "vector", "tensor", "gpsimd"]
    def barrier():
        @block.sync
        def _(sy):
            for nm in sorted(dma_sems):
                W(sy, nm, cur(nm))
        bar_n[0] += 1
        tgt = 5 * 16 * bar_n[0]
        for en in ENGINES:
            @getattr(block, en)
            def _(eng):
                inc("BAR", eng.nop(), 16)
        for en in ENGINES:
            @getattr(block, en)
            def _(eng):
                W(eng, "BAR", tgt)

    @block.sync
    def _(sy):
        for dst, src in [
            (cst[:], consts_h), (idb[:], identb_h), (idf[:], identf_h),
            (maskt[:], masks_h), (onescb[:], onescb_h),
            (onesrf[0:1, :], rows_h[7:8, 0:128]),
            (rowsb[0:1, :], rows_h.rearrange("r d -> (r d)")[None, :]),
            (gwf[:].rearrange("p (kb e) -> p kb e", kb=NKB),
             gwT_h.rearrange("(kb p) e -> p kb e", p=128)),
        ]:
            dinc("SETL", sy.dma_start(out=dst, in_=src))
    @block.vector
    def _(v):
        W(v, "SETL", cur("SETL"))
        CE, SE, SG, WQ, WQS, WK, WKS = [rowsb[0:1, i * D:(i + 1) * D] for i in range(7)]
        TMP = rwork[0:1, 4 * D:5 * D]
        v.tensor_mul(rwork[0:1, 0:D], CE, WQ)
        fence(v, v.tensor_mul(TMP, SE, SG))
        v.tensor_mul(rwork[0:1, D:2 * D], TMP, WQS)
        v.tensor_mul(rwork[0:1, 2 * D:3 * D], CE, WK)
        v.tensor_mul(rwork[0:1, 3 * D:4 * D], TMP, WKS)
        fence(v, v.tensor_copy(ghi[:], gwf[:]))
        fence(v, v.tensor_sub(glof[:], gwf[:], ghi[:]))
        inc("SETV", v.tensor_copy(glo[:], glof[:]))
    @block.gpsimd
    def _(g):
        W(g, "SETV", cur("SETV"))
        dinc("SETG0", g.dma_start(out=rsc_h.rearrange("r d -> (r d)")[None, :], in_=rwork[0:1, 0:4 * D]))
        W(g, "SETG0", 16)
        for pr2 in range(8):
            for row, dst, off in ((0, ropeCW, 0), (2, ropeCW, 256), (1, ropeSW, 0), (3, ropeSW, 256)):
                dinc("SETG", g.dma_start(
                    out=dst[:, pr2 * 512 + off:pr2 * 512 + off + 256],
                    in_=rsc_h[row:row + 1, pr2 * 256:(pr2 + 1) * 256].to_broadcast([128, 256])))
    setup_ps.close()

    if phases >= 1:
        with ExitStack() as ps:
            psb = lambda name, shape, dty: ps.enter_context(nc.sbuf_tensor(name, shape, dty))
            wvsb = psb("wvsb", [128, NKB * D], BF16)
            xf = [psb(f"p1xf{i}", [128, D], F32) for i in range(2)]
            xb = [psb(f"p1xb{i}", [128, D], BF16) for i in range(2)]
            xtc = [psb(f"p1xtc{i}", [128, D], BF16) for i in range(2)]
            vst = [psb(f"p1vst{i}", [128, D], BF16) for i in range(2)]
            ptr = ps.enter_context(nc.psum_tensor("p1ptr", [128, 2048], BF16))
            pv = ps.enter_context(nc.psum_tensor("p1pv", [128, 2048], F32))

            @block.gpsimd
            def _(g):
                W(g, "SETL", cur("SETL"))
                for c in range(4):
                    dinc("WVL", g.dma_start(
                        out=wvsb[:].rearrange("p (kb n) -> p kb n", kb=NKB)[:, c * 4:(c + 1) * 4, :],
                        in_=wvT_h.rearrange("(kb p) n -> p kb n", p=128)[:, c * 4:(c + 1) * 4, :]))
                if "d_wvsb" in dump_h:
                    W(g, "P1SV", 16 * NT)
                    dinc("DMPW", g.dma_start(out=dump_h["d_wvsb"], in_=wvsb[:]))
                    W(g, "DMPW", 16)
            @block.sync
            def _(sy):
                W(sy, "SETG", 16 * 32)
                for t in range(NT):
                    if t >= 2:
                        W(sy, "P1XF", t - 1)
                    dinc("P1LX", sy.dma_start(out=xf[t % 2][:], in_=x_h[t * 128:(t + 1) * 128, :]))
            @block.vector
            def _(v):
                for t in range(NT):
                    b = t % 2
                    W(v, "P1LX", 16 * (t + 1))
                    if t >= 2:
                        W(v, "P1TR", t - 1)
                    inc("P1XF", v.tensor_copy(xb[b][:], xf[b][:]))
                    W(v, "P1TR", t + 1)
                    if t >= 2:
                        W(v, "P1SX", 16 * (t - 1))
                    inc("P1EV", v.tensor_copy(xtc[b][:], ptr[:]))
            @block.tensor
            def _(te):
                W(te, "WVL", 16 * 4)
                for t in range(NT):
                    b = t % 2
                    W(te, "P1XF", t + 1)
                    if t >= 1:
                        W(te, "P1EV", t)
                    for k in range(NKB):
                        instr = te.transpose(ptr[:, k * 128:(k + 1) * 128], xb[b][:, k * 128:(k + 1) * 128], idb[:])
                    inc("P1TR", instr)
                    W(te, "P1EV", t + 1)
                    if t >= 1:
                        W(te, "P1VE", t)
                    for d in range(4):
                        for k in range(NKB):
                            instr = te.matmul(pv[:, d * 512:(d + 1) * 512],
                                              xtc[b][:, k * 128:(k + 1) * 128],
                                              wvsb[:, k * D + d * 512:k * D + d * 512 + 512],
                                              start=(k == 0), stop=(k == NKB - 1),
                                              skip_group_check=True)
                    inc("P1MM", instr)
            @block.scalar
            def _(sc):
                for t in range(NT):
                    b = t % 2
                    W(sc, "P1MM", t + 1)
                    if t >= 2:
                        W(sc, "P1SV", 16 * (t - 1))
                    inc("P1VE", sc.copy(vst[b][:], pv[:]))
                    W(sc, "P1VE", t + 1)
                    dinc("P1SV", sc.dma_start(out=vtok_h[t * 128:(t + 1) * 128, :], in_=vst[b][:]))
                    W(sc, "P1EV", t + 1)
                    dinc("P1SX", sc.dma_start(
                        out=xT_h[:, t * 128:(t + 1) * 128].rearrange("(kb p) c -> p kb c", p=128),
                        in_=xtc[b][:].rearrange("p (kb c) -> p kb c", kb=NKB)))
            barrier()

    if phases >= 2:
        pre = [es.enter_context(nc.sbuf_tensor(f"preb{i}", [128, 2 * MD], BF16)) for i in range(2)]
        @block.gpsimd
        def _(g):
            W(g, "SETL", cur("SETL"))
            nbuf = [0]
            def stage(dst_ap, src_ap, nelem, view=None):
                b = nbuf[0] % 2
                if nbuf[0] >= 2:
                    W(g, "PRES", 16 * (nbuf[0] - 1))
                dinc("PREL", g.dma_start(out=pre[b][:, 0:nelem], in_=src_ap))
                W(g, "PREL", cur("PREL"))
                buf = pre[b][:, 0:nelem] if view is None else view(pre[b][:, 0:nelem])
                dinc("PRES", g.dma_start(out=dst_ap, in_=buf))
                nbuf[0] += 1
            for W_h, WB_h in [(wqT_h, wqB_h), (wkT_h, wkB_h), (woT_h, woB_h)]:
                for c in range(16):
                    stage(WB_h.rearrange("(kb p) n -> p kb n", p=128)[:, c:c + 1, :],
                          W_h.rearrange("(kb p) n -> p kb n", p=128)[:, c:c + 1, :],
                          D)
            for W_h, WB_h in [(wgT_h, wgB_h), (wuT_h, wuB_h)]:
                for e in range(E):
                    for kb in range(NKB):
                        stage(WB_h[e, :, :, kb * 128:(kb + 1) * 128].rearrange("m p c -> p m c"),
                              W_h[e, kb * 128:(kb + 1) * 128, :],
                              MD,
                              view=lambda bb: bb.rearrange("p (m c) -> p m c", m=NMB))
            for e in range(E):
                for m in range(NMB):
                    stage(wdB_h[e, m * 128:(m + 1) * 128, :],
                          wdT_h[e, m * 128:(m + 1) * 128, :],
                          D)

    if phases >= 2 and p2sub >= 1:
        with ExitStack() as ps:
            psb = lambda name, shape, dty: ps.enter_context(nc.sbuf_tensor(name, shape, dty))
            wq_sl = [psb(f"wqsl{i}", [128, NKB * 256], BF16) for i in range(2)]
            wk_sl = [psb(f"wksl{i}", [128, NKB * 256], BF16) for i in range(2)]
            xtc2 = [psb(f"p2xtc{i}", [128, D], BF16) for i in range(2)]
            qkTp = [psb(f"qkTp{i}", [128, 4 * T], BF16) for i in range(2)]
            vhb = [psb(f"vhb{i}", [128, T], BF16) for i in range(2)]
            ostg = [psb(f"ostg{i}", [128, T], BF16) for i in range(2)]
            pqs = [psb(f"pqs{i}", [128, 512], F32) for i in range(2)]
            qrs = psb("qrs", [128, 512], F32)
            jro = psb("jro", [128, 512], F32)
            sqv = jro
            jro2 = psb("jro2", [128, 512], F32)
            qbb = psb("qbb", [128, 512], BF16)
            msd = psb("msd", [128, 16], F32)
            ts1P = [psb(f"ts1{i}", [128, 1024], F32) for i in range(2)]
            ebufP = [psb(f"ebuf{i}", [128, 1024], BF16) for i in range(3)]
            etmpP = psb("etmp", [128, 1024], BF16)
            rdb = psb("rdb", [1, 512], F32)
            rbb = psb("rbb", [128, 512], F32)

            pqk = ps.enter_context(nc.psum_tensor("p2qk", [128, 512], F32))
            pq = pqk[:, 0:256]
            pk = pqk[:, 256:512]
            ptr2_full = ps.enter_context(nc.psum_tensor("p2tr", [128, 1024], BF16))
            ptr2 = ptr2_full[:, 0:512]
            pssP = [ps.enter_context(nc.psum_tensor(f"p2sp{i}", [128, 1024], F32)) for i in range(2)]
            pso = ps.enter_context(nc.psum_tensor("p2o", [128, 512], F32))
            psd = ps.enter_context(nc.psum_tensor("p2d", [128, 512], F32))

            gpi = [0]

            def emit_loads(pr):
                pb = pr % 2
                @block.sync
                def _(sy):
                    if pr == 0:
                        W(sy, "PRES", 16 * 32)
                        W(sy, "P1SX", 16 * NT)
                        W(sy, "P1SV", 16 * NT)
                    if pr >= 2:
                        W(sy, "P2TE", ntt * (pr - 1))
                    dinc("WQL", sy.dma_start(
                        out=wq_sl[pb][:].rearrange("p (kb c) -> p kb c", kb=NKB),
                        in_=wqB_h[:, pr * 256:(pr + 1) * 256].rearrange("(kb p) c -> p kb c", p=128)))
                    dinc("WQL", sy.dma_start(
                        out=wk_sl[pb][:].rearrange("p (kb c) -> p kb c", kb=NKB),
                        in_=wkB_h[:, pr * 256:(pr + 1) * 256].rearrange("(kb p) c -> p kb c", p=128)))

            def emit_xload(pr, t):
                @block.sync
                def _(sy):
                    need = pr * ntt + t
                    if need >= 2:
                        W(sy, "P2MMQ", need - 1)
                    dinc("P2XL", sy.dma_start(
                        out=xtc2[need % 2][:].rearrange("p (kb c) -> p kb c", kb=NKB),
                        in_=xT_h[:, t * 128:(t + 1) * 128].rearrange("(kb p) c -> p kb c", p=128)))

            def emit_prep_tile(pr, t):
                pb = pr % 2
                gi = pr * ntt + t
                xb3 = gi % 2
                @block.tensor
                def _(te):
                    W(te, "P2XL", 16 * (gi + 1))
                    W(te, "WQL", 32 * (pr + 1))
                    if gi >= 1:
                        W(te, "P2CP", gi)
                    for k in range(NKB):
                        te.matmul(pq, xtc2[xb3][:, k * 128:(k + 1) * 128],
                                  wq_sl[pb][:, k * 256:(k + 1) * 256],
                                  start=(k == 0), stop=(k == NKB - 1), skip_group_check=True)
                    for k in range(NKB):
                        instr = te.matmul(pk, xtc2[xb3][:, k * 128:(k + 1) * 128],
                                          wk_sl[pb][:, k * 256:(k + 1) * 256],
                                          start=(k == 0), stop=(k == NKB - 1), skip_group_check=True)
                    inc("P2MMQ", instr)
                @block.vector
                def _(v):
                    W(v, "P2MMQ", gi + 1)
                    if gi >= 2:
                        W(v, "P2RC", gi - 1)
                    inc("P2CP", v.tensor_copy(pqs[gi % 2][:], pqk[:]))
                    W(v, "P2CP", cur("P2CP"))
                    fence(v, v.tensor_mul(sqv[:], pqs[gi % 2][:], pqs[gi % 2][:]))
                    for s2 in range(4):
                        instr = v.tensor_reduce(msd[:, s2:s2 + 1], sqv[:, s2 * 128:(s2 + 1) * 128],
                                                AX.X, ALU.add)
                    fence(v, instr)
                    fence(v, v.tensor_scalar(msd[:, 4:8], msd[:, 0:4], 128.0 * 1e-6, None, ALU.add))
                    inc("P2MS", v.reciprocal(msd[:, 8:12], msd[:, 4:8]))
                @block.scalar
                def _(sc):
                    W(sc, "P2MS", gi + 1)
                    if gi >= 1:
                        W(sc, "P2RC", gi)
                    inc("P2RS", sc.activation(msd[:, 12:16], msd[:, 8:12], AF.Sqrt, scale=128.0))
                @block.vector
                def _(v):
                    W(v, "P2RS", gi + 1)
                    if gi >= 1:
                        W(v, "P2TR", gi)
                    for s2 in range(4):
                        instr = v.tensor_scalar(qrs[:, s2 * 128:(s2 + 1) * 128],
                                                pqs[gi % 2][:, s2 * 128:(s2 + 1) * 128],
                                                msd[:, 12 + s2:13 + s2], None, ALU.mult)
                    fence(v, instr)
                    CW = ropeCW[:, pr * 512:(pr + 1) * 512]
                    SW = ropeSW[:, pr * 512:(pr + 1) * 512]
                    q3 = qrs[:].rearrange("p (n two) -> p n two", two=2)
                    j3 = jro2[:].rearrange("p (n two) -> p n two", two=2)
                    s3 = SW.rearrange("p (n two) -> p n two", two=2)
                    v.tensor_mul(jro[:], qrs[:], CW)
                    v.tensor_mul(j3[:, :, 0:1], q3[:, :, 1:2], s3[:, :, 0:1])
                    fence(v, v.tensor_mul(j3[:, :, 1:2], q3[:, :, 0:1], s3[:, :, 1:2]))
                    inc("P2RC", v.tensor_add(qbb[:], jro[:], jro2[:]))
                @block.tensor
                def _(te):
                    W(te, "P2RC", gi + 1)
                    if gi >= 1:
                        W(te, "P2TE", gi)
                    for s2 in range(4):
                        instr = te.transpose(ptr2_full[:, s2 * 128:(s2 + 1) * 128],
                                             qbb[:, s2 * 128:(s2 + 1) * 128], idb[:])
                    inc("P2TR", instr)
                @block.vector
                def _(v):
                    W(v, "P2TR", gi + 1)
                    if t == 0 and pr >= 2 and p2sub >= 2:
                        W(v, "TAV", (2 * pr - 2) * 72)
                    instr = v.tensor_copy(
                        qkTp[pb][:].rearrange("p (g tt) -> p g tt", g=4)[:, :, t * 128:(t + 1) * 128],
                        ptr2_full[:, 0:512].rearrange("p (g c) -> p g c", g=4))
                    inc("P2TE", instr)

            def emit_attn_tile(pr, hl, n):
                pb = pr % 2
                h = 2 * pr + hl
                QOF = hl * T
                KOF = 2 * T + hl * T
                ni = 4 * n + 4
                np2 = ni // 2
                hn = h * NN + n
                ap0 = gpi[0]
                tanh_fence = {}

                def emit_mms(pi):
                    ap = ap0 + pi
                    @block.tensor
                    def _(te):
                        if pi == 0:
                            W(te, "P2TE", ntt * (pr + 1))
                            W(te, "VHL", 16 * (h + 1))
                            if hn >= 1:
                                W(te, "OE", hn)
                        if ap >= 2:
                            W(te, "TH", ap - 1)
                        te.matmul(pssP[ap % 2][:, 0:512],
                                  qkTp[pb][:, KOF + 2 * pi * 128:KOF + (2 * pi + 1) * 128],
                                  qkTp[pb][:, QOF + n * 512:QOF + (n + 1) * 512],
                                  start=True, stop=True, skip_group_check=True)
                        instr = te.matmul(pssP[ap % 2][:, 512:1024],
                                          qkTp[pb][:, KOF + (2 * pi + 1) * 128:KOF + (2 * pi + 2) * 128],
                                          qkTp[pb][:, QOF + n * 512:QOF + (n + 1) * 512],
                                          start=True, stop=True, skip_group_check=True)
                        inc("TS", instr)

                def emit_tanh(pi):
                    ap = ap0 + pi
                    @block.scalar
                    def _(sc):
                        W(sc, "TS", ap + 1)
                        instr = sc.activation(ts1P[ap % 2][:], pssP[ap % 2][:], AF.Tanh, scale=TANH_SC)
                        inc("FEN_SC_TANH", instr)
                        tanh_fence[pi] = cur("FEN_SC_TANH")

                def emit_exp(pi):
                    ap = ap0 + pi
                    diag = pi >= 2 * n
                    @block.scalar
                    def _(sc):
                        W(sc, "FEN_SC_TANH", tanh_fence[pi])
                        if ap >= 3:
                            W(sc, "TAV", ap - 2)
                        if diag:
                            W(sc, "VM", cur("VM"))
                        instr = sc.activation(etmpP[:] if diag else ebufP[ap % 3][:],
                                              ts1P[ap % 2][:], AF.Exp, scale=EXP_SC)
                        inc("TH", instr)
                    if diag:
                        @block.vector
                        def _(v):
                            W(v, "TH", ap + 1)
                            for half in range(2):
                                off_idx = 2 * pi - 4 * n + half
                                inc("VM", v.tensor_mul(
                                    ebufP[ap % 3][:, half * 512:(half + 1) * 512],
                                    etmpP[:, half * 512:(half + 1) * 512],
                                    maskt[:, off_idx * 512:(off_idx + 1) * 512]))

                def emit_dav(pi):
                    ap = ap0 + pi
                    diag = pi >= 2 * n
                    @block.tensor
                    def _(te):
                        W(te, "TH", ap + 1)
                        if diag:
                            W(te, "VM", cur("VM"))
                        for half in range(2):
                            i = 2 * pi + half
                            te.matmul(psd[0:1, :], onescb[:],
                                      ebufP[ap % 3][:, half * 512:(half + 1) * 512],
                                      start=(i == 0), stop=(i == ni - 1), skip_group_check=True)
                            instr = te.matmul(pso[:],
                                              vhb[h % 2][:, i * 128:(i + 1) * 128],
                                              ebufP[ap % 3][:, half * 512:(half + 1) * 512],
                                              start=(i == 0), stop=(i == ni - 1), skip_group_check=True)
                        inc("TAV", instr)

                for pi in range(np2):
                    emit_mms(pi)
                    emit_tanh(pi)
                    if pi >= 1:
                        emit_exp(pi - 1)
                        emit_dav(pi - 1)
                emit_exp(np2 - 1)
                emit_dav(np2 - 1)
                gpi[0] += np2

                @block.vector
                def _(v):
                    W(v, "TAV", gpi[0])
                    inc("RD", v.reciprocal(rdb[0:1, :], psd[0:1, :]))
                @block.tensor
                def _(te):
                    W(te, "RD", hn + 1)
                    instr = te.matmul(pssP[gpi[0] % 2][:, 0:512], onesrf[0:1, :], rdb[0:1, :],
                                      start=True, stop=True, skip_group_check=True)
                    inc("TB", instr)
                @block.vector
                def _(v):
                    W(v, "TB", hn + 1)
                    if n == 0 and h >= 2:
                        W(v, "AOS", 16 * (h - 1))
                    fence(v, v.tensor_copy(rbb[:], pssP[gpi[0] % 2][:, 0:512]))
                    instr = v.tensor_mul(ostg[hl][:, n * 512:(n + 1) * 512], pso[:], rbb[:])
                    inc("OE", instr)

            def emit_vload(h):
                @block.sync
                def _(sy):
                    if h >= 2:
                        W(sy, "VHU", h - 1)
                    dinc("VHL", sy.dma_start(
                        out=vhb[h % 2][:].rearrange("p (s c) -> p s c", s=32),
                        in_=vtok_h[:, h * 128:(h + 1) * 128].rearrange("(s p) c -> p s c", p=128)))

            emit_loads(0)
            for t in range(ntt):
                emit_xload(0, t)
                emit_prep_tile(0, t)
            for pr in range(npr):
                if pr + 1 < npr:
                    emit_loads(pr + 1)
                natt = 0
                for hl in (range(2) if p2sub >= 2 else []):
                    h = 2 * pr + hl
                    emit_vload(h)
                    for n in range(NN):
                        emit_attn_tile(pr, hl, n)
                        if pr + 1 < npr:
                            for t in range(2 * natt, min(2 * natt + 2, ntt)):
                                emit_xload(pr + 1, t)
                                emit_prep_tile(pr + 1, t)
                        natt += 1
                    @block.tensor
                    def _(te):
                        W(te, "OE", (h + 1) * NN)
                        inc("VHU", te.nop())
                    @block.scalar
                    def _(sc):
                        W(sc, "OE", (h + 1) * NN)
                        dinc("AOS", sc.dma_start(out=aoT_h[h * 128:(h + 1) * 128, :], in_=ostg[hl][:]))
                if p2sub < 2 and pr + 1 < npr:
                    for t in range(ntt):
                        emit_xload(pr + 1, t)
                        emit_prep_tile(pr + 1, t)
            barrier()

    if phases >= 3:
        with ExitStack() as ps:
            psb = lambda name, shape, dty: ps.enter_context(nc.sbuf_tensor(name, shape, dty))
            wosb = psb("wosb", [128, NKB * D], BF16)
            aot = [psb(f"aot{i}", [128, D], BF16) for i in range(2)]
            xres = [psb(f"xres{i}", [128, D], F32) for i in range(2)]
            sres = psb("sres", [128, D], F32)
            junk3 = sres
            stats = psb("stats", [128, 8], F32)
            dscr = psb("dscr", [128, D], F32)
            h1t = [psb(f"h1t{i}", [128, D], F32) for i in range(2)]
            hib = psb("hib", [128, D], BF16)
            lob = psb("lob", [128, D], BF16)
            hTst0 = psb("hTst0", [128, NKB * 512], BF16)
            hTst = [hTst0, hTst0]
            loTst = psb("loTst", [128, NKB * 512], BF16)
            lgsb = psb("lgsb", [128, 512], F32)
            ltok = psb("ltok", [128, 16], F32)
            gsc = psb("gsc", [128, 64], F32)

            po3 = [ps.enter_context(nc.psum_tensor(f"p3o{i}", [128, 512], F32)) for i in range(2)]
            ptrh = ps.enter_context(nc.psum_tensor("p3trh", [128, 2048], BF16))
            ptrl = ps.enter_context(nc.psum_tensor("p3trl", [128, 2048], BF16))
            plg = ps.enter_context(nc.psum_tensor("p3lg", [128, 512], F32))
            ptt = ps.enter_context(nc.psum_tensor("p3tt", [128, 128], F32))

            @block.vector
            def _(v):
                inc("LGZ", v.memset(lgsb[:], 0.0))
            @block.sync
            def _(sy):
                W(sy, "PRES", 16 * 48)
                W(sy, "AOS", 16 * H)
                for c in range(4):
                    dinc("WOL", sy.dma_start(
                        out=wosb[:].rearrange("p (kb n) -> p kb n", kb=NKB)[:, c * 4:(c + 1) * 4, :],
                        in_=woB_h.rearrange("(kb p) n -> p kb n", p=128)[:, c * 4:(c + 1) * 4, :]))
                for t in range(nt3):
                    b = t % 2
                    if t >= 2:
                        W(sy, "P3MM", 4 * (t - 1))
                        W(sy, "P3SE", 4 * (t - 1))
                    dinc("P3AL", sy.dma_start(
                        out=aot[b][:].rearrange("p (kb c) -> p kb c", kb=NKB),
                        in_=aoT_h[:, t * 128:(t + 1) * 128].rearrange("(kb p) c -> p kb c", p=128)))
                    dinc("P3XL", sy.dma_start(out=xres[b][:], in_=x_h[t * 128:(t + 1) * 128, :]))

            for t in range(nt3):
                b = t % 2
                n = t // 4
                nb = n % 2
                @block.tensor
                def _(te):
                    W(te, "P3AL", 16 * (t + 1))
                    if t == 0:
                        W(te, "WOL", 16 * 4)
                    for d in range(4):
                        if 4 * t + d >= 2:
                            W(te, "P3SE", 4 * t + d - 1)
                        if p3x >= 2:
                            inc("P3MM", te.nop())
                            continue
                        for k in range(NKB):
                            instr = te.matmul(po3[d % 2][:],
                                              aot[b][:, k * 128:(k + 1) * 128],
                                              wosb[:, k * D + d * 512:k * D + d * 512 + 512],
                                              start=(k == 0), stop=(k == NKB - 1))
                        inc("P3MM", instr)
                @block.vector
                def _(v):
                    W(v, "P3XL", 16 * (t + 1))
                    for d in range(4):
                        W(v, "P3MM", 4 * t + d + 1)
                        if p3x >= 2:
                            inc("P3SE", v.tensor_copy(sres[:, d * 512:(d + 1) * 512],
                                                      xres[b][:, d * 512:(d + 1) * 512]))
                        else:
                            inc("P3SE", v.tensor_add(sres[:, d * 512:(d + 1) * 512],
                                                     xres[b][:, d * 512:(d + 1) * 512], po3[d % 2][:]))
                    if p3x >= 1:
                        inc("P3LN", v.tensor_copy(dscr[:], sres[:]))
                    else:
                        fence(v, v.tensor_reduce(stats[:, 0:1], sres[:], AX.X, ALU.add))
                        fence(v, v.tensor_scalar(stats[:, 1:2], stats[:, 0:1], 1.0 / D, None, ALU.mult))
                        inc("P3LN", v.tensor_scalar(dscr[:], sres[:], stats[:, 1:2], None, ALU.subtract))
                @block.scalar
                def _(sc):
                    W(sc, "P3LN", t + 1)
                    if p3x >= 1:
                        inc("P3SQ", sc.nop())
                    else:
                        inc("P3SQ", sc.activation(junk3[:], dscr[:], AF.Square, accum_out=stats[:, 2:3]))
                @block.vector
                def _(v):
                    W(v, "P3SQ", t + 1)
                    if p3x >= 1:
                        inc("P3RC", v.engine_nop())
                    else:
                        fence(v, v.tensor_scalar(stats[:, 3:4], stats[:, 2:3], float(D) * 1e-5, None, ALU.add))
                        inc("P3RC", v.reciprocal(stats[:, 4:5], stats[:, 3:4]))
                @block.scalar
                def _(sc):
                    W(sc, "P3RC", t + 1)
                    if p3x >= 1:
                        inc("P3SR", sc.nop())
                    else:
                        inc("P3SR", sc.activation(stats[:, 5:6], stats[:, 4:5], AF.Sqrt, scale=float(D)))
                @block.vector
                def _(v):
                    W(v, "P3SR", t + 1)
                    if t >= 2:
                        W(v, "P3H1S", 16 * (t - 1))
                    if p3x >= 1:
                        inc("P3HD", v.tensor_copy(h1t[b][:], dscr[:]))
                    else:
                        inc("P3HD", v.tensor_scalar(h1t[b][:], dscr[:], stats[:, 5:6], None, ALU.mult))
                    W(v, "P3HD", cur("P3HD"))
                    if t >= 1 and p3sub >= 2:
                        W(v, "P3TR", t)
                    fence(v, v.tensor_copy(hib[:], h1t[b][:]))
                    inc("P3HL", v.tensor_sub(lob[:], h1t[b][:], hib[:]))
                @block.scalar
                def _(sc):
                    W(sc, "P3HD", t + 1)
                    dinc("P3H1S", sc.dma_start(out=h1_h[t * 128:(t + 1) * 128, :], in_=h1t[b][:]))
                if p3sub < 2:
                    continue
                @block.tensor
                def _(te):
                    W(te, "P3HL", t + 1)
                    if t >= 1:
                        W(te, "P3TE", 2 * t)
                    for k in range(NKB):
                        te.transpose(ptrh[:, k * 128:(k + 1) * 128], hib[:, k * 128:(k + 1) * 128], idb[:])
                        instr = te.transpose(ptrl[:, k * 128:(k + 1) * 128], lob[:, k * 128:(k + 1) * 128], idb[:])
                    inc("P3TR", instr)
                @block.vector
                def _(v):
                    W(v, "P3TR", t + 1)
                    if t % 4 == 0 and n >= 1 and p3sub >= 3:
                        W(v, "P3GD", n)
                        W(v, "P3HTS", 16 * n)
                        W(v, "P3GM", n)
                    v.tensor_copy(
                        hTst[nb][:].rearrange("p (kb c) -> p kb c", kb=NKB)[:, :, (t % 4) * 128:(t % 4 + 1) * 128],
                        ptrh[:].rearrange("p (kb c) -> p kb c", kb=NKB))
                    instr = v.tensor_copy(
                        loTst[:].rearrange("p (kb c) -> p kb c", kb=NKB)[:, :, (t % 4) * 128:(t % 4 + 1) * 128],
                        ptrl[:].rearrange("p (kb c) -> p kb c", kb=NKB))
                    inc("P3TE", instr, 2)
                if t % 4 == 3 and p3sub >= 3:
                    @block.tensor
                    def _(te):
                        W(te, "P3TE", 2 * (t + 1))
                        if n == 0:
                            W(te, "SETV", cur("SETV"))
                        if n >= 1:
                            W(te, "P3GE", n)
                        nmm = 0
                        for k in range(NKB):
                            for (lw, rh) in [(ghi, hTst[nb]), (glo, hTst[nb]), (ghi, loTst)]:
                                nmm += 1
                                instr = te.matmul(plg[0:16, :], lw[:, k * 16:(k + 1) * 16],
                                                  rh[:, k * 512:(k + 1) * 512],
                                                  start=(nmm == 1), stop=(nmm == 3 * NKB))
                        inc("P3GM", instr)
                    @block.vector
                    def _(v):
                        W(v, "P3GM", n + 1)
                        W(v, "LGZ", 1)
                        if n >= 1:
                            W(v, "P3TT", 4 * n)
                        inc("P3GE", v.tensor_copy(lgsb[0:16, :], plg[0:16, :]))
                    for c in range(4):
                        @block.tensor
                        def _(te):
                            W(te, "P3GE", n + 1)
                            if 4 * n + c >= 1:
                                W(te, "P3TTE", 4 * n + c)
                            inc("P3TT", te.transpose(ptt[:], lgsb[:, c * 128:(c + 1) * 128], idf[:]))
                        @block.vector
                        def _(v):
                            W(v, "P3TT", 4 * n + c + 1)
                            tt = n * 4 + c
                            if tt >= 1:
                                W(v, "P3EX", tt)
                            inc("P3TTE", v.tensor_copy(ltok[:], ptt[:, 0:16]))
                            W(v, "P3TTE", cur("P3TTE"))
                            fence(v, v.max(gsc[:, 0:8], ltok[:]))
                            inc("P3M8", v.tensor_scalar(gsc[:, 8:24], ltok[:], gsc[:, 5:6], None, ALU.is_ge))
                        @block.scalar
                        def _(sc):
                            W(sc, "P3M8", 4 * n + c + 1)
                            inc("P3EX", sc.activation(gsc[:, 24:40], ltok[:], AF.Exp))
                        @block.vector
                        def _(v):
                            W(v, "P3EX", 4 * n + c + 1)
                            tt = n * 4 + c
                            fence(v, v.tensor_reduce(gsc[:, 40:41], gsc[:, 24:40], AX.X, ALU.add))
                            fence(v, v.reciprocal(gsc[:, 41:42], gsc[:, 40:41]))
                            fence(v, v.tensor_scalar(gsc[:, 42:58], gsc[:, 24:40], gsc[:, 41:42], None, ALU.mult))
                            instr = v.tensor_mul(cwsb[:, tt * 16:(tt + 1) * 16], gsc[:, 42:58], gsc[:, 8:24])
                            inc("P3CW", instr)
                            if c == 3:
                                inc("P3GD", v.engine_nop())
                    @block.scalar
                    def _(sc):
                        W(sc, "P3TE", 2 * (t + 1))
                        dinc("P3HTS", sc.dma_start(
                            out=hT_h[:, n * 512:(n + 1) * 512].rearrange("(kb p) c -> p kb c", p=128),
                            in_=hTst[nb][:].rearrange("p (kb c) -> p kb c", kb=NKB)))
            barrier()

    if phases >= 5:
        with ExitStack() as ps:
            psb = lambda name, shape, dty: ps.enter_context(nc.sbuf_tensor(name, shape, dty))
            wdsb = psb("wdsb", [128, NMB * D], BF16)
            wgsl = [psb(f"wgsl{i}", [128, D], BF16) for i in range(2)]
            wusl = [psb(f"wusl{i}", [128, D], BF16) for i in range(2)]
            hTn = [psb(f"hTn{i}", [128, NKB * 512], BF16) for i in range(2)]
            aTn = psb("aTn", [128, NMB * 512], BF16)
            sg5 = psb("sg5", [128, 512], F32)
            ysb = [psb(f"ysb{i}", [128, D], F32) for i in range(2)]

            pg5 = [ps.enter_context(nc.psum_tensor(f"p5g{i}", [128, 512], F32)) for i in range(2)]
            pu5 = [ps.enter_context(nc.psum_tensor(f"p5u{i}", [128, 512], F32)) for i in range(2)]
            py5 = [ps.enter_context(nc.psum_tensor(f"p5y{i}", [128, 512], F32)) for i in range(2)]

            nws = [0]
            nyt = [0]
            for e in range(E):
                @block.sync
                def _(sy):
                    if e == 0:
                        W(sy, "PRES", cur("PRES"))
                        W(sy, "P3HTS", 16 * NN)
                        W(sy, "P3GD", NN)
                        W(sy, "P3H1S", 16 * NT)
                    if e >= 1:
                        W(sy, "P5YM", e * NN * 16)
                    dinc("P5WD", sy.dma_start(
                        out=wdsb[:].rearrange("p (mb c) -> p mb c", mb=NMB),
                        in_=wdB_h[e].rearrange("(mb p) c -> p mb c", p=128)))
                for n in range(NN):
                    gn = e * NN + n
                    hb = gn % 2
                    @block.sync
                    def _(sy):
                        if gn >= 2:
                            W(sy, "P5GU", (gn - 1) * NMB)
                        dinc("P5HL", sy.dma_start(
                            out=hTn[hb][:].rearrange("p (kb c) -> p kb c", kb=NKB),
                            in_=hT_h[:, n * 512:(n + 1) * 512].rearrange("(kb p) c -> p kb c", p=128)))
                        for m in range(NMB):
                            wb = nws[0] % 2
                            if nws[0] >= 2:
                                W(sy, "P5GU", nws[0] - 1)
                            dinc("P5WL", sy.dma_start(out=wgsl[wb][:], in_=wgB_h[e, m]))
                            dinc("P5WL", sy.dma_start(out=wusl[wb][:], in_=wuB_h[e, m]))
                            nws[0] += 1
                    for m in range(NMB):
                        gm = gn * NMB + m
                        wb = gm % 2
                        gb = gm % 2
                        @block.tensor
                        def _(te):
                            W(te, "P5HL", 16 * (gn + 1))
                            W(te, "P5WL", 32 * (gm + 1))
                            if gm >= 2:
                                W(te, "P5AE", gm - 1)
                            for k in range(NKB):
                                te.matmul(pg5[gb][:], wgsl[wb][:, k * 128:(k + 1) * 128],
                                          hTn[hb][:, k * 512:(k + 1) * 512],
                                          start=(k == 0), stop=(k == NKB - 1),
                                          skip_group_check=True)
                                instr = te.matmul(pu5[gb][:], wusl[wb][:, k * 128:(k + 1) * 128],
                                                  hTn[hb][:, k * 512:(k + 1) * 512],
                                                  start=(k == 0), stop=(k == NKB - 1),
                                                  skip_group_check=True)
                            inc("P5GU", instr)
                        @block.scalar
                        def _(sc):
                            W(sc, "P5GU", gm + 1)
                            if gm >= 1:
                                W(sc, "P5AE", gm)
                            inc("P5SI", sc.activation(sg5[:], pg5[gb][:], AF.Silu))
                        @block.vector
                        def _(v):
                            W(v, "P5SI", gm + 1)
                            if gn >= 1 and m == 0:
                                W(v, "P5YM", gn * 16)
                            inc("P5AE", v.tensor_mul(aTn[:, m * 512:(m + 1) * 512], sg5[:], pu5[gb][:]))
                    for tj in range(4):
                        yb = (4 * gn + tj) % 2
                        for dd in range(4):
                            yp = nyt[0] % 2
                            @block.tensor
                            def _(te):
                                W(te, "P5AE", (gn + 1) * NMB)
                                if n == 0 and tj == 0 and dd == 0:
                                    W(te, "P5WD", 16 * (e + 1))
                                if nyt[0] >= 2:
                                    W(te, "P5YE", nyt[0] - 1)
                                for m in range(NMB):
                                    instr = te.matmul(py5[yp][:],
                                                      aTn[:, m * 512 + tj * 128:m * 512 + (tj + 1) * 128],
                                                      wdsb[:, m * D + dd * 512:m * D + dd * 512 + 512],
                                                      start=(m == 0), stop=(m == NMB - 1))
                                inc("P5YM", instr)
                            @block.vector
                            def _(v):
                                W(v, "P5YM", nyt[0] + 1)
                                if nyt[0] >= 8:
                                    W(v, "P5YS", 16 * (nyt[0] // 4 - 1))
                                tt = n * 4 + tj
                                inc("P5YE", v.tensor_scalar(ysb[yb][:, dd * 512:(dd + 1) * 512], py5[yp][:],
                                                            cwsb[:, tt * 16 + e:tt * 16 + e + 1], None, ALU.mult))
                            nyt[0] += 1
                        @block.gpsimd
                        def _(g):
                            W(g, "P5YE", nyt[0])
                            tt = n * 4 + tj
                            if e == 0:
                                instr = g.dma_start(out=yac_h[tt * 128:(tt + 1) * 128, :], in_=ysb[yb][:])
                            else:
                                instr = g.dma_start(out=yac_h[tt * 128:(tt + 1) * 128, :], in_=ysb[yb][:],
                                                    accum_op=ALU.add)
                            dinc("P5YS", instr)
            barrier()

    if phases >= 6:
        with ExitStack() as ps:
            psb = lambda name, shape, dty: ps.enter_context(nc.sbuf_tensor(name, shape, dty))
            yld = [psb(f"yld{i}", [128, D], F32) for i in range(2)]
            hld = [psb(f"hld{i}", [128, D], F32) for i in range(2)]
            s6 = psb("s6", [128, D], F32)
            junk6 = psb("junk6", [128, D], F32)
            st6 = psb("st6", [128, 8], F32)
            dscr6 = psb("dscr6", [128, D], F32)
            ot6 = [psb(f"ot6{i}", [128, D], F32) for i in range(2)]
            @block.sync
            def _(sy):
                W(sy, "P5YS", 16 * NT * E)
                W(sy, "P3H1S", 16 * NT)
                for t in range(NT):
                    b = t % 2
                    if t >= 2:
                        W(sy, "P6A", t - 1)
                    dinc("P6L", sy.dma_start(out=yld[b][:], in_=yac_h[t * 128:(t + 1) * 128, :]))
                    dinc("P6L", sy.dma_start(out=hld[b][:], in_=h1_h[t * 128:(t + 1) * 128, :]))
            for t in range(NT):
                b = t % 2
                @block.vector
                def _(v):
                    W(v, "P6L", 32 * (t + 1))
                    inc("P6A", v.tensor_add(s6[:], yld[b][:], hld[b][:]))
                    W(v, "P6A", cur("P6A"))
                    fence(v, v.tensor_reduce(st6[:, 0:1], s6[:], AX.X, ALU.add))
                    fence(v, v.tensor_scalar(st6[:, 1:2], st6[:, 0:1], 1.0 / D, None, ALU.mult))
                    inc("P6D", v.tensor_scalar(dscr6[:], s6[:], st6[:, 1:2], None, ALU.subtract))
                @block.scalar
                def _(sc):
                    W(sc, "P6D", t + 1)
                    inc("P6SQ", sc.activation(junk6[:], dscr6[:], AF.Square, accum_out=st6[:, 2:3]))
                @block.vector
                def _(v):
                    W(v, "P6SQ", t + 1)
                    fence(v, v.tensor_scalar(st6[:, 3:4], st6[:, 2:3], float(D) * 1e-5, None, ALU.add))
                    inc("P6RC", v.reciprocal(st6[:, 4:5], st6[:, 3:4]))
                @block.scalar
                def _(sc):
                    W(sc, "P6RC", t + 1)
                    inc("P6SR", sc.activation(st6[:, 5:6], st6[:, 4:5], AF.Sqrt, scale=float(D)))
                @block.vector
                def _(v):
                    W(v, "P6SR", t + 1)
                    if t >= 2:
                        W(v, "P6S", 16 * (t - 1))
                    inc("P6N", v.tensor_scalar(ot6[b][:], dscr6[:], st6[:, 5:6], None, ALU.mult))
                @block.scalar
                def _(sc):
                    W(sc, "P6N", t + 1)
                    dinc("P6S", sc.dma_start(out=out_h[t * 128:(t + 1) * 128, :], in_=ot6[b][:]))
            @block.gpsimd
            def _(g):
                W(g, "P6S", 16 * NT)

    if dump:
        barrier()
        @block.sync
        def _(sy):
            pairs = {"d_xT": xT_h, "d_vtok": vtok_h, "d_aoT": aoT_h, "d_h1": h1_h,
                     "d_hT": hT_h, "d_yac": yac_h}
            for name, ap in dump_h.items():
                if name in pairs:
                    dinc("DMP", sy.dma_start(out=ap, in_=pairs[name]))
                elif name == "d_wgB":
                    dinc("DMP", sy.dma_start(out=ap, in_=wgB_h[0]))
        if "d_cw" in dump_h:
            @block.scalar
            def _(sc):
                dinc("DMP", sc.dma_start(out=dump_h["d_cw"], in_=cwsb[:]))
        @block.gpsimd
        def _(g):
            W(g, "DMP", cur("DMP"))

    for nm, val in waits_log:
        if val > cur(nm):
            raise RuntimeError(f"wait on {nm} for {val} but only {cur(nm)} increments emitted")
    es.close()
    return nc


def _host_prep(inp):
    f32 = np.float32
    x = np.ascontiguousarray(np.asarray(inp["L0_attn_x"], f32))
    wqT = np.ascontiguousarray(np.asarray(inp["L0_attn_wq"], f32).T)
    wkT = np.ascontiguousarray(np.asarray(inp["L0_attn_wk"], f32).T)
    wvT = np.ascontiguousarray(np.asarray(inp["L0_attn_wv"], f32).T)
    woT = np.ascontiguousarray(np.asarray(inp["L0_attn_wo"], f32).T)
    gwT = np.ascontiguousarray(np.asarray(inp["L0_ffn_gate_w"], f32).T)
    wgT = np.ascontiguousarray(np.transpose(np.asarray(inp["L0_ffn_w_gate"], f32), (0, 2, 1)))
    wuT = np.ascontiguousarray(np.transpose(np.asarray(inp["L0_ffn_w_up"], f32), (0, 2, 1)))
    wdT = np.ascontiguousarray(np.transpose(np.asarray(inp["L0_ffn_w_down"], f32), (0, 2, 1)))
    cos16 = np.asarray(inp["L0_attn_rope_cos"], f32)[:H]
    sin16 = np.asarray(inp["L0_attn_rope_sin"], f32)[:H]
    qnw = np.asarray(inp["L0_attn_q_norm_w"], f32)
    knw = np.asarray(inp["L0_attn_k_norm_w"], f32)

    cosE = np.repeat(cos16, 2, axis=1).reshape(-1)
    sinE = np.repeat(sin16, 2, axis=1).reshape(-1)
    sign = np.tile(np.array([-1.0, 1.0], f32), D // 2)
    swap = lambda w: np.ascontiguousarray(w.reshape(-1, 2)[:, ::-1]).reshape(-1)
    rows = np.stack([
        cosE, sinE, sign,
        np.tile(qnw, H), np.tile(swap(qnw), H),
        np.tile(knw, H), np.tile(swap(knw), H),
        np.ones(D, f32),
    ]).astype(f32)

    import ml_dtypes
    masks = np.zeros((128, 2048), np.float32)
    rr = np.arange(128)[:, None]
    cc = np.arange(512)[None, :]
    for oi, off in enumerate((0, 128, 256, 384)):
        masks[:, oi * 512:(oi + 1) * 512] = (cc >= rr + off)
    masks = masks.astype(ml_dtypes.bfloat16)
    identb = np.eye(128, dtype=np.float32).astype(ml_dtypes.bfloat16)
    identf = np.eye(128, dtype=np.float32)
    onescb = np.ones((128, 1), np.float32).astype(ml_dtypes.bfloat16)
    consts = np.tile(np.array([TANH_SC, EXP_SC, 128.0, float(D)], f32)[None, :], (128, 1))

    shared = dict(wqT=wqT, wkT=wkT, wvT=wvT, woT=woT, gwT=gwT,
                  wgT=wgT, wuT=wuT, wdT=wdT, rows=rows, masks=masks,
                  identb=identb, identf=identf, onescb=onescb, consts=consts)
    return x, shared


def kernel(**inp):
    x, shared = _host_prep(inp)
    nc = build(phases=6)
    in_maps = [dict(shared, x=np.ascontiguousarray(x[b])) for b in range(8)]
    res = run_bass_kernel_spmd(nc, in_maps, core_ids=list(range(8)))
    return np.stack([np.asarray(res.results[b]["out"]) for b in range(8)]).astype(np.float32)


# version 05
# speedup vs baseline: 1.1004x; 1.0274x over previous
import numpy as np
from contextlib import ExitStack
import concourse.bass as bass
import concourse.mybir as mybir
from concourse.bass_utils import run_bass_kernel_spmd

F32 = mybir.dt.float32
BF16 = mybir.dt.bfloat16
AF = mybir.ActivationFunctionType
ALU = mybir.AluOpType
AX = mybir.AxisListType

T, D, H, DH = 4096, 2048, 16, 128
E, MD = 16, 1408
NKB = D // 128
NMB = MD // 128
NT = T // 128
NN = T // 512
TANH_SC = float((1.0 / np.sqrt(128.0)) / 50.0)
EXP_SC = 50.0


def build(phases=6, dump=(), p2sub=2, npr=8, ntt=NT, p3sub=3, nt3=NT, p3x=0):
    nc = bass.Bass("TRN2", target_bir_lowering=False)
    dtn = nc.dram_tensor
    x_h = dtn("x", [T, D], F32, kind="ExternalInput").ap()
    wqT_h = dtn("wqT", [D, D], F32, kind="ExternalInput").ap()
    wkT_h = dtn("wkT", [D, D], F32, kind="ExternalInput").ap()
    wvT_h = dtn("wvT", [D, D], F32, kind="ExternalInput").ap()
    woT_h = dtn("woT", [D, D], F32, kind="ExternalInput").ap()
    gwT_h = dtn("gwT", [D, E], F32, kind="ExternalInput").ap()
    wgT_h = dtn("wgT", [E, D, MD], F32, kind="ExternalInput").ap()
    wuT_h = dtn("wuT", [E, D, MD], F32, kind="ExternalInput").ap()
    wdT_h = dtn("wdT", [E, MD, D], F32, kind="ExternalInput").ap()
    rows_h = dtn("rows", [8, D], F32, kind="ExternalInput").ap()
    masks_h = dtn("masks", [128, 2048], BF16, kind="ExternalInput").ap()
    identb_h = dtn("identb", [128, 128], BF16, kind="ExternalInput").ap()
    identf_h = dtn("identf", [128, 128], F32, kind="ExternalInput").ap()
    onescb_h = dtn("onescb", [128, 1], BF16, kind="ExternalInput").ap()
    consts_h = dtn("consts", [128, 4], F32, kind="ExternalInput").ap()

    xT_h = dtn("xTs", [D, T], BF16, kind="Internal").ap()
    vtok_h = dtn("vtoks", [T, D], BF16, kind="Internal").ap()
    aoT_h = dtn("aoTs", [D, T], BF16, kind="Internal").ap()
    h1_h = dtn("h1s", [T, D], F32, kind="Internal").ap()
    hT_h = dtn("hTs", [D, T], BF16, kind="Internal").ap()
    wqB_h = dtn("wqB", [D, D], BF16, kind="Internal").ap()
    wkB_h = dtn("wkB", [D, D], BF16, kind="Internal").ap()
    woB_h = dtn("woB", [D, D], BF16, kind="Internal").ap()
    wgB_h = dtn("wgB", [E, NMB, 128, D], BF16, kind="Internal").ap()
    wuB_h = dtn("wuB", [E, NMB, 128, D], BF16, kind="Internal").ap()
    wdB_h = dtn("wdB", [E, MD, D], BF16, kind="Internal").ap()
    yac_h = dtn("yac", [T, D], F32, kind="Internal").ap()
    rsc_h = dtn("ropescratch", [4, D], F32, kind="Internal").ap()
    out_h = dtn("out", [T, D], F32, kind="ExternalOutput").ap()

    dump_h = {}
    for name, shp, dty in [
        ("d_xT", [D, T], BF16), ("d_vtok", [T, D], BF16),
        ("d_aoT", [D, T], BF16), ("d_h1", [T, D], F32),
        ("d_hT", [D, T], BF16), ("d_cw", [128, NT * E], F32),
        ("d_yac", [T, D], F32), ("d_wgB", [NMB, 128, D], BF16),
        ("d_wvsb", [128, NKB * D], BF16),
        ("d_qT", [128, 2 * T], BF16), ("d_kT", [128, 2 * T], BF16),
        ("d_pqs", [128, 512], F32), ("d_qro", [128, 256], BF16),
        ("d_wqB", [D, D], BF16), ("d_xtc2", [128, D], BF16), ("d_wqsl", [128, NKB * 256], BF16),
        ("d_msd", [128, 16], F32),
    ]:
        if name in dump:
            dump_h[name] = dtn(name, shp, dty, kind="ExternalOutput").ap()

    es = ExitStack()
    sb = lambda name, shape, dty: es.enter_context(nc.sbuf_tensor(name, shape, dty))

    cst = sb("cst", [128, 4], F32)
    idb = sb("idb", [128, 128], BF16)
    idf = sb("idf", [128, 128], F32)
    maskt = sb("maskt", [128, 2048], BF16)
    onescb = sb("s_onescb", [128, 1], BF16)
    onesrf = sb("onesrf", [1, 128], F32)
    ropeCW = sb("ropeCW", [128, 2 * D], BF16)
    ropeSW = sb("ropeSW", [128, 2 * D], BF16)
    ghi = sb("ghi", [128, NKB * E], BF16)
    glo = sb("glo", [128, NKB * E], BF16)
    gwf = sb("gwf", [128, NKB * E], F32)
    glof = sb("glof", [128, NKB * E], F32)
    cwsb = sb("cwsb", [128, NT * E], F32)
    setup_ps = ExitStack()
    rowsb = setup_ps.enter_context(nc.sbuf_tensor("rowsb", [1, 8 * D], F32))
    rwork = setup_ps.enter_context(nc.sbuf_tensor("rwork", [1, 5 * D], F32))

    sems = {}
    dma_sems = set()
    def S(name):
        if name not in sems:
            sems[name] = [es.enter_context(nc.semaphore(name)), 0]
        return sems[name]
    def inc(name, instr, n=1):
        s = S(name)
        instr.then_inc(s[0], n)
        s[1] += n
        if n >= 16:
            dma_sems.add(name)
        return instr
    def cur(name):
        return S(name)[1]
    def dinc(name, instr):
        return inc(name, instr, 16)
    waits_log = []
    def W(eng, name, val):
        waits_log.append((name, val))
        eng.wait_ge(S(name)[0], val)
    def fence(eng, instr, name=None):
        if name is None:
            name = "FEN_" + type(eng).__name__
        inc(name, instr)
        W(eng, name, cur(name))
        return instr

    block = es.enter_context(nc.Block())
    nc.const_aps.register(F32, TANH_SC, cst[:, 0:1])
    nc.const_aps.register(F32, EXP_SC, cst[:, 1:2])
    nc.const_aps.register(F32, 128.0, cst[:, 2:3])
    nc.const_aps.register(F32, float(D), cst[:, 3:4])

    bar_n = [0]
    ENGINES = ["sync", "scalar", "vector", "tensor", "gpsimd"]
    def barrier():
        @block.sync
        def _(sy):
            for nm in sorted(dma_sems):
                W(sy, nm, cur(nm))
        bar_n[0] += 1
        tgt = 5 * 16 * bar_n[0]
        for en in ENGINES:
            @getattr(block, en)
            def _(eng):
                inc("BAR", eng.nop(), 16)
        for en in ENGINES:
            @getattr(block, en)
            def _(eng):
                W(eng, "BAR", tgt)

    @block.sync
    def _(sy):
        for dst, src in [
            (cst[:], consts_h), (idb[:], identb_h), (idf[:], identf_h),
            (maskt[:], masks_h), (onescb[:], onescb_h),
            (onesrf[0:1, :], rows_h[7:8, 0:128]),
            (rowsb[0:1, :], rows_h.rearrange("r d -> (r d)")[None, :]),
            (gwf[:].rearrange("p (kb e) -> p kb e", kb=NKB),
             gwT_h.rearrange("(kb p) e -> p kb e", p=128)),
        ]:
            dinc("SETL", sy.dma_start(out=dst, in_=src))
    @block.vector
    def _(v):
        W(v, "SETL", cur("SETL"))
        CE, SE, SG, WQ, WQS, WK, WKS = [rowsb[0:1, i * D:(i + 1) * D] for i in range(7)]
        TMP = rwork[0:1, 4 * D:5 * D]
        v.tensor_mul(rwork[0:1, 0:D], CE, WQ)
        fence(v, v.tensor_mul(TMP, SE, SG))
        v.tensor_mul(rwork[0:1, D:2 * D], TMP, WQS)
        v.tensor_mul(rwork[0:1, 2 * D:3 * D], CE, WK)
        v.tensor_mul(rwork[0:1, 3 * D:4 * D], TMP, WKS)
        fence(v, v.tensor_copy(ghi[:], gwf[:]))
        fence(v, v.tensor_sub(glof[:], gwf[:], ghi[:]))
        inc("SETV", v.tensor_copy(glo[:], glof[:]))
    @block.gpsimd
    def _(g):
        W(g, "SETV", cur("SETV"))
        dinc("SETG0", g.dma_start(out=rsc_h.rearrange("r d -> (r d)")[None, :], in_=rwork[0:1, 0:4 * D]))
        W(g, "SETG0", 16)
        for pr2 in range(8):
            for row, dst, off in ((0, ropeCW, 0), (2, ropeCW, 256), (1, ropeSW, 0), (3, ropeSW, 256)):
                dinc("SETG", g.dma_start(
                    out=dst[:, pr2 * 512 + off:pr2 * 512 + off + 256],
                    in_=rsc_h[row:row + 1, pr2 * 256:(pr2 + 1) * 256].to_broadcast([128, 256])))
    setup_ps.close()

    if phases >= 1:
        with ExitStack() as ps:
            psb = lambda name, shape, dty: ps.enter_context(nc.sbuf_tensor(name, shape, dty))
            wvsb = psb("wvsb", [128, NKB * D], BF16)
            xf = [psb(f"p1xf{i}", [128, D], F32) for i in range(2)]
            xb = [psb(f"p1xb{i}", [128, D], BF16) for i in range(2)]
            xtc = [psb(f"p1xtc{i}", [128, D], BF16) for i in range(2)]
            vst = [psb(f"p1vst{i}", [128, D], BF16) for i in range(2)]
            ptr = ps.enter_context(nc.psum_tensor("p1ptr", [128, 2048], BF16))
            pv = ps.enter_context(nc.psum_tensor("p1pv", [128, 2048], F32))

            @block.gpsimd
            def _(g):
                W(g, "SETL", cur("SETL"))
                for c in range(4):
                    dinc("WVL", g.dma_start(
                        out=wvsb[:].rearrange("p (kb n) -> p kb n", kb=NKB)[:, c * 4:(c + 1) * 4, :],
                        in_=wvT_h.rearrange("(kb p) n -> p kb n", p=128)[:, c * 4:(c + 1) * 4, :]))
                if "d_wvsb" in dump_h:
                    W(g, "P1SV", 16 * NT)
                    dinc("DMPW", g.dma_start(out=dump_h["d_wvsb"], in_=wvsb[:]))
                    W(g, "DMPW", 16)
            @block.sync
            def _(sy):
                W(sy, "SETG", 16 * 32)
                for t in range(NT):
                    if t >= 2:
                        W(sy, "P1XF", t - 1)
                    dinc("P1LX", sy.dma_start(out=xf[t % 2][:], in_=x_h[t * 128:(t + 1) * 128, :]))
            @block.vector
            def _(v):
                for t in range(NT):
                    b = t % 2
                    W(v, "P1LX", 16 * (t + 1))
                    if t >= 2:
                        W(v, "P1TR", t - 1)
                    inc("P1XF", v.tensor_copy(xb[b][:], xf[b][:]))
                    W(v, "P1TR", t + 1)
                    if t >= 2:
                        W(v, "P1SX", 16 * (t - 1))
                    inc("P1EV", v.tensor_copy(xtc[b][:], ptr[:]))
            @block.tensor
            def _(te):
                W(te, "WVL", 16 * 4)
                for t in range(NT):
                    b = t % 2
                    W(te, "P1XF", t + 1)
                    if t >= 1:
                        W(te, "P1EV", t)
                    for k in range(NKB):
                        instr = te.transpose(ptr[:, k * 128:(k + 1) * 128], xb[b][:, k * 128:(k + 1) * 128], idb[:])
                    inc("P1TR", instr)
                    W(te, "P1EV", t + 1)
                    if t >= 1:
                        W(te, "P1VE", t)
                    for d in range(4):
                        for k in range(NKB):
                            instr = te.matmul(pv[:, d * 512:(d + 1) * 512],
                                              xtc[b][:, k * 128:(k + 1) * 128],
                                              wvsb[:, k * D + d * 512:k * D + d * 512 + 512],
                                              start=(k == 0), stop=(k == NKB - 1),
                                              skip_group_check=True)
                    inc("P1MM", instr)
            @block.scalar
            def _(sc):
                for t in range(NT):
                    b = t % 2
                    W(sc, "P1MM", t + 1)
                    if t >= 2:
                        W(sc, "P1SV", 16 * (t - 1))
                    inc("P1VE", sc.copy(vst[b][:], pv[:]))
                    W(sc, "P1VE", t + 1)
                    dinc("P1SV", sc.dma_start(out=vtok_h[t * 128:(t + 1) * 128, :], in_=vst[b][:]))
                    W(sc, "P1EV", t + 1)
                    dinc("P1SX", sc.dma_start(
                        out=xT_h[:, t * 128:(t + 1) * 128].rearrange("(kb p) c -> p kb c", p=128),
                        in_=xtc[b][:].rearrange("p (kb c) -> p kb c", kb=NKB)))
            barrier()

    if phases >= 2:
        pre = [es.enter_context(nc.sbuf_tensor(f"preb{i}", [128, 2048], BF16)) for i in range(2)]
        @block.gpsimd
        def _(g):
            W(g, "SETL", cur("SETL"))
            nbuf = [0]
            def stage(dst_ap, src_ap, nelem, view=None):
                b = nbuf[0] % 2
                if nbuf[0] >= 2:
                    W(g, "PRES", 16 * (nbuf[0] - 1))
                dinc("PREL", g.dma_start(out=pre[b][:, 0:nelem], in_=src_ap))
                W(g, "PREL", cur("PREL"))
                buf = pre[b][:, 0:nelem] if view is None else view(pre[b][:, 0:nelem])
                dinc("PRES", g.dma_start(out=dst_ap, in_=buf))
                nbuf[0] += 1
            for W_h, WB_h in [(wqT_h, wqB_h), (wkT_h, wkB_h), (woT_h, woB_h)]:
                for c in range(16):
                    stage(WB_h.rearrange("(kb p) n -> p kb n", p=128)[:, c:c + 1, :],
                          W_h.rearrange("(kb p) n -> p kb n", p=128)[:, c:c + 1, :],
                          D)
            for W_h, WB_h in [(wgT_h, wgB_h), (wuT_h, wuB_h)]:
                for e in range(E):
                    for kb in range(NKB):
                        stage(WB_h[e, :, :, kb * 128:(kb + 1) * 128].rearrange("m p c -> p m c"),
                              W_h[e, kb * 128:(kb + 1) * 128, :],
                              MD,
                              view=lambda bb: bb.rearrange("p (m c) -> p m c", m=NMB))
            for e in range(E):
                for m in range(NMB):
                    stage(wdB_h[e, m * 128:(m + 1) * 128, :],
                          wdT_h[e, m * 128:(m + 1) * 128, :],
                          D)

    if phases >= 2 and p2sub >= 1:
        with ExitStack() as ps:
            psb = lambda name, shape, dty: ps.enter_context(nc.sbuf_tensor(name, shape, dty))
            wq_sl = [psb(f"wqsl{i}", [128, NKB * 256], BF16) for i in range(2)]
            wk_sl = [psb(f"wksl{i}", [128, NKB * 256], BF16) for i in range(2)]
            xtc2 = [psb(f"p2xtc{i}", [128, D], BF16) for i in range(2)]
            qkTp = [psb(f"qkTp{i}", [128, 4 * T], BF16) for i in range(2)]
            vhb = [psb(f"vhb{i}", [128, T], BF16) for i in range(2)]
            ostg = [psb(f"ostg{i}", [128, T], BF16) for i in range(2)]
            pqs = [psb(f"pqs{i}", [128, 512], F32) for i in range(2)]
            qrs = psb("qrs", [128, 512], F32)
            jro = psb("jro", [128, 512], F32)
            sqv = jro
            jro2 = psb("jro2", [128, 512], F32)
            qbb = [psb(f"qbb{i}", [128, 512], BF16) for i in range(2)]
            msd = psb("msd", [128, 16], F32)
            ts1P = [psb(f"ts1{i}", [128, 1024], F32) for i in range(2)]
            ebufP = [psb(f"ebuf{i}", [128, 1024], BF16) for i in range(3)]
            etmpP = psb("etmp", [128, 1024], BF16)
            rdb = psb("rdb", [1, 512], F32)
            rbb = psb("rbb", [128, 512], F32)

            pqk = ps.enter_context(nc.psum_tensor("p2qk", [128, 512], F32))
            pq = pqk[:, 0:256]
            pk = pqk[:, 256:512]
            ptr2_full = ps.enter_context(nc.psum_tensor("p2tr", [128, 1024], BF16))
            ptr2 = ptr2_full[:, 0:512]
            pssP = [ps.enter_context(nc.psum_tensor(f"p2sp{i}", [128, 1024], F32)) for i in range(2)]
            pso = ps.enter_context(nc.psum_tensor("p2o", [128, 512], F32))
            psd = ps.enter_context(nc.psum_tensor("p2d", [128, 512], F32))

            gpi = [0]

            def emit_loads(pr):
                pb = pr % 2
                @block.sync
                def _(sy):
                    if pr == 0:
                        W(sy, "PRES", 16 * 32)
                        W(sy, "P1SX", 16 * NT)
                        W(sy, "P1SV", 16 * NT)
                    if pr >= 2:
                        W(sy, "P2TE", ntt * (pr - 1))
                    dinc("WQL", sy.dma_start(
                        out=wq_sl[pb][:].rearrange("p (kb c) -> p kb c", kb=NKB),
                        in_=wqB_h[:, pr * 256:(pr + 1) * 256].rearrange("(kb p) c -> p kb c", p=128)))
                    dinc("WQL", sy.dma_start(
                        out=wk_sl[pb][:].rearrange("p (kb c) -> p kb c", kb=NKB),
                        in_=wkB_h[:, pr * 256:(pr + 1) * 256].rearrange("(kb p) c -> p kb c", p=128)))

            def emit_xload(pr, t):
                @block.sync
                def _(sy):
                    need = pr * ntt + t
                    if need >= 2:
                        W(sy, "P2MMQ", need - 1)
                    dinc("P2XL", sy.dma_start(
                        out=xtc2[need % 2][:].rearrange("p (kb c) -> p kb c", kb=NKB),
                        in_=xT_h[:, t * 128:(t + 1) * 128].rearrange("(kb p) c -> p kb c", p=128)))

            def emit_prep_tile(pr, t):
                pb = pr % 2
                gi = pr * ntt + t
                xb3 = gi % 2
                @block.tensor
                def _(te):
                    W(te, "P2XL", 16 * (gi + 1))
                    W(te, "WQL", 32 * (pr + 1))
                    if gi >= 1:
                        W(te, "P2CP", gi)
                    for k in range(NKB):
                        te.matmul(pq, xtc2[xb3][:, k * 128:(k + 1) * 128],
                                  wq_sl[pb][:, k * 256:(k + 1) * 256],
                                  start=(k == 0), stop=(k == NKB - 1), skip_group_check=True)
                    for k in range(NKB):
                        instr = te.matmul(pk, xtc2[xb3][:, k * 128:(k + 1) * 128],
                                          wk_sl[pb][:, k * 256:(k + 1) * 256],
                                          start=(k == 0), stop=(k == NKB - 1), skip_group_check=True)
                    inc("P2MMQ", instr)
                @block.vector
                def _(v):
                    W(v, "P2MMQ", gi + 1)
                    if gi >= 2:
                        W(v, "P2RC", gi - 1)
                    inc("P2CP", v.tensor_copy(pqs[gi % 2][:], pqk[:]))
                    W(v, "P2CP", cur("P2CP"))
                    fence(v, v.tensor_mul(sqv[:], pqs[gi % 2][:], pqs[gi % 2][:]))
                    for s2 in range(4):
                        instr = v.tensor_reduce(msd[:, s2:s2 + 1], sqv[:, s2 * 128:(s2 + 1) * 128],
                                                AX.X, ALU.add)
                    fence(v, instr)
                    fence(v, v.tensor_scalar(msd[:, 4:8], msd[:, 0:4], 128.0 * 1e-6, None, ALU.add))
                    inc("P2MS", v.reciprocal(msd[:, 8:12], msd[:, 4:8]))
                @block.scalar
                def _(sc):
                    W(sc, "P2MS", gi + 1)
                    if gi >= 1:
                        W(sc, "P2RC", gi)
                    inc("P2RS", sc.activation(msd[:, 12:16], msd[:, 8:12], AF.Sqrt, scale=128.0))
                @block.vector
                def _(v):
                    W(v, "P2RS", gi + 1)
                    if gi >= 2:
                        W(v, "P2TR", gi - 1)
                    for s2 in range(4):
                        instr = v.tensor_scalar(qrs[:, s2 * 128:(s2 + 1) * 128],
                                                pqs[gi % 2][:, s2 * 128:(s2 + 1) * 128],
                                                msd[:, 12 + s2:13 + s2], None, ALU.mult)
                    fence(v, instr)
                    CW = ropeCW[:, pr * 512:(pr + 1) * 512]
                    SW = ropeSW[:, pr * 512:(pr + 1) * 512]
                    q3 = qrs[:].rearrange("p (n two) -> p n two", two=2)
                    j3 = jro2[:].rearrange("p (n two) -> p n two", two=2)
                    s3 = SW.rearrange("p (n two) -> p n two", two=2)
                    v.tensor_mul(jro[:], qrs[:], CW)
                    v.tensor_mul(j3[:, :, 0:1], q3[:, :, 1:2], s3[:, :, 0:1])
                    fence(v, v.tensor_mul(j3[:, :, 1:2], q3[:, :, 0:1], s3[:, :, 1:2]))
                    inc("P2RC", v.tensor_add(qbb[gi % 2][:], jro[:], jro2[:]))

            def emit_prep_tr(pr, t):
                pb = pr % 2
                gi = pr * ntt + t
                @block.tensor
                def _(te):
                    W(te, "P2RC", gi + 1)
                    if gi >= 1:
                        W(te, "P2TE", gi)
                    for s2 in range(4):
                        instr = te.transpose(ptr2_full[:, s2 * 128:(s2 + 1) * 128],
                                             qbb[gi % 2][:, s2 * 128:(s2 + 1) * 128], idb[:])
                    inc("P2TR", instr)
                @block.vector
                def _(v):
                    W(v, "P2TR", gi + 1)
                    if t == 0 and pr >= 2 and p2sub >= 2:
                        W(v, "TAV", (2 * pr - 2) * 72)
                    instr = v.tensor_copy(
                        qkTp[pb][:].rearrange("p (g tt) -> p g tt", g=4)[:, :, t * 128:(t + 1) * 128],
                        ptr2_full[:, 0:512].rearrange("p (g c) -> p g c", g=4))
                    inc("P2TE", instr)

            def emit_attn_tile(pr, hl, n):
                pb = pr % 2
                h = 2 * pr + hl
                QOF = hl * T
                KOF = 2 * T + hl * T
                ni = 4 * n + 4
                np2 = ni // 2
                hn = h * NN + n
                ap0 = gpi[0]
                tanh_fence = {}

                def emit_mms(pi):
                    ap = ap0 + pi
                    @block.tensor
                    def _(te):
                        if pi == 0:
                            W(te, "P2TE", ntt * (pr + 1))
                            W(te, "VHL", 16 * (h + 1))
                            if hn >= 1:
                                W(te, "OE", hn)
                        if ap >= 2:
                            W(te, "TH", ap - 1)
                        te.matmul(pssP[ap % 2][:, 0:512],
                                  qkTp[pb][:, KOF + 2 * pi * 128:KOF + (2 * pi + 1) * 128],
                                  qkTp[pb][:, QOF + n * 512:QOF + (n + 1) * 512],
                                  start=True, stop=True, skip_group_check=True)
                        instr = te.matmul(pssP[ap % 2][:, 512:1024],
                                          qkTp[pb][:, KOF + (2 * pi + 1) * 128:KOF + (2 * pi + 2) * 128],
                                          qkTp[pb][:, QOF + n * 512:QOF + (n + 1) * 512],
                                          start=True, stop=True, skip_group_check=True)
                        inc("TS", instr)

                def emit_tanh(pi):
                    ap = ap0 + pi
                    @block.scalar
                    def _(sc):
                        W(sc, "TS", ap + 1)
                        instr = sc.activation(ts1P[ap % 2][:], pssP[ap % 2][:], AF.Tanh, scale=TANH_SC)
                        inc("FEN_SC_TANH", instr)
                        tanh_fence[pi] = cur("FEN_SC_TANH")

                def emit_exp(pi):
                    ap = ap0 + pi
                    diag = pi >= 2 * n
                    @block.scalar
                    def _(sc):
                        W(sc, "FEN_SC_TANH", tanh_fence[pi])
                        if ap >= 3:
                            W(sc, "TAV", ap - 2)
                        if diag:
                            W(sc, "VM", cur("VM"))
                        instr = sc.activation(etmpP[:] if diag else ebufP[ap % 3][:],
                                              ts1P[ap % 2][:], AF.Exp, scale=EXP_SC)
                        inc("TH", instr)
                    if diag:
                        @block.vector
                        def _(v):
                            W(v, "TH", ap + 1)
                            for half in range(2):
                                off_idx = 2 * pi - 4 * n + half
                                inc("VM", v.tensor_mul(
                                    ebufP[ap % 3][:, half * 512:(half + 1) * 512],
                                    etmpP[:, half * 512:(half + 1) * 512],
                                    maskt[:, off_idx * 512:(off_idx + 1) * 512]))

                def emit_dav(pi):
                    ap = ap0 + pi
                    diag = pi >= 2 * n
                    @block.tensor
                    def _(te):
                        W(te, "TH", ap + 1)
                        if diag:
                            W(te, "VM", cur("VM"))
                        for half in range(2):
                            i = 2 * pi + half
                            te.matmul(psd[0:1, :], onescb[:],
                                      ebufP[ap % 3][:, half * 512:(half + 1) * 512],
                                      start=(i == 0), stop=(i == ni - 1), skip_group_check=True)
                            instr = te.matmul(pso[:],
                                              vhb[h % 2][:, i * 128:(i + 1) * 128],
                                              ebufP[ap % 3][:, half * 512:(half + 1) * 512],
                                              start=(i == 0), stop=(i == ni - 1), skip_group_check=True)
                        inc("TAV", instr)

                for pi in range(np2):
                    emit_mms(pi)
                    emit_tanh(pi)
                    if pi >= 1:
                        emit_exp(pi - 1)
                        emit_dav(pi - 1)
                emit_exp(np2 - 1)
                emit_dav(np2 - 1)
                gpi[0] += np2

                @block.vector
                def _(v):
                    W(v, "TAV", gpi[0])
                    inc("RD", v.reciprocal(rdb[0:1, :], psd[0:1, :]))
                @block.tensor
                def _(te):
                    W(te, "RD", hn + 1)
                    instr = te.matmul(pssP[gpi[0] % 2][:, 0:512], onesrf[0:1, :], rdb[0:1, :],
                                      start=True, stop=True, skip_group_check=True)
                    inc("TB", instr)
                @block.vector
                def _(v):
                    W(v, "TB", hn + 1)
                    if n == 0 and h >= 2:
                        W(v, "AOS", 16 * (h - 1))
                    fence(v, v.tensor_copy(rbb[:], pssP[gpi[0] % 2][:, 0:512]))
                    instr = v.tensor_mul(ostg[hl][:, n * 512:(n + 1) * 512], pso[:], rbb[:])
                    inc("OE", instr)

            def emit_vload(h):
                @block.sync
                def _(sy):
                    if h >= 2:
                        W(sy, "VHU", h - 1)
                    dinc("VHL", sy.dma_start(
                        out=vhb[h % 2][:].rearrange("p (s c) -> p s c", s=32),
                        in_=vtok_h[:, h * 128:(h + 1) * 128].rearrange("(s p) c -> p s c", p=128)))

            tr_pend = []
            def push_prep(pr, t):
                emit_xload(pr, t)
                emit_prep_tile(pr, t)
                tr_pend.append((pr, t))
                while len(tr_pend) > 2:
                    emit_prep_tr(*tr_pend.pop(0))
            def flush_tr():
                while tr_pend:
                    emit_prep_tr(*tr_pend.pop(0))
            emit_loads(0)
            for t in range(ntt):
                push_prep(0, t)
            flush_tr()
            for pr in range(npr):
                if pr + 1 < npr:
                    emit_loads(pr + 1)
                natt = 0
                for hl in (range(2) if p2sub >= 2 else []):
                    h = 2 * pr + hl
                    emit_vload(h)
                    for n in range(NN):
                        emit_attn_tile(pr, hl, n)
                        if pr + 1 < npr:
                            for t in range(2 * natt, min(2 * natt + 2, ntt)):
                                push_prep(pr + 1, t)
                        natt += 1
                    @block.tensor
                    def _(te):
                        W(te, "OE", (h + 1) * NN)
                        inc("VHU", te.nop())
                    @block.scalar
                    def _(sc):
                        W(sc, "OE", (h + 1) * NN)
                        dinc("AOS", sc.dma_start(out=aoT_h[h * 128:(h + 1) * 128, :], in_=ostg[hl][:]))
                if p2sub < 2 and pr + 1 < npr:
                    for t in range(ntt):
                        push_prep(pr + 1, t)
                flush_tr()
            barrier()

    if phases >= 3:
        with ExitStack() as ps:
            psb = lambda name, shape, dty: ps.enter_context(nc.sbuf_tensor(name, shape, dty))
            wosb = psb("wosb", [128, NKB * D], BF16)
            aot = [psb(f"aot{i}", [128, D], BF16) for i in range(2)]
            xres = [psb(f"xres{i}", [128, D], F32) for i in range(2)]
            sres = psb("sres", [128, D], F32)
            junk3 = sres
            stats = psb("stats", [128, 8], F32)
            dscr = psb("dscr", [128, D], F32)
            h1t = [psb(f"h1t{i}", [128, D], F32) for i in range(2)]
            hib = psb("hib", [128, D], BF16)
            lob = psb("lob", [128, D], BF16)
            hTst0 = psb("hTst0", [128, NKB * 512], BF16)
            hTst = [hTst0, hTst0]
            loTst = psb("loTst", [128, NKB * 512], BF16)
            lgsb = psb("lgsb", [128, 512], F32)
            ltok = psb("ltok", [128, 16], F32)
            gsc = psb("gsc", [128, 64], F32)

            po3 = [ps.enter_context(nc.psum_tensor(f"p3o{i}", [128, 512], F32)) for i in range(2)]
            ptrh = ps.enter_context(nc.psum_tensor("p3trh", [128, 2048], BF16))
            ptrl = ps.enter_context(nc.psum_tensor("p3trl", [128, 2048], BF16))
            plg = ps.enter_context(nc.psum_tensor("p3lg", [128, 512], F32))
            ptt = ps.enter_context(nc.psum_tensor("p3tt", [128, 128], F32))

            @block.vector
            def _(v):
                inc("LGZ", v.memset(lgsb[:], 0.0))
            @block.sync
            def _(sy):
                W(sy, "PRES", 16 * 48)
                W(sy, "AOS", 16 * H)
                for c in range(4):
                    dinc("WOL", sy.dma_start(
                        out=wosb[:].rearrange("p (kb n) -> p kb n", kb=NKB)[:, c * 4:(c + 1) * 4, :],
                        in_=woB_h.rearrange("(kb p) n -> p kb n", p=128)[:, c * 4:(c + 1) * 4, :]))
                for t in range(nt3):
                    b = t % 2
                    if t >= 2:
                        W(sy, "P3MM", 4 * (t - 1))
                        W(sy, "P3SE", 4 * (t - 1))
                    dinc("P3AL", sy.dma_start(
                        out=aot[b][:].rearrange("p (kb c) -> p kb c", kb=NKB),
                        in_=aoT_h[:, t * 128:(t + 1) * 128].rearrange("(kb p) c -> p kb c", p=128)))
                    dinc("P3XL", sy.dma_start(out=xres[b][:], in_=x_h[t * 128:(t + 1) * 128, :]))

            for t in range(nt3):
                b = t % 2
                n = t // 4
                nb = n % 2
                @block.tensor
                def _(te):
                    W(te, "P3AL", 16 * (t + 1))
                    if t == 0:
                        W(te, "WOL", 16 * 4)
                    for d in range(4):
                        if 4 * t + d >= 2:
                            W(te, "P3SE", 4 * t + d - 1)
                        if p3x >= 2:
                            inc("P3MM", te.nop())
                            continue
                        for k in range(NKB):
                            instr = te.matmul(po3[d % 2][:],
                                              aot[b][:, k * 128:(k + 1) * 128],
                                              wosb[:, k * D + d * 512:k * D + d * 512 + 512],
                                              start=(k == 0), stop=(k == NKB - 1))
                        inc("P3MM", instr)
                @block.vector
                def _(v):
                    W(v, "P3XL", 16 * (t + 1))
                    for d in range(4):
                        W(v, "P3MM", 4 * t + d + 1)
                        if p3x >= 2:
                            inc("P3SE", v.tensor_copy(sres[:, d * 512:(d + 1) * 512],
                                                      xres[b][:, d * 512:(d + 1) * 512]))
                        else:
                            inc("P3SE", v.tensor_add(sres[:, d * 512:(d + 1) * 512],
                                                     xres[b][:, d * 512:(d + 1) * 512], po3[d % 2][:]))
                    if p3x >= 1:
                        inc("P3LN", v.tensor_copy(dscr[:], sres[:]))
                    else:
                        fence(v, v.tensor_reduce(stats[:, 0:1], sres[:], AX.X, ALU.add))
                        fence(v, v.tensor_scalar(stats[:, 1:2], stats[:, 0:1], 1.0 / D, None, ALU.mult))
                        inc("P3LN", v.tensor_scalar(dscr[:], sres[:], stats[:, 1:2], None, ALU.subtract))
                @block.scalar
                def _(sc):
                    W(sc, "P3LN", t + 1)
                    if p3x >= 1:
                        inc("P3SQ", sc.nop())
                    else:
                        inc("P3SQ", sc.activation(junk3[:], dscr[:], AF.Square, accum_out=stats[:, 2:3]))
                @block.vector
                def _(v):
                    W(v, "P3SQ", t + 1)
                    if p3x >= 1:
                        inc("P3RC", v.engine_nop())
                    else:
                        fence(v, v.tensor_scalar(stats[:, 3:4], stats[:, 2:3], float(D) * 1e-5, None, ALU.add))
                        inc("P3RC", v.reciprocal(stats[:, 4:5], stats[:, 3:4]))
                @block.scalar
                def _(sc):
                    W(sc, "P3RC", t + 1)
                    if p3x >= 1:
                        inc("P3SR", sc.nop())
                    else:
                        inc("P3SR", sc.activation(stats[:, 5:6], stats[:, 4:5], AF.Sqrt, scale=float(D)))
                @block.vector
                def _(v):
                    W(v, "P3SR", t + 1)
                    if t >= 2:
                        W(v, "P3H1S", 16 * (t - 1))
                    if p3x >= 1:
                        inc("P3HD", v.tensor_copy(h1t[b][:], dscr[:]))
                    else:
                        inc("P3HD", v.tensor_scalar(h1t[b][:], dscr[:], stats[:, 5:6], None, ALU.mult))
                    W(v, "P3HD", cur("P3HD"))
                    if t >= 1 and p3sub >= 2:
                        W(v, "P3TR", t)
                    fence(v, v.tensor_copy(hib[:], h1t[b][:]))
                    inc("P3HL", v.tensor_sub(lob[:], h1t[b][:], hib[:]))
                @block.scalar
                def _(sc):
                    W(sc, "P3HD", t + 1)
                    dinc("P3H1S", sc.dma_start(out=h1_h[t * 128:(t + 1) * 128, :], in_=h1t[b][:]))
                if p3sub < 2:
                    continue
                @block.tensor
                def _(te):
                    W(te, "P3HL", t + 1)
                    if t >= 1:
                        W(te, "P3TE", 2 * t)
                    for k in range(NKB):
                        te.transpose(ptrh[:, k * 128:(k + 1) * 128], hib[:, k * 128:(k + 1) * 128], idb[:])
                        instr = te.transpose(ptrl[:, k * 128:(k + 1) * 128], lob[:, k * 128:(k + 1) * 128], idb[:])
                    inc("P3TR", instr)
                @block.vector
                def _(v):
                    W(v, "P3TR", t + 1)
                    if t % 4 == 0 and n >= 1 and p3sub >= 3:
                        W(v, "P3GD", n)
                        W(v, "P3HTS", 16 * n)
                        W(v, "P3GM", n)
                    v.tensor_copy(
                        hTst[nb][:].rearrange("p (kb c) -> p kb c", kb=NKB)[:, :, (t % 4) * 128:(t % 4 + 1) * 128],
                        ptrh[:].rearrange("p (kb c) -> p kb c", kb=NKB))
                    instr = v.tensor_copy(
                        loTst[:].rearrange("p (kb c) -> p kb c", kb=NKB)[:, :, (t % 4) * 128:(t % 4 + 1) * 128],
                        ptrl[:].rearrange("p (kb c) -> p kb c", kb=NKB))
                    inc("P3TE", instr, 2)
                if t % 4 == 3 and p3sub >= 3:
                    @block.tensor
                    def _(te):
                        W(te, "P3TE", 2 * (t + 1))
                        if n == 0:
                            W(te, "SETV", cur("SETV"))
                        if n >= 1:
                            W(te, "P3GE", n)
                        nmm = 0
                        for k in range(NKB):
                            for (lw, rh) in [(ghi, hTst[nb]), (glo, hTst[nb]), (ghi, loTst)]:
                                nmm += 1
                                instr = te.matmul(plg[0:16, :], lw[:, k * 16:(k + 1) * 16],
                                                  rh[:, k * 512:(k + 1) * 512],
                                                  start=(nmm == 1), stop=(nmm == 3 * NKB))
                        inc("P3GM", instr)
                    @block.vector
                    def _(v):
                        W(v, "P3GM", n + 1)
                        W(v, "LGZ", 1)
                        if n >= 1:
                            W(v, "P3TT", 4 * n)
                        inc("P3GE", v.tensor_copy(lgsb[0:16, :], plg[0:16, :]))
                    for c in range(4):
                        @block.tensor
                        def _(te):
                            W(te, "P3GE", n + 1)
                            if 4 * n + c >= 1:
                                W(te, "P3TTE", 4 * n + c)
                            inc("P3TT", te.transpose(ptt[:], lgsb[:, c * 128:(c + 1) * 128], idf[:]))
                        @block.vector
                        def _(v):
                            W(v, "P3TT", 4 * n + c + 1)
                            tt = n * 4 + c
                            if tt >= 1:
                                W(v, "P3EX", tt)
                            inc("P3TTE", v.tensor_copy(ltok[:], ptt[:, 0:16]))
                            W(v, "P3TTE", cur("P3TTE"))
                            fence(v, v.max(gsc[:, 0:8], ltok[:]))
                            inc("P3M8", v.tensor_scalar(gsc[:, 8:24], ltok[:], gsc[:, 5:6], None, ALU.is_ge))
                        @block.scalar
                        def _(sc):
                            W(sc, "P3M8", 4 * n + c + 1)
                            inc("P3EX", sc.activation(gsc[:, 24:40], ltok[:], AF.Exp))
                        @block.vector
                        def _(v):
                            W(v, "P3EX", 4 * n + c + 1)
                            tt = n * 4 + c
                            fence(v, v.tensor_reduce(gsc[:, 40:41], gsc[:, 24:40], AX.X, ALU.add))
                            fence(v, v.reciprocal(gsc[:, 41:42], gsc[:, 40:41]))
                            fence(v, v.tensor_scalar(gsc[:, 42:58], gsc[:, 24:40], gsc[:, 41:42], None, ALU.mult))
                            instr = v.tensor_mul(cwsb[:, tt * 16:(tt + 1) * 16], gsc[:, 42:58], gsc[:, 8:24])
                            inc("P3CW", instr)
                            if c == 3:
                                inc("P3GD", v.engine_nop())
                    @block.scalar
                    def _(sc):
                        W(sc, "P3TE", 2 * (t + 1))
                        dinc("P3HTS", sc.dma_start(
                            out=hT_h[:, n * 512:(n + 1) * 512].rearrange("(kb p) c -> p kb c", p=128),
                            in_=hTst[nb][:].rearrange("p (kb c) -> p kb c", kb=NKB)))
            barrier()

    if phases >= 5:
        with ExitStack() as ps:
            psb = lambda name, shape, dty: ps.enter_context(nc.sbuf_tensor(name, shape, dty))
            wdsb = psb("wdsb", [128, NMB * D], BF16)
            wgsl = [psb(f"wgsl{i}", [128, D], BF16) for i in range(2)]
            wusl = [psb(f"wusl{i}", [128, D], BF16) for i in range(2)]
            hTn = [psb(f"hTn{i}", [128, NKB * 512], BF16) for i in range(2)]
            aTn = psb("aTn", [128, NMB * 512], BF16)
            sg5 = psb("sg5", [128, 512], F32)
            ysb = [psb(f"ysb{i}", [128, D], F32) for i in range(2)]

            pg5 = [ps.enter_context(nc.psum_tensor(f"p5g{i}", [128, 512], F32)) for i in range(2)]
            pu5 = [ps.enter_context(nc.psum_tensor(f"p5u{i}", [128, 512], F32)) for i in range(2)]
            py5 = [ps.enter_context(nc.psum_tensor(f"p5y{i}", [128, 512], F32)) for i in range(2)]

            nws = [0]
            nyt = [0]
            for e in range(E):
                @block.sync
                def _(sy):
                    if e == 0:
                        W(sy, "PRES", cur("PRES"))
                        W(sy, "P3HTS", 16 * NN)
                        W(sy, "P3GD", NN)
                        W(sy, "P3H1S", 16 * NT)
                    if e >= 1:
                        W(sy, "P5YM", e * NN * 16)
                    dinc("P5WD", sy.dma_start(
                        out=wdsb[:].rearrange("p (mb c) -> p mb c", mb=NMB),
                        in_=wdB_h[e].rearrange("(mb p) c -> p mb c", p=128)))
                for n in range(NN):
                    gn = e * NN + n
                    hb = gn % 2
                    @block.sync
                    def _(sy):
                        if gn >= 2:
                            W(sy, "P5GU", (gn - 1) * NMB)
                        dinc("P5HL", sy.dma_start(
                            out=hTn[hb][:].rearrange("p (kb c) -> p kb c", kb=NKB),
                            in_=hT_h[:, n * 512:(n + 1) * 512].rearrange("(kb p) c -> p kb c", p=128)))
                        for m in range(NMB):
                            wb = nws[0] % 2
                            if nws[0] >= 2:
                                W(sy, "P5GU", nws[0] - 1)
                            dinc("P5WL", sy.dma_start(out=wgsl[wb][:], in_=wgB_h[e, m]))
                            dinc("P5WL", sy.dma_start(out=wusl[wb][:], in_=wuB_h[e, m]))
                            nws[0] += 1
                    for m in range(NMB):
                        gm = gn * NMB + m
                        wb = gm % 2
                        gb = gm % 2
                        @block.tensor
                        def _(te):
                            W(te, "P5HL", 16 * (gn + 1))
                            W(te, "P5WL", 32 * (gm + 1))
                            if gm >= 2:
                                W(te, "P5AE", gm - 1)
                            for k in range(NKB):
                                te.matmul(pg5[gb][:], wgsl[wb][:, k * 128:(k + 1) * 128],
                                          hTn[hb][:, k * 512:(k + 1) * 512],
                                          start=(k == 0), stop=(k == NKB - 1),
                                          skip_group_check=True)
                                instr = te.matmul(pu5[gb][:], wusl[wb][:, k * 128:(k + 1) * 128],
                                                  hTn[hb][:, k * 512:(k + 1) * 512],
                                                  start=(k == 0), stop=(k == NKB - 1),
                                                  skip_group_check=True)
                            inc("P5GU", instr)
                        @block.scalar
                        def _(sc):
                            W(sc, "P5GU", gm + 1)
                            if gm >= 1:
                                W(sc, "P5AE", gm)
                            inc("P5SI", sc.activation(sg5[:], pg5[gb][:], AF.Silu))
                        @block.vector
                        def _(v):
                            W(v, "P5SI", gm + 1)
                            if gn >= 1 and m == 0:
                                W(v, "P5YM", gn * 16)
                            inc("P5AE", v.tensor_mul(aTn[:, m * 512:(m + 1) * 512], sg5[:], pu5[gb][:]))
                    for tj in range(4):
                        yb = (4 * gn + tj) % 2
                        for dd in range(4):
                            yp = nyt[0] % 2
                            @block.tensor
                            def _(te):
                                W(te, "P5AE", (gn + 1) * NMB)
                                if n == 0 and tj == 0 and dd == 0:
                                    W(te, "P5WD", 16 * (e + 1))
                                if nyt[0] >= 2:
                                    W(te, "P5YE", nyt[0] - 1)
                                for m in range(NMB):
                                    instr = te.matmul(py5[yp][:],
                                                      aTn[:, m * 512 + tj * 128:m * 512 + (tj + 1) * 128],
                                                      wdsb[:, m * D + dd * 512:m * D + dd * 512 + 512],
                                                      start=(m == 0), stop=(m == NMB - 1))
                                inc("P5YM", instr)
                            @block.vector
                            def _(v):
                                W(v, "P5YM", nyt[0] + 1)
                                if nyt[0] >= 8:
                                    W(v, "P5YS", 16 * (nyt[0] // 4 - 1))
                                tt = n * 4 + tj
                                inc("P5YE", v.tensor_scalar(ysb[yb][:, dd * 512:(dd + 1) * 512], py5[yp][:],
                                                            cwsb[:, tt * 16 + e:tt * 16 + e + 1], None, ALU.mult))
                            nyt[0] += 1
                        @block.gpsimd
                        def _(g):
                            W(g, "P5YE", nyt[0])
                            tt = n * 4 + tj
                            if e == 0:
                                instr = g.dma_start(out=yac_h[tt * 128:(tt + 1) * 128, :], in_=ysb[yb][:])
                            else:
                                instr = g.dma_start(out=yac_h[tt * 128:(tt + 1) * 128, :], in_=ysb[yb][:],
                                                    accum_op=ALU.add)
                            dinc("P5YS", instr)
            barrier()

    if phases >= 6:
        with ExitStack() as ps:
            psb = lambda name, shape, dty: ps.enter_context(nc.sbuf_tensor(name, shape, dty))
            yld = [psb(f"yld{i}", [128, D], F32) for i in range(2)]
            hld = [psb(f"hld{i}", [128, D], F32) for i in range(2)]
            s6 = psb("s6", [128, D], F32)
            junk6 = psb("junk6", [128, D], F32)
            st6 = psb("st6", [128, 8], F32)
            dscr6 = psb("dscr6", [128, D], F32)
            ot6 = [psb(f"ot6{i}", [128, D], F32) for i in range(2)]
            @block.sync
            def _(sy):
                W(sy, "P5YS", 16 * NT * E)
                W(sy, "P3H1S", 16 * NT)
                for t in range(NT):
                    b = t % 2
                    if t >= 2:
                        W(sy, "P6A", t - 1)
                    dinc("P6L", sy.dma_start(out=yld[b][:], in_=yac_h[t * 128:(t + 1) * 128, :]))
                    dinc("P6L", sy.dma_start(out=hld[b][:], in_=h1_h[t * 128:(t + 1) * 128, :]))
            for t in range(NT):
                b = t % 2
                @block.vector
                def _(v):
                    W(v, "P6L", 32 * (t + 1))
                    inc("P6A", v.tensor_add(s6[:], yld[b][:], hld[b][:]))
                    W(v, "P6A", cur("P6A"))
                    fence(v, v.tensor_reduce(st6[:, 0:1], s6[:], AX.X, ALU.add))
                    fence(v, v.tensor_scalar(st6[:, 1:2], st6[:, 0:1], 1.0 / D, None, ALU.mult))
                    inc("P6D", v.tensor_scalar(dscr6[:], s6[:], st6[:, 1:2], None, ALU.subtract))
                @block.scalar
                def _(sc):
                    W(sc, "P6D", t + 1)
                    inc("P6SQ", sc.activation(junk6[:], dscr6[:], AF.Square, accum_out=st6[:, 2:3]))
                @block.vector
                def _(v):
                    W(v, "P6SQ", t + 1)
                    fence(v, v.tensor_scalar(st6[:, 3:4], st6[:, 2:3], float(D) * 1e-5, None, ALU.add))
                    inc("P6RC", v.reciprocal(st6[:, 4:5], st6[:, 3:4]))
                @block.scalar
                def _(sc):
                    W(sc, "P6RC", t + 1)
                    inc("P6SR", sc.activation(st6[:, 5:6], st6[:, 4:5], AF.Sqrt, scale=float(D)))
                @block.vector
                def _(v):
                    W(v, "P6SR", t + 1)
                    if t >= 2:
                        W(v, "P6S", 16 * (t - 1))
                    inc("P6N", v.tensor_scalar(ot6[b][:], dscr6[:], st6[:, 5:6], None, ALU.mult))
                @block.scalar
                def _(sc):
                    W(sc, "P6N", t + 1)
                    dinc("P6S", sc.dma_start(out=out_h[t * 128:(t + 1) * 128, :], in_=ot6[b][:]))
            @block.gpsimd
            def _(g):
                W(g, "P6S", 16 * NT)

    if dump:
        barrier()
        @block.sync
        def _(sy):
            pairs = {"d_xT": xT_h, "d_vtok": vtok_h, "d_aoT": aoT_h, "d_h1": h1_h,
                     "d_hT": hT_h, "d_yac": yac_h}
            for name, ap in dump_h.items():
                if name in pairs:
                    dinc("DMP", sy.dma_start(out=ap, in_=pairs[name]))
                elif name == "d_wgB":
                    dinc("DMP", sy.dma_start(out=ap, in_=wgB_h[0]))
        if "d_cw" in dump_h:
            @block.scalar
            def _(sc):
                dinc("DMP", sc.dma_start(out=dump_h["d_cw"], in_=cwsb[:]))
        @block.gpsimd
        def _(g):
            W(g, "DMP", cur("DMP"))

    for nm, val in waits_log:
        if val > cur(nm):
            raise RuntimeError(f"wait on {nm} for {val} but only {cur(nm)} increments emitted")
    es.close()
    return nc


def _host_prep(inp):
    f32 = np.float32
    x = np.ascontiguousarray(np.asarray(inp["L0_attn_x"], f32))
    wqT = np.ascontiguousarray(np.asarray(inp["L0_attn_wq"], f32).T)
    wkT = np.ascontiguousarray(np.asarray(inp["L0_attn_wk"], f32).T)
    wvT = np.ascontiguousarray(np.asarray(inp["L0_attn_wv"], f32).T)
    woT = np.ascontiguousarray(np.asarray(inp["L0_attn_wo"], f32).T)
    gwT = np.ascontiguousarray(np.asarray(inp["L0_ffn_gate_w"], f32).T)
    wgT = np.ascontiguousarray(np.transpose(np.asarray(inp["L0_ffn_w_gate"], f32), (0, 2, 1)))
    wuT = np.ascontiguousarray(np.transpose(np.asarray(inp["L0_ffn_w_up"], f32), (0, 2, 1)))
    wdT = np.ascontiguousarray(np.transpose(np.asarray(inp["L0_ffn_w_down"], f32), (0, 2, 1)))
    cos16 = np.asarray(inp["L0_attn_rope_cos"], f32)[:H]
    sin16 = np.asarray(inp["L0_attn_rope_sin"], f32)[:H]
    qnw = np.asarray(inp["L0_attn_q_norm_w"], f32)
    knw = np.asarray(inp["L0_attn_k_norm_w"], f32)

    cosE = np.repeat(cos16, 2, axis=1).reshape(-1)
    sinE = np.repeat(sin16, 2, axis=1).reshape(-1)
    sign = np.tile(np.array([-1.0, 1.0], f32), D // 2)
    swap = lambda w: np.ascontiguousarray(w.reshape(-1, 2)[:, ::-1]).reshape(-1)
    rows = np.stack([
        cosE, sinE, sign,
        np.tile(qnw, H), np.tile(swap(qnw), H),
        np.tile(knw, H), np.tile(swap(knw), H),
        np.ones(D, f32),
    ]).astype(f32)

    import ml_dtypes
    masks = np.zeros((128, 2048), np.float32)
    rr = np.arange(128)[:, None]
    cc = np.arange(512)[None, :]
    for oi, off in enumerate((0, 128, 256, 384)):
        masks[:, oi * 512:(oi + 1) * 512] = (cc >= rr + off)
    masks = masks.astype(ml_dtypes.bfloat16)
    identb = np.eye(128, dtype=np.float32).astype(ml_dtypes.bfloat16)
    identf = np.eye(128, dtype=np.float32)
    onescb = np.ones((128, 1), np.float32).astype(ml_dtypes.bfloat16)
    consts = np.tile(np.array([TANH_SC, EXP_SC, 128.0, float(D)], f32)[None, :], (128, 1))

    shared = dict(wqT=wqT, wkT=wkT, wvT=wvT, woT=woT, gwT=gwT,
                  wgT=wgT, wuT=wuT, wdT=wdT, rows=rows, masks=masks,
                  identb=identb, identf=identf, onescb=onescb, consts=consts)
    return x, shared


def kernel(**inp):
    x, shared = _host_prep(inp)
    nc = build(phases=6)
    in_maps = [dict(shared, x=np.ascontiguousarray(x[b])) for b in range(8)]
    res = run_bass_kernel_spmd(nc, in_maps, core_ids=list(range(8)))
    return np.stack([np.asarray(res.results[b]["out"]) for b in range(8)]).astype(np.float32)


# version 07
# speedup vs baseline: 1.1086x; 1.0075x over previous
import numpy as np
from contextlib import ExitStack
import concourse.bass as bass
import concourse.mybir as mybir
from concourse.bass_utils import run_bass_kernel_spmd

F32 = mybir.dt.float32
BF16 = mybir.dt.bfloat16
AF = mybir.ActivationFunctionType
ALU = mybir.AluOpType
AX = mybir.AxisListType

T, D, H, DH = 4096, 2048, 16, 128
E, MD = 16, 1408
NKB = D // 128
NMB = MD // 128
NT = T // 128
NN = T // 512
TANH_SC = float((1.0 / np.sqrt(128.0)) / 50.0)
EXP_SC = 50.0


def build(phases=6, dump=(), p2sub=2, npr=8, ntt=NT, p3sub=3, nt3=NT, p3x=0):
    nc = bass.Bass("TRN2", target_bir_lowering=False)
    dtn = nc.dram_tensor
    x_h = dtn("x", [T, D], F32, kind="ExternalInput").ap()
    wqT_h = dtn("wqT", [D, D], F32, kind="ExternalInput").ap()
    wkT_h = dtn("wkT", [D, D], F32, kind="ExternalInput").ap()
    wvT_h = dtn("wvT", [D, D], F32, kind="ExternalInput").ap()
    woT_h = dtn("woT", [D, D], F32, kind="ExternalInput").ap()
    gwT_h = dtn("gwT", [D, E], F32, kind="ExternalInput").ap()
    wgT_h = dtn("wgT", [E, D, MD], F32, kind="ExternalInput").ap()
    wuT_h = dtn("wuT", [E, D, MD], F32, kind="ExternalInput").ap()
    wdT_h = dtn("wdT", [E, MD, D], F32, kind="ExternalInput").ap()
    rows_h = dtn("rows", [8, D], F32, kind="ExternalInput").ap()
    masks_h = dtn("masks", [128, 2048], BF16, kind="ExternalInput").ap()
    identb_h = dtn("identb", [128, 128], BF16, kind="ExternalInput").ap()
    identf_h = dtn("identf", [128, 128], F32, kind="ExternalInput").ap()
    onescb_h = dtn("onescb", [128, 1], BF16, kind="ExternalInput").ap()
    consts_h = dtn("consts", [128, 4], F32, kind="ExternalInput").ap()

    xT_h = dtn("xTs", [D, T], BF16, kind="Internal").ap()
    vtok_h = dtn("vtoks", [T, D], BF16, kind="Internal").ap()
    aoT_h = dtn("aoTs", [D, T], BF16, kind="Internal").ap()
    h1_h = dtn("h1s", [T, D], F32, kind="Internal").ap()
    hT_h = dtn("hTs", [D, T], BF16, kind="Internal").ap()
    wqB_h = dtn("wqB", [D, D], BF16, kind="Internal").ap()
    wkB_h = dtn("wkB", [D, D], BF16, kind="Internal").ap()
    woB_h = dtn("woB", [D, D], BF16, kind="Internal").ap()
    wgB_h = dtn("wgB", [E, NMB, 128, D], BF16, kind="Internal").ap()
    wuB_h = dtn("wuB", [E, NMB, 128, D], BF16, kind="Internal").ap()
    wdB_h = dtn("wdB", [E, MD, D], BF16, kind="Internal").ap()
    yac_h = dtn("yac", [T, D], F32, kind="Internal").ap()
    rsc_h = dtn("ropescratch", [4, D], F32, kind="Internal").ap()
    out_h = dtn("out", [T, D], F32, kind="ExternalOutput").ap()

    dump_h = {}
    for name, shp, dty in [
        ("d_xT", [D, T], BF16), ("d_vtok", [T, D], BF16),
        ("d_aoT", [D, T], BF16), ("d_h1", [T, D], F32),
        ("d_hT", [D, T], BF16), ("d_cw", [128, NT * E], F32),
        ("d_yac", [T, D], F32), ("d_wgB", [NMB, 128, D], BF16),
        ("d_wvsb", [128, NKB * D], BF16),
        ("d_qT", [128, 2 * T], BF16), ("d_kT", [128, 2 * T], BF16),
        ("d_pqs", [128, 512], F32), ("d_qro", [128, 256], BF16),
        ("d_wqB", [D, D], BF16), ("d_xtc2", [128, D], BF16), ("d_wqsl", [128, NKB * 256], BF16),
        ("d_msd", [128, 16], F32),
    ]:
        if name in dump:
            dump_h[name] = dtn(name, shp, dty, kind="ExternalOutput").ap()

    es = ExitStack()
    sb = lambda name, shape, dty: es.enter_context(nc.sbuf_tensor(name, shape, dty))

    cst = sb("cst", [128, 4], F32)
    idb = sb("idb", [128, 128], BF16)
    idf = sb("idf", [128, 128], F32)
    maskt = sb("maskt", [128, 2048], BF16)
    onescb = sb("s_onescb", [128, 1], BF16)
    onesrf = sb("onesrf", [1, 128], F32)
    ropeCW = sb("ropeCW", [128, 2 * D], BF16)
    ropeSW = sb("ropeSW", [128, 2 * D], BF16)
    ghi = sb("ghi", [128, NKB * E], BF16)
    glo = sb("glo", [128, NKB * E], BF16)
    gwf = sb("gwf", [128, NKB * E], F32)
    glof = sb("glof", [128, NKB * E], F32)
    cwsb = sb("cwsb", [128, NT * E], F32)
    setup_ps = ExitStack()
    rowsb = setup_ps.enter_context(nc.sbuf_tensor("rowsb", [1, 8 * D], F32))
    rwork = setup_ps.enter_context(nc.sbuf_tensor("rwork", [1, 5 * D], F32))

    sems = {}
    dma_sems = set()
    def S(name):
        if name not in sems:
            sems[name] = [es.enter_context(nc.semaphore(name)), 0]
        return sems[name]
    def inc(name, instr, n=1):
        s = S(name)
        instr.then_inc(s[0], n)
        s[1] += n
        if n >= 16:
            dma_sems.add(name)
        return instr
    def cur(name):
        return S(name)[1]
    def dinc(name, instr):
        return inc(name, instr, 16)
    waits_log = []
    def W(eng, name, val):
        waits_log.append((name, val))
        eng.wait_ge(S(name)[0], val)
    def fence(eng, instr, name=None):
        if name is None:
            name = "FEN_" + type(eng).__name__
        inc(name, instr)
        W(eng, name, cur(name))
        return instr

    block = es.enter_context(nc.Block())
    nc.const_aps.register(F32, TANH_SC, cst[:, 0:1])
    nc.const_aps.register(F32, EXP_SC, cst[:, 1:2])
    nc.const_aps.register(F32, 128.0, cst[:, 2:3])
    nc.const_aps.register(F32, float(D), cst[:, 3:4])

    bar_n = [0]
    ENGINES = ["sync", "scalar", "vector", "tensor", "gpsimd"]
    def barrier():
        @block.sync
        def _(sy):
            for nm in sorted(dma_sems):
                W(sy, nm, cur(nm))
        bar_n[0] += 1
        tgt = 5 * 16 * bar_n[0]
        for en in ENGINES:
            @getattr(block, en)
            def _(eng):
                inc("BAR", eng.nop(), 16)
        for en in ENGINES:
            @getattr(block, en)
            def _(eng):
                W(eng, "BAR", tgt)

    @block.sync
    def _(sy):
        for dst, src in [
            (cst[:], consts_h), (idb[:], identb_h), (idf[:], identf_h),
            (maskt[:], masks_h), (onescb[:], onescb_h),
            (onesrf[0:1, :], rows_h[7:8, 0:128]),
            (rowsb[0:1, :], rows_h.rearrange("r d -> (r d)")[None, :]),
            (gwf[:].rearrange("p (kb e) -> p kb e", kb=NKB),
             gwT_h.rearrange("(kb p) e -> p kb e", p=128)),
        ]:
            dinc("SETL", sy.dma_start(out=dst, in_=src))
    @block.vector
    def _(v):
        W(v, "SETL", cur("SETL"))
        CE, SE, SG, WQ, WQS, WK, WKS = [rowsb[0:1, i * D:(i + 1) * D] for i in range(7)]
        TMP = rwork[0:1, 4 * D:5 * D]
        v.tensor_mul(rwork[0:1, 0:D], CE, WQ)
        fence(v, v.tensor_mul(TMP, SE, SG))
        v.tensor_mul(rwork[0:1, D:2 * D], TMP, WQS)
        v.tensor_mul(rwork[0:1, 2 * D:3 * D], CE, WK)
        v.tensor_mul(rwork[0:1, 3 * D:4 * D], TMP, WKS)
        fence(v, v.tensor_copy(ghi[:], gwf[:]))
        fence(v, v.tensor_sub(glof[:], gwf[:], ghi[:]))
        inc("SETV", v.tensor_copy(glo[:], glof[:]))
    @block.gpsimd
    def _(g):
        W(g, "SETV", cur("SETV"))
        dinc("SETG0", g.dma_start(out=rsc_h.rearrange("r d -> (r d)")[None, :], in_=rwork[0:1, 0:4 * D]))
        W(g, "SETG0", 16)
        for pr2 in range(8):
            for row, dst, off in ((0, ropeCW, 0), (2, ropeCW, 256), (1, ropeSW, 0), (3, ropeSW, 256)):
                dinc("SETG", g.dma_start(
                    out=dst[:, pr2 * 512 + off:pr2 * 512 + off + 256],
                    in_=rsc_h[row:row + 1, pr2 * 256:(pr2 + 1) * 256].to_broadcast([128, 256])))
    setup_ps.close()

    if phases >= 1:
        with ExitStack() as ps:
            psb = lambda name, shape, dty: ps.enter_context(nc.sbuf_tensor(name, shape, dty))
            wvsb = psb("wvsb", [128, NKB * D], BF16)
            xf = [psb(f"p1xf{i}", [128, D], F32) for i in range(2)]
            xb = [psb(f"p1xb{i}", [128, D], BF16) for i in range(2)]
            xtc = [psb(f"p1xtc{i}", [128, D], BF16) for i in range(2)]
            vst = [psb(f"p1vst{i}", [128, D], BF16) for i in range(2)]
            ptr = ps.enter_context(nc.psum_tensor("p1ptr", [128, 2048], BF16))
            pv = ps.enter_context(nc.psum_tensor("p1pv", [128, 2048], F32))

            @block.gpsimd
            def _(g):
                W(g, "SETL", cur("SETL"))
                for c in range(4):
                    dinc("WVL", g.dma_start(
                        out=wvsb[:].rearrange("p (kb n) -> p kb n", kb=NKB)[:, c * 4:(c + 1) * 4, :],
                        in_=wvT_h.rearrange("(kb p) n -> p kb n", p=128)[:, c * 4:(c + 1) * 4, :]))
                if "d_wvsb" in dump_h:
                    W(g, "P1SV", 16 * NT)
                    dinc("DMPW", g.dma_start(out=dump_h["d_wvsb"], in_=wvsb[:]))
                    W(g, "DMPW", 16)
            @block.sync
            def _(sy):
                W(sy, "SETG", 16 * 32)
                for t in range(NT):
                    if t >= 2:
                        W(sy, "P1XF", t - 1)
                    dinc("P1LX", sy.dma_start(out=xf[t % 2][:], in_=x_h[t * 128:(t + 1) * 128, :]))
            @block.vector
            def _(v):
                for t in range(NT):
                    b = t % 2
                    W(v, "P1LX", 16 * (t + 1))
                    if t >= 2:
                        W(v, "P1TR", t - 1)
                    inc("P1XF", v.tensor_copy(xb[b][:], xf[b][:]))
                    W(v, "P1TR", t + 1)
                    if t >= 2:
                        W(v, "P1SX", 16 * (t - 1))
                    inc("P1EV", v.tensor_copy(xtc[b][:], ptr[:]))
            @block.tensor
            def _(te):
                W(te, "WVL", 16 * 4)
                for t in range(NT):
                    b = t % 2
                    W(te, "P1XF", t + 1)
                    if t >= 1:
                        W(te, "P1EV", t)
                    for k in range(NKB):
                        instr = te.transpose(ptr[:, k * 128:(k + 1) * 128], xb[b][:, k * 128:(k + 1) * 128], idb[:])
                    inc("P1TR", instr)
                    W(te, "P1EV", t + 1)
                    if t >= 1:
                        W(te, "P1VE", t)
                    for d in range(4):
                        for k in range(NKB):
                            instr = te.matmul(pv[:, d * 512:(d + 1) * 512],
                                              xtc[b][:, k * 128:(k + 1) * 128],
                                              wvsb[:, k * D + d * 512:k * D + d * 512 + 512],
                                              start=(k == 0), stop=(k == NKB - 1),
                                              skip_group_check=True)
                    inc("P1MM", instr)
            @block.scalar
            def _(sc):
                for t in range(NT):
                    b = t % 2
                    W(sc, "P1MM", t + 1)
                    if t >= 2:
                        W(sc, "P1SV", 16 * (t - 1))
                    inc("P1VE", sc.copy(vst[b][:], pv[:]))
                    W(sc, "P1VE", t + 1)
                    dinc("P1SV", sc.dma_start(out=vtok_h[t * 128:(t + 1) * 128, :], in_=vst[b][:]))
                    W(sc, "P1EV", t + 1)
                    dinc("P1SX", sc.dma_start(
                        out=xT_h[:, t * 128:(t + 1) * 128].rearrange("(kb p) c -> p kb c", p=128),
                        in_=xtc[b][:].rearrange("p (kb c) -> p kb c", kb=NKB)))
            barrier()

    if phases >= 2:
        pre = [es.enter_context(nc.sbuf_tensor(f"preb{i}", [128, 2048], BF16)) for i in range(2)]
        @block.gpsimd
        def _(g):
            W(g, "SETL", cur("SETL"))
            nbuf = [0]
            def stage(dst_ap, src_ap, nelem, view=None):
                b = nbuf[0] % 2
                if nbuf[0] >= 2:
                    W(g, "PRES", 16 * (nbuf[0] - 1))
                dinc("PREL", g.dma_start(out=pre[b][:, 0:nelem], in_=src_ap))
                W(g, "PREL", cur("PREL"))
                buf = pre[b][:, 0:nelem] if view is None else view(pre[b][:, 0:nelem])
                dinc("PRES", g.dma_start(out=dst_ap, in_=buf))
                nbuf[0] += 1
            for W_h, WB_h in [(wqT_h, wqB_h), (wkT_h, wkB_h), (woT_h, woB_h)]:
                for c in range(16):
                    stage(WB_h.rearrange("(kb p) n -> p kb n", p=128)[:, c:c + 1, :],
                          W_h.rearrange("(kb p) n -> p kb n", p=128)[:, c:c + 1, :],
                          D)
            for W_h, WB_h in [(wgT_h, wgB_h), (wuT_h, wuB_h)]:
                for e in range(E):
                    for kb in range(NKB):
                        stage(WB_h[e, :, :, kb * 128:(kb + 1) * 128].rearrange("m p c -> p m c"),
                              W_h[e, kb * 128:(kb + 1) * 128, :],
                              MD,
                              view=lambda bb: bb.rearrange("p (m c) -> p m c", m=NMB))
            for e in range(E):
                for m in range(NMB):
                    stage(wdB_h[e, m * 128:(m + 1) * 128, :],
                          wdT_h[e, m * 128:(m + 1) * 128, :],
                          D)

    if phases >= 2 and p2sub >= 1:
        with ExitStack() as ps:
            psb = lambda name, shape, dty: ps.enter_context(nc.sbuf_tensor(name, shape, dty))
            wq_sl = [psb(f"wqsl{i}", [128, NKB * 256], BF16) for i in range(2)]
            wk_sl = [psb(f"wksl{i}", [128, NKB * 256], BF16) for i in range(2)]
            xtc2 = [psb(f"p2xtc{i}", [128, D], BF16) for i in range(2)]
            qkTp = [psb(f"qkTp{i}", [128, 4 * T], BF16) for i in range(2)]
            vhb = [psb(f"vhb{i}", [128, T], BF16) for i in range(2)]
            ostg = [psb(f"ostg{i}", [128, T], BF16) for i in range(2)]
            pqs = [psb(f"pqs{i}", [128, 512], F32) for i in range(2)]
            qrs = psb("qrs", [128, 512], F32)
            jro = psb("jro", [128, 512], F32)
            sqv = jro
            jro2 = psb("jro2", [128, 512], F32)
            qbb = [psb(f"qbb{i}", [128, 512], BF16) for i in range(2)]
            msd = psb("msd", [128, 16], F32)
            ts1P = [psb(f"ts1{i}", [128, 1024], F32) for i in range(2)]
            ebufP = [psb(f"ebuf{i}", [128, 1024], BF16) for i in range(3)]
            etmpP = psb("etmp", [128, 1024], BF16)
            rdb = psb("rdb", [1, 512], F32)
            rbb = psb("rbb", [128, 512], F32)

            pqk = ps.enter_context(nc.psum_tensor("p2qk", [128, 512], F32))
            pq = pqk[:, 0:256]
            pk = pqk[:, 256:512]
            ptr2_full = ps.enter_context(nc.psum_tensor("p2tr", [128, 1024], BF16))
            ptr2 = ptr2_full[:, 0:512]
            pssP = [ps.enter_context(nc.psum_tensor(f"p2sp{i}", [128, 1024], F32)) for i in range(2)]
            pso = ps.enter_context(nc.psum_tensor("p2o", [128, 512], F32))
            psd = ps.enter_context(nc.psum_tensor("p2d", [128, 512], F32))

            gpi = [0]

            def emit_loads(pr):
                pb = pr % 2
                @block.sync
                def _(sy):
                    if pr == 0:
                        W(sy, "PRES", 16 * 32)
                        W(sy, "P1SX", 16 * NT)
                        W(sy, "P1SV", 16 * NT)
                    if pr >= 2:
                        W(sy, "P2TE", ntt * (pr - 1))
                    dinc("WQL", sy.dma_start(
                        out=wq_sl[pb][:].rearrange("p (kb c) -> p kb c", kb=NKB),
                        in_=wqB_h[:, pr * 256:(pr + 1) * 256].rearrange("(kb p) c -> p kb c", p=128)))
                    dinc("WQL", sy.dma_start(
                        out=wk_sl[pb][:].rearrange("p (kb c) -> p kb c", kb=NKB),
                        in_=wkB_h[:, pr * 256:(pr + 1) * 256].rearrange("(kb p) c -> p kb c", p=128)))

            def emit_xload(pr, t):
                @block.sync
                def _(sy):
                    need = pr * ntt + t
                    if need >= 2:
                        W(sy, "P2MMQ", need - 1)
                    dinc("P2XL", sy.dma_start(
                        out=xtc2[need % 2][:].rearrange("p (kb c) -> p kb c", kb=NKB),
                        in_=xT_h[:, t * 128:(t + 1) * 128].rearrange("(kb p) c -> p kb c", p=128)))

            def emit_prep_tile(pr, t):
                pb = pr % 2
                gi = pr * ntt + t
                xb3 = gi % 2
                @block.tensor
                def _(te):
                    W(te, "P2XL", 16 * (gi + 1))
                    W(te, "WQL", 32 * (pr + 1))
                    if gi >= 1:
                        W(te, "P2CP", gi)
                    for k in range(NKB):
                        te.matmul(pq, xtc2[xb3][:, k * 128:(k + 1) * 128],
                                  wq_sl[pb][:, k * 256:(k + 1) * 256],
                                  start=(k == 0), stop=(k == NKB - 1), skip_group_check=True)
                    for k in range(NKB):
                        instr = te.matmul(pk, xtc2[xb3][:, k * 128:(k + 1) * 128],
                                          wk_sl[pb][:, k * 256:(k + 1) * 256],
                                          start=(k == 0), stop=(k == NKB - 1), skip_group_check=True)
                    inc("P2MMQ", instr)
                @block.vector
                def _(v):
                    W(v, "P2MMQ", gi + 1)
                    if gi >= 2:
                        W(v, "P2RC", gi - 1)
                    inc("P2CP", v.tensor_copy(pqs[gi % 2][:], pqk[:]))
                    W(v, "P2CP", cur("P2CP"))
                    fence(v, v.tensor_mul(sqv[:], pqs[gi % 2][:], pqs[gi % 2][:]))
                    for s2 in range(4):
                        instr = v.tensor_reduce(msd[:, s2:s2 + 1], sqv[:, s2 * 128:(s2 + 1) * 128],
                                                AX.X, ALU.add)
                    fence(v, instr)
                    fence(v, v.tensor_scalar(msd[:, 4:8], msd[:, 0:4], 128.0 * 1e-6, None, ALU.add))
                    inc("P2MS", v.reciprocal(msd[:, 8:12], msd[:, 4:8]))
                @block.scalar
                def _(sc):
                    W(sc, "P2MS", gi + 1)
                    if gi >= 1:
                        W(sc, "P2RC", gi)
                    inc("P2RS", sc.activation(msd[:, 12:16], msd[:, 8:12], AF.Sqrt, scale=128.0))
                @block.vector
                def _(v):
                    W(v, "P2RS", gi + 1)
                    if gi >= 2:
                        W(v, "P2TR", gi - 1)
                    for s2 in range(4):
                        instr = v.tensor_scalar(qrs[:, s2 * 128:(s2 + 1) * 128],
                                                pqs[gi % 2][:, s2 * 128:(s2 + 1) * 128],
                                                msd[:, 12 + s2:13 + s2], None, ALU.mult)
                    fence(v, instr)
                    CW = ropeCW[:, pr * 512:(pr + 1) * 512]
                    SW = ropeSW[:, pr * 512:(pr + 1) * 512]
                    q3 = qrs[:].rearrange("p (n two) -> p n two", two=2)
                    j3 = jro2[:].rearrange("p (n two) -> p n two", two=2)
                    s3 = SW.rearrange("p (n two) -> p n two", two=2)
                    v.tensor_mul(jro[:], qrs[:], CW)
                    v.tensor_mul(j3[:, :, 0:1], q3[:, :, 1:2], s3[:, :, 0:1])
                    fence(v, v.tensor_mul(j3[:, :, 1:2], q3[:, :, 0:1], s3[:, :, 1:2]))
                    inc("P2RC", v.tensor_add(qbb[gi % 2][:], jro[:], jro2[:]))

            def emit_prep_tr(pr, t):
                pb = pr % 2
                gi = pr * ntt + t
                @block.tensor
                def _(te):
                    W(te, "P2RC", gi + 1)
                    if gi >= 1:
                        W(te, "P2TE", gi)
                    for s2 in range(4):
                        instr = te.transpose(ptr2_full[:, s2 * 128:(s2 + 1) * 128],
                                             qbb[gi % 2][:, s2 * 128:(s2 + 1) * 128], idb[:])
                    inc("P2TR", instr)
                @block.vector
                def _(v):
                    W(v, "P2TR", gi + 1)
                    if t == 0 and pr >= 2 and p2sub >= 2:
                        W(v, "TAV", (2 * pr - 2) * 72)
                    instr = v.tensor_copy(
                        qkTp[pb][:].rearrange("p (g tt) -> p g tt", g=4)[:, :, t * 128:(t + 1) * 128],
                        ptr2_full[:, 0:512].rearrange("p (g c) -> p g c", g=4))
                    inc("P2TE", instr)

            def emit_attn_tile(pr, hl, n):
                pb = pr % 2
                h = 2 * pr + hl
                QOF = hl * T
                KOF = 2 * T + hl * T
                ni = 4 * n + 4
                np2 = ni // 2
                hn = h * NN + n
                ap0 = gpi[0]
                tanh_fence = {}

                def emit_mms(pi):
                    ap = ap0 + pi
                    @block.tensor
                    def _(te):
                        if pi == 0:
                            W(te, "P2TE", ntt * (pr + 1))
                            W(te, "VHL", 16 * (h + 1))
                            if hn >= 1:
                                W(te, "OE", hn)
                        if ap >= 2:
                            W(te, "FEN_SC_TANH", ap - 1)
                        te.matmul(pssP[ap % 2][:, 0:512],
                                  qkTp[pb][:, KOF + 2 * pi * 128:KOF + (2 * pi + 1) * 128],
                                  qkTp[pb][:, QOF + n * 512:QOF + (n + 1) * 512],
                                  start=True, stop=True, skip_group_check=True)
                        instr = te.matmul(pssP[ap % 2][:, 512:1024],
                                          qkTp[pb][:, KOF + (2 * pi + 1) * 128:KOF + (2 * pi + 2) * 128],
                                          qkTp[pb][:, QOF + n * 512:QOF + (n + 1) * 512],
                                          start=True, stop=True, skip_group_check=True)
                        inc("TS", instr)

                def emit_tanh(pi):
                    ap = ap0 + pi
                    @block.scalar
                    def _(sc):
                        W(sc, "TS", ap + 1)
                        instr = sc.activation(ts1P[ap % 2][:], pssP[ap % 2][:], AF.Tanh, scale=TANH_SC)
                        inc("FEN_SC_TANH", instr)
                        tanh_fence[pi] = cur("FEN_SC_TANH")

                def emit_exp(pi):
                    ap = ap0 + pi
                    diag = pi >= 2 * n
                    @block.scalar
                    def _(sc):
                        W(sc, "FEN_SC_TANH", tanh_fence[pi])
                        if ap >= 3:
                            W(sc, "TAV", ap - 2)
                        if diag:
                            W(sc, "VM", cur("VM"))
                        instr = sc.activation(etmpP[:] if diag else ebufP[ap % 3][:],
                                              ts1P[ap % 2][:], AF.Exp, scale=EXP_SC)
                        inc("TH", instr)
                    if diag:
                        @block.vector
                        def _(v):
                            W(v, "TH", ap + 1)
                            for half in range(2):
                                off_idx = 2 * pi - 4 * n + half
                                inc("VM", v.tensor_mul(
                                    ebufP[ap % 3][:, half * 512:(half + 1) * 512],
                                    etmpP[:, half * 512:(half + 1) * 512],
                                    maskt[:, off_idx * 512:(off_idx + 1) * 512]))

                def emit_dav(pi):
                    ap = ap0 + pi
                    diag = pi >= 2 * n
                    @block.tensor
                    def _(te):
                        W(te, "TH", ap + 1)
                        if diag:
                            W(te, "VM", cur("VM"))
                        for half in range(2):
                            i = 2 * pi + half
                            te.matmul(psd[0:1, :], onescb[:],
                                      ebufP[ap % 3][:, half * 512:(half + 1) * 512],
                                      start=(i == 0), stop=(i == ni - 1), skip_group_check=True)
                            instr = te.matmul(pso[:],
                                              vhb[h % 2][:, i * 128:(i + 1) * 128],
                                              ebufP[ap % 3][:, half * 512:(half + 1) * 512],
                                              start=(i == 0), stop=(i == ni - 1), skip_group_check=True)
                        inc("TAV", instr)

                done = [0]
                for r in range(0, np2, 2):
                    emit_mms(r)
                    if r + 1 < np2:
                        emit_mms(r + 1)
                    for q in (r - 2, r - 1):
                        if 0 <= q:
                            emit_exp(q)
                            emit_dav(q)
                            done[0] = q + 1
                    emit_tanh(r)
                    if r + 1 < np2:
                        emit_tanh(r + 1)
                for q in range(done[0], np2):
                    emit_exp(q)
                    emit_dav(q)
                gpi[0] += np2

                @block.vector
                def _(v):
                    W(v, "TAV", gpi[0])
                    inc("RD", v.reciprocal(rdb[0:1, :], psd[0:1, :]))
                @block.tensor
                def _(te):
                    W(te, "RD", hn + 1)
                    instr = te.matmul(pssP[gpi[0] % 2][:, 0:512], onesrf[0:1, :], rdb[0:1, :],
                                      start=True, stop=True, skip_group_check=True)
                    inc("TB", instr)
                @block.vector
                def _(v):
                    W(v, "TB", hn + 1)
                    if n == 0 and h >= 2:
                        W(v, "AOS", 16 * (h - 1))
                    fence(v, v.tensor_copy(rbb[:], pssP[gpi[0] % 2][:, 0:512]))
                    instr = v.tensor_mul(ostg[hl][:, n * 512:(n + 1) * 512], pso[:], rbb[:])
                    inc("OE", instr)

            def emit_vload(h):
                @block.sync
                def _(sy):
                    if h >= 2:
                        W(sy, "VHU", h - 1)
                    dinc("VHL", sy.dma_start(
                        out=vhb[h % 2][:].rearrange("p (s c) -> p s c", s=32),
                        in_=vtok_h[:, h * 128:(h + 1) * 128].rearrange("(s p) c -> p s c", p=128)))

            tr_pend = []
            def push_prep(pr, t):
                emit_xload(pr, t)
                emit_prep_tile(pr, t)
                tr_pend.append((pr, t))
                while len(tr_pend) > 2:
                    emit_prep_tr(*tr_pend.pop(0))
            def flush_tr():
                while tr_pend:
                    emit_prep_tr(*tr_pend.pop(0))
            emit_loads(0)
            for t in range(ntt):
                push_prep(0, t)
            flush_tr()
            for pr in range(npr):
                if pr + 1 < npr:
                    emit_loads(pr + 1)
                natt = 0
                for hl in (range(2) if p2sub >= 2 else []):
                    h = 2 * pr + hl
                    emit_vload(h)
                    for n in range(NN):
                        emit_attn_tile(pr, hl, n)
                        if pr + 1 < npr:
                            for t in range(2 * natt, min(2 * natt + 2, ntt)):
                                push_prep(pr + 1, t)
                        natt += 1
                    @block.tensor
                    def _(te):
                        W(te, "OE", (h + 1) * NN)
                        inc("VHU", te.nop())
                    @block.scalar
                    def _(sc):
                        W(sc, "OE", (h + 1) * NN)
                        dinc("AOS", sc.dma_start(out=aoT_h[h * 128:(h + 1) * 128, :], in_=ostg[hl][:]))
                if p2sub < 2 and pr + 1 < npr:
                    for t in range(ntt):
                        push_prep(pr + 1, t)
                flush_tr()
            barrier()

    if phases >= 3:
        with ExitStack() as ps:
            psb = lambda name, shape, dty: ps.enter_context(nc.sbuf_tensor(name, shape, dty))
            wosb = psb("wosb", [128, NKB * D], BF16)
            aot = [psb(f"aot{i}", [128, D], BF16) for i in range(2)]
            xres = [psb(f"xres{i}", [128, D], F32) for i in range(2)]
            sres = psb("sres", [128, D], F32)
            junk3 = sres
            stats = psb("stats", [128, 8], F32)
            dscr = psb("dscr", [128, D], F32)
            h1t = [psb(f"h1t{i}", [128, D], F32) for i in range(2)]
            hib = psb("hib", [128, D], BF16)
            lob = psb("lob", [128, D], BF16)
            hTst0 = psb("hTst0", [128, NKB * 512], BF16)
            hTst = [hTst0, hTst0]
            loTst = psb("loTst", [128, NKB * 512], BF16)
            lgsb = psb("lgsb", [128, 512], F32)
            ltok = psb("ltok", [128, 16], F32)
            gsc = psb("gsc", [128, 64], F32)

            po3 = [ps.enter_context(nc.psum_tensor(f"p3o{i}", [128, 512], F32)) for i in range(2)]
            ptrh = ps.enter_context(nc.psum_tensor("p3trh", [128, 2048], BF16))
            ptrl = ps.enter_context(nc.psum_tensor("p3trl", [128, 2048], BF16))
            plg = ps.enter_context(nc.psum_tensor("p3lg", [128, 512], F32))
            ptt = ps.enter_context(nc.psum_tensor("p3tt", [128, 128], F32))

            @block.vector
            def _(v):
                inc("LGZ", v.memset(lgsb[:], 0.0))
            @block.sync
            def _(sy):
                W(sy, "PRES", 16 * 48)
                W(sy, "AOS", 16 * H)
                for c in range(4):
                    dinc("WOL", sy.dma_start(
                        out=wosb[:].rearrange("p (kb n) -> p kb n", kb=NKB)[:, c * 4:(c + 1) * 4, :],
                        in_=woB_h.rearrange("(kb p) n -> p kb n", p=128)[:, c * 4:(c + 1) * 4, :]))
                for t in range(nt3):
                    b = t % 2
                    if t >= 2:
                        W(sy, "P3MM", 4 * (t - 1))
                        W(sy, "P3SE", 4 * (t - 1))
                    dinc("P3AL", sy.dma_start(
                        out=aot[b][:].rearrange("p (kb c) -> p kb c", kb=NKB),
                        in_=aoT_h[:, t * 128:(t + 1) * 128].rearrange("(kb p) c -> p kb c", p=128)))
                    dinc("P3XL", sy.dma_start(out=xres[b][:], in_=x_h[t * 128:(t + 1) * 128, :]))

            for t in range(nt3):
                b = t % 2
                n = t // 4
                nb = n % 2
                @block.tensor
                def _(te):
                    W(te, "P3AL", 16 * (t + 1))
                    if t == 0:
                        W(te, "WOL", 16 * 4)
                    for d in range(4):
                        if 4 * t + d >= 2:
                            W(te, "P3SE", 4 * t + d - 1)
                        if p3x >= 2:
                            inc("P3MM", te.nop())
                            continue
                        for k in range(NKB):
                            instr = te.matmul(po3[d % 2][:],
                                              aot[b][:, k * 128:(k + 1) * 128],
                                              wosb[:, k * D + d * 512:k * D + d * 512 + 512],
                                              start=(k == 0), stop=(k == NKB - 1))
                        inc("P3MM", instr)
                @block.vector
                def _(v):
                    W(v, "P3XL", 16 * (t + 1))
                    for d in range(4):
                        W(v, "P3MM", 4 * t + d + 1)
                        if p3x >= 2:
                            inc("P3SE", v.tensor_copy(sres[:, d * 512:(d + 1) * 512],
                                                      xres[b][:, d * 512:(d + 1) * 512]))
                        else:
                            inc("P3SE", v.tensor_add(sres[:, d * 512:(d + 1) * 512],
                                                     xres[b][:, d * 512:(d + 1) * 512], po3[d % 2][:]))
                    if p3x >= 1:
                        inc("P3LN", v.tensor_copy(dscr[:], sres[:]))
                    else:
                        fence(v, v.tensor_reduce(stats[:, 0:1], sres[:], AX.X, ALU.add))
                        fence(v, v.tensor_scalar(stats[:, 1:2], stats[:, 0:1], 1.0 / D, None, ALU.mult))
                        inc("P3LN", v.tensor_scalar(dscr[:], sres[:], stats[:, 1:2], None, ALU.subtract))
                @block.scalar
                def _(sc):
                    W(sc, "P3LN", t + 1)
                    if p3x >= 1:
                        inc("P3SQ", sc.nop())
                    else:
                        inc("P3SQ", sc.activation(junk3[:], dscr[:], AF.Square, accum_out=stats[:, 2:3]))
                @block.vector
                def _(v):
                    W(v, "P3SQ", t + 1)
                    if p3x >= 1:
                        inc("P3RC", v.engine_nop())
                    else:
                        fence(v, v.tensor_scalar(stats[:, 3:4], stats[:, 2:3], float(D) * 1e-5, None, ALU.add))
                        inc("P3RC", v.reciprocal(stats[:, 4:5], stats[:, 3:4]))
                @block.scalar
                def _(sc):
                    W(sc, "P3RC", t + 1)
                    if p3x >= 1:
                        inc("P3SR", sc.nop())
                    else:
                        inc("P3SR", sc.activation(stats[:, 5:6], stats[:, 4:5], AF.Sqrt, scale=float(D)))
                @block.vector
                def _(v):
                    W(v, "P3SR", t + 1)
                    if t >= 2:
                        W(v, "P3H1S", 16 * (t - 1))
                    if p3x >= 1:
                        inc("P3HD", v.tensor_copy(h1t[b][:], dscr[:]))
                    else:
                        inc("P3HD", v.tensor_scalar(h1t[b][:], dscr[:], stats[:, 5:6], None, ALU.mult))
                    W(v, "P3HD", cur("P3HD"))
                    if t >= 1 and p3sub >= 2:
                        W(v, "P3TR", t)
                    fence(v, v.tensor_copy(hib[:], h1t[b][:]))
                    inc("P3HL", v.tensor_sub(lob[:], h1t[b][:], hib[:]))
                @block.scalar
                def _(sc):
                    W(sc, "P3HD", t + 1)
                    dinc("P3H1S", sc.dma_start(out=h1_h[t * 128:(t + 1) * 128, :], in_=h1t[b][:]))
                if p3sub < 2:
                    continue
                @block.tensor
                def _(te):
                    W(te, "P3HL", t + 1)
                    if t >= 1:
                        W(te, "P3TE", 2 * t)
                    for k in range(NKB):
                        te.transpose(ptrh[:, k * 128:(k + 1) * 128], hib[:, k * 128:(k + 1) * 128], idb[:])
                        instr = te.transpose(ptrl[:, k * 128:(k + 1) * 128], lob[:, k * 128:(k + 1) * 128], idb[:])
                    inc("P3TR", instr)
                @block.vector
                def _(v):
                    W(v, "P3TR", t + 1)
                    if t % 4 == 0 and n >= 1 and p3sub >= 3:
                        W(v, "P3GD", n)
                        W(v, "P3HTS", 16 * n)
                        W(v, "P3GM", n)
                    v.tensor_copy(
                        hTst[nb][:].rearrange("p (kb c) -> p kb c", kb=NKB)[:, :, (t % 4) * 128:(t % 4 + 1) * 128],
                        ptrh[:].rearrange("p (kb c) -> p kb c", kb=NKB))
                    instr = v.tensor_copy(
                        loTst[:].rearrange("p (kb c) -> p kb c", kb=NKB)[:, :, (t % 4) * 128:(t % 4 + 1) * 128],
                        ptrl[:].rearrange("p (kb c) -> p kb c", kb=NKB))
                    inc("P3TE", instr, 2)
                if t % 4 == 3 and p3sub >= 3:
                    @block.tensor
                    def _(te):
                        W(te, "P3TE", 2 * (t + 1))
                        if n == 0:
                            W(te, "SETV", cur("SETV"))
                        if n >= 1:
                            W(te, "P3GE", n)
                        nmm = 0
                        for k in range(NKB):
                            for (lw, rh) in [(ghi, hTst[nb]), (glo, hTst[nb]), (ghi, loTst)]:
                                nmm += 1
                                instr = te.matmul(plg[0:16, :], lw[:, k * 16:(k + 1) * 16],
                                                  rh[:, k * 512:(k + 1) * 512],
                                                  start=(nmm == 1), stop=(nmm == 3 * NKB))
                        inc("P3GM", instr)
                    @block.vector
                    def _(v):
                        W(v, "P3GM", n + 1)
                        W(v, "LGZ", 1)
                        if n >= 1:
                            W(v, "P3TT", 4 * n)
                        inc("P3GE", v.tensor_copy(lgsb[0:16, :], plg[0:16, :]))
                    for c in range(4):
                        @block.tensor
                        def _(te):
                            W(te, "P3GE", n + 1)
                            if 4 * n + c >= 1:
                                W(te, "P3TTE", 4 * n + c)
                            inc("P3TT", te.transpose(ptt[:], lgsb[:, c * 128:(c + 1) * 128], idf[:]))
                        @block.vector
                        def _(v):
                            W(v, "P3TT", 4 * n + c + 1)
                            tt = n * 4 + c
                            if tt >= 1:
                                W(v, "P3EX", tt)
                            inc("P3TTE", v.tensor_copy(ltok[:], ptt[:, 0:16]))
                            W(v, "P3TTE", cur("P3TTE"))
                            fence(v, v.max(gsc[:, 0:8], ltok[:]))
                            inc("P3M8", v.tensor_scalar(gsc[:, 8:24], ltok[:], gsc[:, 5:6], None, ALU.is_ge))
                        @block.scalar
                        def _(sc):
                            W(sc, "P3M8", 4 * n + c + 1)
                            inc("P3EX", sc.activation(gsc[:, 24:40], ltok[:], AF.Exp))
                        @block.vector
                        def _(v):
                            W(v, "P3EX", 4 * n + c + 1)
                            tt = n * 4 + c
                            fence(v, v.tensor_reduce(gsc[:, 40:41], gsc[:, 24:40], AX.X, ALU.add))
                            fence(v, v.reciprocal(gsc[:, 41:42], gsc[:, 40:41]))
                            fence(v, v.tensor_scalar(gsc[:, 42:58], gsc[:, 24:40], gsc[:, 41:42], None, ALU.mult))
                            instr = v.tensor_mul(cwsb[:, tt * 16:(tt + 1) * 16], gsc[:, 42:58], gsc[:, 8:24])
                            inc("P3CW", instr)
                            if c == 3:
                                inc("P3GD", v.engine_nop())
                    @block.scalar
                    def _(sc):
                        W(sc, "P3TE", 2 * (t + 1))
                        dinc("P3HTS", sc.dma_start(
                            out=hT_h[:, n * 512:(n + 1) * 512].rearrange("(kb p) c -> p kb c", p=128),
                            in_=hTst[nb][:].rearrange("p (kb c) -> p kb c", kb=NKB)))
            barrier()

    if phases >= 5:
        with ExitStack() as ps:
            psb = lambda name, shape, dty: ps.enter_context(nc.sbuf_tensor(name, shape, dty))
            wdsb = psb("wdsb", [128, NMB * D], BF16)
            wgsl = [psb(f"wgsl{i}", [128, D], BF16) for i in range(2)]
            wusl = [psb(f"wusl{i}", [128, D], BF16) for i in range(2)]
            hTn = [psb(f"hTn{i}", [128, NKB * 512], BF16) for i in range(2)]
            aTn = psb("aTn", [128, NMB * 512], BF16)
            sg5 = psb("sg5", [128, 512], F32)
            ysb = [psb(f"ysb{i}", [128, D], F32) for i in range(2)]

            pg5 = [ps.enter_context(nc.psum_tensor(f"p5g{i}", [128, 512], F32)) for i in range(2)]
            pu5 = [ps.enter_context(nc.psum_tensor(f"p5u{i}", [128, 512], F32)) for i in range(2)]
            py5 = [ps.enter_context(nc.psum_tensor(f"p5y{i}", [128, 512], F32)) for i in range(2)]

            nws = [0]
            nyt = [0]
            for e in range(E):
                @block.sync
                def _(sy):
                    if e == 0:
                        W(sy, "PRES", cur("PRES"))
                        W(sy, "P3HTS", 16 * NN)
                        W(sy, "P3GD", NN)
                        W(sy, "P3H1S", 16 * NT)
                    if e >= 1:
                        W(sy, "P5YM", e * NN * 16)
                    dinc("P5WD", sy.dma_start(
                        out=wdsb[:].rearrange("p (mb c) -> p mb c", mb=NMB),
                        in_=wdB_h[e].rearrange("(mb p) c -> p mb c", p=128)))
                for n in range(NN):
                    gn = e * NN + n
                    hb = gn % 2
                    @block.sync
                    def _(sy):
                        if gn >= 2:
                            W(sy, "P5GU", (gn - 1) * NMB)
                        dinc("P5HL", sy.dma_start(
                            out=hTn[hb][:].rearrange("p (kb c) -> p kb c", kb=NKB),
                            in_=hT_h[:, n * 512:(n + 1) * 512].rearrange("(kb p) c -> p kb c", p=128)))
                        for m in range(NMB):
                            wb = nws[0] % 2
                            if nws[0] >= 2:
                                W(sy, "P5GU", nws[0] - 1)
                            dinc("P5WL", sy.dma_start(out=wgsl[wb][:], in_=wgB_h[e, m]))
                            dinc("P5WL", sy.dma_start(out=wusl[wb][:], in_=wuB_h[e, m]))
                            nws[0] += 1
                    for m in range(NMB):
                        gm = gn * NMB + m
                        wb = gm % 2
                        gb = gm % 2
                        @block.tensor
                        def _(te):
                            W(te, "P5HL", 16 * (gn + 1))
                            W(te, "P5WL", 32 * (gm + 1))
                            if gm >= 2:
                                W(te, "P5AE", gm - 1)
                            for k in range(NKB):
                                te.matmul(pg5[gb][:], wgsl[wb][:, k * 128:(k + 1) * 128],
                                          hTn[hb][:, k * 512:(k + 1) * 512],
                                          start=(k == 0), stop=(k == NKB - 1),
                                          skip_group_check=True)
                                instr = te.matmul(pu5[gb][:], wusl[wb][:, k * 128:(k + 1) * 128],
                                                  hTn[hb][:, k * 512:(k + 1) * 512],
                                                  start=(k == 0), stop=(k == NKB - 1),
                                                  skip_group_check=True)
                            inc("P5GU", instr)
                        @block.scalar
                        def _(sc):
                            W(sc, "P5GU", gm + 1)
                            if gm >= 1:
                                W(sc, "P5AE", gm)
                            inc("P5SI", sc.activation(sg5[:], pg5[gb][:], AF.Silu))
                        @block.vector
                        def _(v):
                            W(v, "P5SI", gm + 1)
                            if gn >= 1 and m == 0:
                                W(v, "P5YM", gn * 16)
                            inc("P5AE", v.tensor_mul(aTn[:, m * 512:(m + 1) * 512], sg5[:], pu5[gb][:]))
                    for tj in range(4):
                        yb = (4 * gn + tj) % 2
                        for dd in range(4):
                            yp = nyt[0] % 2
                            @block.tensor
                            def _(te):
                                W(te, "P5AE", (gn + 1) * NMB)
                                if n == 0 and tj == 0 and dd == 0:
                                    W(te, "P5WD", 16 * (e + 1))
                                if nyt[0] >= 2:
                                    W(te, "P5YE", nyt[0] - 1)
                                for m in range(NMB):
                                    instr = te.matmul(py5[yp][:],
                                                      aTn[:, m * 512 + tj * 128:m * 512 + (tj + 1) * 128],
                                                      wdsb[:, m * D + dd * 512:m * D + dd * 512 + 512],
                                                      start=(m == 0), stop=(m == NMB - 1))
                                inc("P5YM", instr)
                            @block.vector
                            def _(v):
                                W(v, "P5YM", nyt[0] + 1)
                                if nyt[0] >= 8:
                                    W(v, "P5YS", 16 * (nyt[0] // 4 - 1))
                                tt = n * 4 + tj
                                inc("P5YE", v.tensor_scalar(ysb[yb][:, dd * 512:(dd + 1) * 512], py5[yp][:],
                                                            cwsb[:, tt * 16 + e:tt * 16 + e + 1], None, ALU.mult))
                            nyt[0] += 1
                        @block.gpsimd
                        def _(g):
                            W(g, "P5YE", nyt[0])
                            tt = n * 4 + tj
                            if e == 0:
                                instr = g.dma_start(out=yac_h[tt * 128:(tt + 1) * 128, :], in_=ysb[yb][:])
                            else:
                                instr = g.dma_start(out=yac_h[tt * 128:(tt + 1) * 128, :], in_=ysb[yb][:],
                                                    accum_op=ALU.add)
                            dinc("P5YS", instr)
            barrier()

    if phases >= 6:
        with ExitStack() as ps:
            psb = lambda name, shape, dty: ps.enter_context(nc.sbuf_tensor(name, shape, dty))
            yld = [psb(f"yld{i}", [128, D], F32) for i in range(2)]
            hld = [psb(f"hld{i}", [128, D], F32) for i in range(2)]
            s6 = psb("s6", [128, D], F32)
            junk6 = psb("junk6", [128, D], F32)
            st6 = psb("st6", [128, 8], F32)
            dscr6 = psb("dscr6", [128, D], F32)
            ot6 = [psb(f"ot6{i}", [128, D], F32) for i in range(2)]
            @block.sync
            def _(sy):
                W(sy, "P5YS", 16 * NT * E)
                W(sy, "P3H1S", 16 * NT)
                for t in range(NT):
                    b = t % 2
                    if t >= 2:
                        W(sy, "P6A", t - 1)
                    dinc("P6L", sy.dma_start(out=yld[b][:], in_=yac_h[t * 128:(t + 1) * 128, :]))
                    dinc("P6L", sy.dma_start(out=hld[b][:], in_=h1_h[t * 128:(t + 1) * 128, :]))
            for t in range(NT):
                b = t % 2
                @block.vector
                def _(v):
                    W(v, "P6L", 32 * (t + 1))
                    inc("P6A", v.tensor_add(s6[:], yld[b][:], hld[b][:]))
                    W(v, "P6A", cur("P6A"))
                    fence(v, v.tensor_reduce(st6[:, 0:1], s6[:], AX.X, ALU.add))
                    fence(v, v.tensor_scalar(st6[:, 1:2], st6[:, 0:1], 1.0 / D, None, ALU.mult))
                    inc("P6D", v.tensor_scalar(dscr6[:], s6[:], st6[:, 1:2], None, ALU.subtract))
                @block.scalar
                def _(sc):
                    W(sc, "P6D", t + 1)
                    inc("P6SQ", sc.activation(junk6[:], dscr6[:], AF.Square, accum_out=st6[:, 2:3]))
                @block.vector
                def _(v):
                    W(v, "P6SQ", t + 1)
                    fence(v, v.tensor_scalar(st6[:, 3:4], st6[:, 2:3], float(D) * 1e-5, None, ALU.add))
                    inc("P6RC", v.reciprocal(st6[:, 4:5], st6[:, 3:4]))
                @block.scalar
                def _(sc):
                    W(sc, "P6RC", t + 1)
                    inc("P6SR", sc.activation(st6[:, 5:6], st6[:, 4:5], AF.Sqrt, scale=float(D)))
                @block.vector
                def _(v):
                    W(v, "P6SR", t + 1)
                    if t >= 2:
                        W(v, "P6S", 16 * (t - 1))
                    inc("P6N", v.tensor_scalar(ot6[b][:], dscr6[:], st6[:, 5:6], None, ALU.mult))
                @block.scalar
                def _(sc):
                    W(sc, "P6N", t + 1)
                    dinc("P6S", sc.dma_start(out=out_h[t * 128:(t + 1) * 128, :], in_=ot6[b][:]))
            @block.gpsimd
            def _(g):
                W(g, "P6S", 16 * NT)

    if dump:
        barrier()
        @block.sync
        def _(sy):
            pairs = {"d_xT": xT_h, "d_vtok": vtok_h, "d_aoT": aoT_h, "d_h1": h1_h,
                     "d_hT": hT_h, "d_yac": yac_h}
            for name, ap in dump_h.items():
                if name in pairs:
                    dinc("DMP", sy.dma_start(out=ap, in_=pairs[name]))
                elif name == "d_wgB":
                    dinc("DMP", sy.dma_start(out=ap, in_=wgB_h[0]))
        if "d_cw" in dump_h:
            @block.scalar
            def _(sc):
                dinc("DMP", sc.dma_start(out=dump_h["d_cw"], in_=cwsb[:]))
        @block.gpsimd
        def _(g):
            W(g, "DMP", cur("DMP"))

    for nm, val in waits_log:
        if val > cur(nm):
            raise RuntimeError(f"wait on {nm} for {val} but only {cur(nm)} increments emitted")
    es.close()
    return nc


def _host_prep(inp):
    f32 = np.float32
    x = np.ascontiguousarray(np.asarray(inp["L0_attn_x"], f32))
    wqT = np.ascontiguousarray(np.asarray(inp["L0_attn_wq"], f32).T)
    wkT = np.ascontiguousarray(np.asarray(inp["L0_attn_wk"], f32).T)
    wvT = np.ascontiguousarray(np.asarray(inp["L0_attn_wv"], f32).T)
    woT = np.ascontiguousarray(np.asarray(inp["L0_attn_wo"], f32).T)
    gwT = np.ascontiguousarray(np.asarray(inp["L0_ffn_gate_w"], f32).T)
    wgT = np.ascontiguousarray(np.transpose(np.asarray(inp["L0_ffn_w_gate"], f32), (0, 2, 1)))
    wuT = np.ascontiguousarray(np.transpose(np.asarray(inp["L0_ffn_w_up"], f32), (0, 2, 1)))
    wdT = np.ascontiguousarray(np.transpose(np.asarray(inp["L0_ffn_w_down"], f32), (0, 2, 1)))
    cos16 = np.asarray(inp["L0_attn_rope_cos"], f32)[:H]
    sin16 = np.asarray(inp["L0_attn_rope_sin"], f32)[:H]
    qnw = np.asarray(inp["L0_attn_q_norm_w"], f32)
    knw = np.asarray(inp["L0_attn_k_norm_w"], f32)

    cosE = np.repeat(cos16, 2, axis=1).reshape(-1)
    sinE = np.repeat(sin16, 2, axis=1).reshape(-1)
    sign = np.tile(np.array([-1.0, 1.0], f32), D // 2)
    swap = lambda w: np.ascontiguousarray(w.reshape(-1, 2)[:, ::-1]).reshape(-1)
    rows = np.stack([
        cosE, sinE, sign,
        np.tile(qnw, H), np.tile(swap(qnw), H),
        np.tile(knw, H), np.tile(swap(knw), H),
        np.ones(D, f32),
    ]).astype(f32)

    import ml_dtypes
    masks = np.zeros((128, 2048), np.float32)
    rr = np.arange(128)[:, None]
    cc = np.arange(512)[None, :]
    for oi, off in enumerate((0, 128, 256, 384)):
        masks[:, oi * 512:(oi + 1) * 512] = (cc >= rr + off)
    masks = masks.astype(ml_dtypes.bfloat16)
    identb = np.eye(128, dtype=np.float32).astype(ml_dtypes.bfloat16)
    identf = np.eye(128, dtype=np.float32)
    onescb = np.ones((128, 1), np.float32).astype(ml_dtypes.bfloat16)
    consts = np.tile(np.array([TANH_SC, EXP_SC, 128.0, float(D)], f32)[None, :], (128, 1))

    shared = dict(wqT=wqT, wkT=wkT, wvT=wvT, woT=woT, gwT=gwT,
                  wgT=wgT, wuT=wuT, wdT=wdT, rows=rows, masks=masks,
                  identb=identb, identf=identf, onescb=onescb, consts=consts)
    return x, shared


def kernel(**inp):
    x, shared = _host_prep(inp)
    nc = build(phases=6)
    in_maps = [dict(shared, x=np.ascontiguousarray(x[b])) for b in range(8)]
    res = run_bass_kernel_spmd(nc, in_maps, core_ids=list(range(8)))
    return np.stack([np.asarray(res.results[b]["out"]) for b in range(8)]).astype(np.float32)


# version 09
# speedup vs baseline: 1.2094x; 1.0909x over previous
import numpy as np
from contextlib import ExitStack
import concourse.bass as bass
import concourse.mybir as mybir
from concourse.bass_utils import run_bass_kernel_spmd

F32 = mybir.dt.float32
BF16 = mybir.dt.bfloat16
AF = mybir.ActivationFunctionType
ALU = mybir.AluOpType
AX = mybir.AxisListType

T, D, H, DH = 4096, 2048, 16, 128
E, MD = 16, 1408
NKB = D // 128
NMB = MD // 128
NT = T // 128
NN = T // 512
TANH_SC = float((1.0 / np.sqrt(128.0)) / 50.0)
EXP_SC = 50.0


def build(phases=6, dump=(), p2sub=2, npr=8, ntt=NT, p3sub=3, nt3=NT, p3x=0):
    nc = bass.Bass("TRN2", target_bir_lowering=False)
    dtn = nc.dram_tensor
    x_h = dtn("x", [T, D], F32, kind="ExternalInput").ap()
    wqT_h = dtn("wqT", [D, D], F32, kind="ExternalInput").ap()
    wkT_h = dtn("wkT", [D, D], F32, kind="ExternalInput").ap()
    wvT_h = dtn("wvT", [D, D], F32, kind="ExternalInput").ap()
    woT_h = dtn("woT", [D, D], F32, kind="ExternalInput").ap()
    gwT_h = dtn("gwT", [D, E], F32, kind="ExternalInput").ap()
    wgT_h = dtn("wgT", [E, D, MD], F32, kind="ExternalInput").ap()
    wuT_h = dtn("wuT", [E, D, MD], F32, kind="ExternalInput").ap()
    wdT_h = dtn("wdT", [E, MD, D], F32, kind="ExternalInput").ap()
    rows_h = dtn("rows", [8, D], F32, kind="ExternalInput").ap()
    masks_h = dtn("masks", [128, 2048], BF16, kind="ExternalInput").ap()
    identb_h = dtn("identb", [128, 128], BF16, kind="ExternalInput").ap()
    identf_h = dtn("identf", [128, 128], F32, kind="ExternalInput").ap()
    onescb_h = dtn("onescb", [128, 1], BF16, kind="ExternalInput").ap()
    consts_h = dtn("consts", [128, 4], F32, kind="ExternalInput").ap()

    xT_h = dtn("xTs", [D, T], BF16, kind="Internal").ap()
    vtok_h = dtn("vtoks", [T, D], BF16, kind="Internal").ap()
    aoT_h = dtn("aoTs", [D, T], BF16, kind="Internal").ap()
    h1_h = dtn("h1s", [T, D], F32, kind="Internal").ap()
    hT_h = dtn("hTs", [D, T], BF16, kind="Internal").ap()
    wqB_h = dtn("wqB", [D, D], BF16, kind="Internal").ap()
    wkB_h = dtn("wkB", [D, D], BF16, kind="Internal").ap()
    woB_h = dtn("woB", [D, D], BF16, kind="Internal").ap()
    wgB_h = dtn("wgB", [E, NMB, 128, D], BF16, kind="Internal").ap()
    wuB_h = dtn("wuB", [E, NMB, 128, D], BF16, kind="Internal").ap()
    wdB_h = dtn("wdB", [E, MD, D], BF16, kind="Internal").ap()
    yac_h = dtn("yac", [T, D], F32, kind="Internal").ap()
    rsc_h = dtn("ropescratch", [4, D], F32, kind="Internal").ap()
    out_h = dtn("out", [T, D], F32, kind="ExternalOutput").ap()

    dump_h = {}
    for name, shp, dty in [
        ("d_xT", [D, T], BF16), ("d_vtok", [T, D], BF16),
        ("d_aoT", [D, T], BF16), ("d_h1", [T, D], F32),
        ("d_hT", [D, T], BF16), ("d_cw", [128, NT * E], F32),
        ("d_yac", [T, D], F32), ("d_wgB", [NMB, 128, D], BF16),
        ("d_wvsb", [128, NKB * D], BF16),
        ("d_qT", [128, 2 * T], BF16), ("d_kT", [128, 2 * T], BF16),
        ("d_pqs", [128, 512], F32), ("d_qro", [128, 256], BF16),
        ("d_wqB", [D, D], BF16), ("d_xtc2", [128, D], BF16), ("d_wqsl", [128, NKB * 256], BF16),
        ("d_msd", [128, 16], F32),
    ]:
        if name in dump:
            dump_h[name] = dtn(name, shp, dty, kind="ExternalOutput").ap()

    es = ExitStack()
    sb = lambda name, shape, dty: es.enter_context(nc.sbuf_tensor(name, shape, dty))

    cst = sb("cst", [128, 4], F32)
    idb = sb("idb", [128, 128], BF16)
    idf = sb("idf", [128, 128], F32)
    maskt = sb("maskt", [128, 2048], BF16)
    onescb = sb("s_onescb", [128, 1], BF16)
    onesrf = sb("onesrf", [1, 128], F32)
    ropeCW = sb("ropeCW", [128, 2 * D], BF16)
    ropeSW = sb("ropeSW", [128, 2 * D], BF16)
    ghi = sb("ghi", [128, NKB * E], BF16)
    glo = sb("glo", [128, NKB * E], BF16)
    gwf = sb("gwf", [128, NKB * E], F32)
    glof = sb("glof", [128, NKB * E], F32)
    cwsb = sb("cwsb", [128, NT * E], F32)
    setup_ps = ExitStack()
    rowsb = setup_ps.enter_context(nc.sbuf_tensor("rowsb", [1, 8 * D], F32))
    rwork = setup_ps.enter_context(nc.sbuf_tensor("rwork", [1, 5 * D], F32))

    sems = {}
    dma_sems = set()
    def S(name):
        if name not in sems:
            sems[name] = [es.enter_context(nc.semaphore(name)), 0]
        return sems[name]
    def inc(name, instr, n=1):
        s = S(name)
        instr.then_inc(s[0], n)
        s[1] += n
        if n >= 16:
            dma_sems.add(name)
        return instr
    def cur(name):
        return S(name)[1]
    def dinc(name, instr):
        return inc(name, instr, 16)
    waits_log = []
    def W(eng, name, val):
        waits_log.append((name, val))
        eng.wait_ge(S(name)[0], val)
    def fence(eng, instr, name=None):
        if name is None:
            name = "FEN_" + type(eng).__name__
        inc(name, instr)
        W(eng, name, cur(name))
        return instr

    block = es.enter_context(nc.Block())
    nc.const_aps.register(F32, TANH_SC, cst[:, 0:1])
    nc.const_aps.register(F32, EXP_SC, cst[:, 1:2])
    nc.const_aps.register(F32, 128.0, cst[:, 2:3])
    nc.const_aps.register(F32, float(D), cst[:, 3:4])

    bar_n = [0]
    ENGINES = ["sync", "scalar", "vector", "tensor", "gpsimd"]
    def barrier():
        @block.sync
        def _(sy):
            for nm in sorted(dma_sems):
                W(sy, nm, cur(nm))
        bar_n[0] += 1
        tgt = 5 * 16 * bar_n[0]
        for en in ENGINES:
            @getattr(block, en)
            def _(eng):
                inc("BAR", eng.nop(), 16)
        for en in ENGINES:
            @getattr(block, en)
            def _(eng):
                W(eng, "BAR", tgt)

    @block.sync
    def _(sy):
        for dst, src in [
            (cst[:], consts_h), (idb[:], identb_h), (idf[:], identf_h),
            (maskt[:], masks_h), (onescb[:], onescb_h),
            (onesrf[0:1, :], rows_h[7:8, 0:128]),
            (rowsb[0:1, :], rows_h.rearrange("r d -> (r d)")[None, :]),
            (gwf[:].rearrange("p (kb e) -> p kb e", kb=NKB),
             gwT_h.rearrange("(kb p) e -> p kb e", p=128)),
        ]:
            dinc("SETL", sy.dma_start(out=dst, in_=src))
    @block.vector
    def _(v):
        W(v, "SETL", cur("SETL"))
        CE, SE, SG, WQ, WQS, WK, WKS = [rowsb[0:1, i * D:(i + 1) * D] for i in range(7)]
        TMP = rwork[0:1, 4 * D:5 * D]
        v.tensor_mul(rwork[0:1, 0:D], CE, WQ)
        fence(v, v.tensor_mul(TMP, SE, SG))
        v.tensor_mul(rwork[0:1, D:2 * D], TMP, WQS)
        v.tensor_mul(rwork[0:1, 2 * D:3 * D], CE, WK)
        v.tensor_mul(rwork[0:1, 3 * D:4 * D], TMP, WKS)
        fence(v, v.tensor_copy(ghi[:], gwf[:]))
        fence(v, v.tensor_sub(glof[:], gwf[:], ghi[:]))
        inc("SETV", v.tensor_copy(glo[:], glof[:]))
    @block.gpsimd
    def _(g):
        W(g, "SETV", cur("SETV"))
        dinc("SETG0", g.dma_start(out=rsc_h.rearrange("r d -> (r d)")[None, :], in_=rwork[0:1, 0:4 * D]))
        W(g, "SETG0", 16)
        for pr2 in range(8):
            for row, dst, off in ((0, ropeCW, 0), (2, ropeCW, 256), (1, ropeSW, 0), (3, ropeSW, 256)):
                dinc("SETG", g.dma_start(
                    out=dst[:, pr2 * 512 + off:pr2 * 512 + off + 256],
                    in_=rsc_h[row:row + 1, pr2 * 256:(pr2 + 1) * 256].to_broadcast([128, 256])))
    setup_ps.close()

    if phases >= 1:
        with ExitStack() as ps:
            psb = lambda name, shape, dty: ps.enter_context(nc.sbuf_tensor(name, shape, dty))
            wvsb = psb("wvsb", [128, NKB * D], BF16)
            xf = [psb(f"p1xf{i}", [128, D], F32) for i in range(2)]
            xb = [psb(f"p1xb{i}", [128, D], BF16) for i in range(2)]
            xtc = [psb(f"p1xtc{i}", [128, D], BF16) for i in range(2)]
            vst = [psb(f"p1vst{i}", [128, D], BF16) for i in range(2)]
            ptr = ps.enter_context(nc.psum_tensor("p1ptr", [128, 2048], BF16))
            pv = ps.enter_context(nc.psum_tensor("p1pv", [128, 2048], F32))

            @block.gpsimd
            def _(g):
                W(g, "SETL", cur("SETL"))
                for c in range(4):
                    dinc("WVL", g.dma_start(
                        out=wvsb[:].rearrange("p (kb n) -> p kb n", kb=NKB)[:, c * 4:(c + 1) * 4, :],
                        in_=wvT_h.rearrange("(kb p) n -> p kb n", p=128)[:, c * 4:(c + 1) * 4, :]))
                if "d_wvsb" in dump_h:
                    W(g, "P1SV", 16 * NT)
                    dinc("DMPW", g.dma_start(out=dump_h["d_wvsb"], in_=wvsb[:]))
                    W(g, "DMPW", 16)
            @block.sync
            def _(sy):
                W(sy, "SETG", 16 * 32)
                for t in range(NT):
                    if t >= 2:
                        W(sy, "P1XF", t - 1)
                    dinc("P1LX", sy.dma_start(out=xf[t % 2][:], in_=x_h[t * 128:(t + 1) * 128, :]))
            @block.vector
            def _(v):
                for t in range(NT):
                    b = t % 2
                    W(v, "P1LX", 16 * (t + 1))
                    if t >= 2:
                        W(v, "P1TR", t - 1)
                    inc("P1XF", v.tensor_copy(xb[b][:], xf[b][:]))
                    W(v, "P1TR", t + 1)
                    if t >= 2:
                        W(v, "P1SX", 16 * (t - 1))
                    inc("P1EV", v.tensor_copy(xtc[b][:], ptr[:]))
            @block.tensor
            def _(te):
                W(te, "WVL", 16 * 4)
                for t in range(NT):
                    b = t % 2
                    W(te, "P1XF", t + 1)
                    if t >= 1:
                        W(te, "P1EV", t)
                    for k in range(NKB):
                        instr = te.transpose(ptr[:, k * 128:(k + 1) * 128], xb[b][:, k * 128:(k + 1) * 128], idb[:])
                    inc("P1TR", instr)
                    W(te, "P1EV", t + 1)
                    if t >= 1:
                        W(te, "P1VE", t)
                    for d in range(4):
                        for k in range(NKB):
                            instr = te.matmul(pv[:, d * 512:(d + 1) * 512],
                                              xtc[b][:, k * 128:(k + 1) * 128],
                                              wvsb[:, k * D + d * 512:k * D + d * 512 + 512],
                                              start=(k == 0), stop=(k == NKB - 1),
                                              skip_group_check=True)
                    inc("P1MM", instr)
            @block.scalar
            def _(sc):
                for t in range(NT):
                    b = t % 2
                    W(sc, "P1MM", t + 1)
                    if t >= 2:
                        W(sc, "P1SV", 16 * (t - 1))
                    inc("P1VE", sc.copy(vst[b][:], pv[:]))
                    W(sc, "P1VE", t + 1)
                    dinc("P1SV", sc.dma_start(out=vtok_h[t * 128:(t + 1) * 128, :], in_=vst[b][:]))
                    W(sc, "P1EV", t + 1)
                    dinc("P1SX", sc.dma_start(
                        out=xT_h[:, t * 128:(t + 1) * 128].rearrange("(kb p) c -> p kb c", p=128),
                        in_=xtc[b][:].rearrange("p (kb c) -> p kb c", kb=NKB)))
            barrier()

    if phases >= 2:
        pre = [es.enter_context(nc.sbuf_tensor(f"preb{i}", [128, 2048], BF16)) for i in range(2)]
        @block.gpsimd
        def _(g):
            W(g, "SETL", cur("SETL"))
            nbuf = [0]
            def stage(dst_ap, src_ap, nelem, view=None):
                b = nbuf[0] % 2
                if nbuf[0] >= 2:
                    W(g, "PRES", 16 * (nbuf[0] - 1))
                dinc("PREL", g.dma_start(out=pre[b][:, 0:nelem], in_=src_ap))
                W(g, "PREL", cur("PREL"))
                buf = pre[b][:, 0:nelem] if view is None else view(pre[b][:, 0:nelem])
                dinc("PRES", g.dma_start(out=dst_ap, in_=buf))
                nbuf[0] += 1
            for W_h, WB_h in [(wqT_h, wqB_h), (wkT_h, wkB_h), (woT_h, woB_h)]:
                for c in range(16):
                    stage(WB_h.rearrange("(kb p) n -> p kb n", p=128)[:, c:c + 1, :],
                          W_h.rearrange("(kb p) n -> p kb n", p=128)[:, c:c + 1, :],
                          D)
            for W_h, WB_h in [(wgT_h, wgB_h), (wuT_h, wuB_h)]:
                for e in range(E):
                    for kb in range(NKB):
                        stage(WB_h[e, :, :, kb * 128:(kb + 1) * 128].rearrange("m p c -> p m c"),
                              W_h[e, kb * 128:(kb + 1) * 128, :],
                              MD,
                              view=lambda bb: bb.rearrange("p (m c) -> p m c", m=NMB))
            for e in range(E):
                for m in range(NMB):
                    stage(wdB_h[e, m * 128:(m + 1) * 128, :],
                          wdT_h[e, m * 128:(m + 1) * 128, :],
                          D)

    if phases >= 2 and p2sub >= 1:
        with ExitStack() as ps:
            psb = lambda name, shape, dty: ps.enter_context(nc.sbuf_tensor(name, shape, dty))
            wq_sl = [psb(f"wqsl{i}", [128, NKB * 256], BF16) for i in range(2)]
            wk_sl = [psb(f"wksl{i}", [128, NKB * 256], BF16) for i in range(2)]
            xtc2 = [psb(f"p2xtc{i}", [128, D], BF16) for i in range(2)]
            qkTp = [psb(f"qkTp{i}", [128, 4 * T], BF16) for i in range(2)]
            vhb = [psb(f"vhb{i}", [128, T], BF16) for i in range(2)]
            ostg = [psb(f"ostg{i}", [128, T], BF16) for i in range(2)]
            pqs = [psb(f"pqs{i}", [128, 512], F32) for i in range(2)]
            qrs = psb("qrs", [128, 512], F32)
            jro = psb("jro", [128, 512], F32)
            sqv = jro
            jro2 = psb("jro2", [128, 512], F32)
            qbb = [psb(f"qbb{i}", [128, 512], BF16) for i in range(2)]
            msd = psb("msd", [128, 16], F32)
            ts1P = [psb(f"ts1{i}", [128, 1024], F32) for i in range(2)]
            ebufP = [psb(f"ebuf{i}", [128, 1024], BF16) for i in range(3)]
            etmpP = psb("etmp", [128, 1024], BF16)
            rdb = psb("rdb", [1, 512], F32)
            rbb = psb("rbb", [128, 512], F32)

            pqk = ps.enter_context(nc.psum_tensor("p2qk", [128, 512], F32))
            pq = pqk[:, 0:256]
            pk = pqk[:, 256:512]
            ptr2_full = ps.enter_context(nc.psum_tensor("p2tr", [128, 1024], BF16))
            ptr2 = ptr2_full[:, 0:512]
            pssP = [ps.enter_context(nc.psum_tensor(f"p2sp{i}", [128, 1024], F32)) for i in range(2)]
            pso = ps.enter_context(nc.psum_tensor("p2o", [128, 512], F32))
            psd = ps.enter_context(nc.psum_tensor("p2d", [128, 512], F32))

            gpi = [0]

            def emit_loads(pr):
                pb = pr % 2
                @block.sync
                def _(sy):
                    if pr == 0:
                        W(sy, "PRES", 16 * 32)
                        W(sy, "P1SX", 16 * NT)
                        W(sy, "P1SV", 16 * NT)
                    if pr >= 2:
                        W(sy, "P2TE", ntt * (pr - 1))
                    dinc("WQL", sy.dma_start(
                        out=wq_sl[pb][:].rearrange("p (kb c) -> p kb c", kb=NKB),
                        in_=wqB_h[:, pr * 256:(pr + 1) * 256].rearrange("(kb p) c -> p kb c", p=128)))
                    dinc("WQL", sy.dma_start(
                        out=wk_sl[pb][:].rearrange("p (kb c) -> p kb c", kb=NKB),
                        in_=wkB_h[:, pr * 256:(pr + 1) * 256].rearrange("(kb p) c -> p kb c", p=128)))

            def emit_xload(pr, t):
                @block.sync
                def _(sy):
                    need = pr * ntt + t
                    if need >= 2:
                        W(sy, "P2MMQ", need - 1)
                    dinc("P2XL", sy.dma_start(
                        out=xtc2[need % 2][:].rearrange("p (kb c) -> p kb c", kb=NKB),
                        in_=xT_h[:, t * 128:(t + 1) * 128].rearrange("(kb p) c -> p kb c", p=128)))

            def emit_prep_tile(pr, t):
                pb = pr % 2
                gi = pr * ntt + t
                xb3 = gi % 2
                @block.tensor
                def _(te):
                    W(te, "P2XL", 16 * (gi + 1))
                    W(te, "WQL", 32 * (pr + 1))
                    if gi >= 1:
                        W(te, "P2CP", gi)
                    for k in range(NKB):
                        te.matmul(pq, xtc2[xb3][:, k * 128:(k + 1) * 128],
                                  wq_sl[pb][:, k * 256:(k + 1) * 256],
                                  start=(k == 0), stop=(k == NKB - 1), skip_group_check=True)
                    for k in range(NKB):
                        instr = te.matmul(pk, xtc2[xb3][:, k * 128:(k + 1) * 128],
                                          wk_sl[pb][:, k * 256:(k + 1) * 256],
                                          start=(k == 0), stop=(k == NKB - 1), skip_group_check=True)
                    inc("P2MMQ", instr)
                @block.vector
                def _(v):
                    W(v, "P2MMQ", gi + 1)
                    if gi >= 2:
                        W(v, "P2RC", gi - 1)
                    inc("P2CP", v.tensor_copy(pqs[gi % 2][:], pqk[:]))
                    W(v, "P2CP", cur("P2CP"))
                    fence(v, v.tensor_mul(sqv[:], pqs[gi % 2][:], pqs[gi % 2][:]))
                    for s2 in range(4):
                        instr = v.tensor_reduce(msd[:, s2:s2 + 1], sqv[:, s2 * 128:(s2 + 1) * 128],
                                                AX.X, ALU.add)
                    fence(v, instr)
                    fence(v, v.tensor_scalar(msd[:, 4:8], msd[:, 0:4], 128.0 * 1e-6, None, ALU.add))
                    inc("P2MS", v.reciprocal(msd[:, 8:12], msd[:, 4:8]))
                @block.scalar
                def _(sc):
                    W(sc, "P2MS", gi + 1)
                    if gi >= 1:
                        W(sc, "P2RC", gi)
                    inc("P2RS", sc.activation(msd[:, 12:16], msd[:, 8:12], AF.Sqrt, scale=128.0))
                @block.vector
                def _(v):
                    W(v, "P2RS", gi + 1)
                    if gi >= 2:
                        W(v, "P2TR", gi - 1)
                    for s2 in range(4):
                        instr = v.tensor_scalar(qrs[:, s2 * 128:(s2 + 1) * 128],
                                                pqs[gi % 2][:, s2 * 128:(s2 + 1) * 128],
                                                msd[:, 12 + s2:13 + s2], None, ALU.mult)
                    fence(v, instr)
                    CW = ropeCW[:, pr * 512:(pr + 1) * 512]
                    SW = ropeSW[:, pr * 512:(pr + 1) * 512]
                    q3 = qrs[:].rearrange("p (n two) -> p n two", two=2)
                    j3 = jro2[:].rearrange("p (n two) -> p n two", two=2)
                    s3 = SW.rearrange("p (n two) -> p n two", two=2)
                    v.tensor_mul(jro[:], qrs[:], CW)
                    v.tensor_mul(j3[:, :, 0:1], q3[:, :, 1:2], s3[:, :, 0:1])
                    fence(v, v.tensor_mul(j3[:, :, 1:2], q3[:, :, 0:1], s3[:, :, 1:2]))
                    inc("P2RC", v.tensor_add(qbb[gi % 2][:], jro[:], jro2[:]))

            def emit_prep_tr(pr, t):
                pb = pr % 2
                gi = pr * ntt + t
                @block.tensor
                def _(te):
                    W(te, "P2RC", gi + 1)
                    if gi >= 1:
                        W(te, "P2TE", gi)
                    for s2 in range(4):
                        instr = te.transpose(ptr2_full[:, s2 * 128:(s2 + 1) * 128],
                                             qbb[gi % 2][:, s2 * 128:(s2 + 1) * 128], idb[:])
                    inc("P2TR", instr)
                @block.vector
                def _(v):
                    W(v, "P2TR", gi + 1)
                    if t == 0 and pr >= 2 and p2sub >= 2:
                        W(v, "TAV", (2 * pr - 2) * 72)
                    instr = v.tensor_copy(
                        qkTp[pb][:].rearrange("p (g tt) -> p g tt", g=4)[:, :, t * 128:(t + 1) * 128],
                        ptr2_full[:, 0:512].rearrange("p (g c) -> p g c", g=4))
                    inc("P2TE", instr)

            def emit_attn_tile(pr, hl, n):
                pb = pr % 2
                h = 2 * pr + hl
                QOF = hl * T
                KOF = 2 * T + hl * T
                ni = 4 * n + 4
                np2 = ni // 2
                hn = h * NN + n
                ap0 = gpi[0]
                tanh_fence = {}

                def emit_mms(pi):
                    ap = ap0 + pi
                    @block.tensor
                    def _(te):
                        if pi == 0:
                            W(te, "P2TE", ntt * (pr + 1))
                            W(te, "VHL", 16 * (h + 1))
                            if hn >= 1:
                                W(te, "OE", hn)
                        if ap >= 2:
                            W(te, "FEN_SC_TANH", ap - 1)
                        te.matmul(pssP[ap % 2][:, 0:512],
                                  qkTp[pb][:, KOF + 2 * pi * 128:KOF + (2 * pi + 1) * 128],
                                  qkTp[pb][:, QOF + n * 512:QOF + (n + 1) * 512],
                                  start=True, stop=True, skip_group_check=True)
                        instr = te.matmul(pssP[ap % 2][:, 512:1024],
                                          qkTp[pb][:, KOF + (2 * pi + 1) * 128:KOF + (2 * pi + 2) * 128],
                                          qkTp[pb][:, QOF + n * 512:QOF + (n + 1) * 512],
                                          start=True, stop=True, skip_group_check=True)
                        inc("TS", instr)

                def emit_tanh(pi):
                    ap = ap0 + pi
                    @block.scalar
                    def _(sc):
                        W(sc, "TS", ap + 1)
                        instr = sc.activation(ts1P[ap % 2][:], pssP[ap % 2][:], AF.Tanh, scale=TANH_SC)
                        inc("FEN_SC_TANH", instr)
                        tanh_fence[pi] = cur("FEN_SC_TANH")

                def emit_exp(pi):
                    ap = ap0 + pi
                    diag = pi >= 2 * n
                    @block.scalar
                    def _(sc):
                        W(sc, "FEN_SC_TANH", tanh_fence[pi])
                        if ap >= 3:
                            W(sc, "TAV", ap - 2)
                        if diag:
                            W(sc, "VM", cur("VM"))
                        instr = sc.activation(etmpP[:] if diag else ebufP[ap % 3][:],
                                              ts1P[ap % 2][:], AF.Exp, scale=EXP_SC)
                        inc("TH", instr)
                    if diag:
                        @block.vector
                        def _(v):
                            W(v, "TH", ap + 1)
                            for half in range(2):
                                off_idx = 2 * pi - 4 * n + half
                                inc("VM", v.tensor_mul(
                                    ebufP[ap % 3][:, half * 512:(half + 1) * 512],
                                    etmpP[:, half * 512:(half + 1) * 512],
                                    maskt[:, off_idx * 512:(off_idx + 1) * 512]))

                def emit_dav(pi):
                    ap = ap0 + pi
                    diag = pi >= 2 * n
                    @block.tensor
                    def _(te):
                        W(te, "TH", ap + 1)
                        if diag:
                            W(te, "VM", cur("VM"))
                        for half in range(2):
                            i = 2 * pi + half
                            te.matmul(psd[0:1, :], onescb[:],
                                      ebufP[ap % 3][:, half * 512:(half + 1) * 512],
                                      start=(i == 0), stop=(i == ni - 1), skip_group_check=True)
                            instr = te.matmul(pso[:],
                                              vhb[h % 2][:, i * 128:(i + 1) * 128],
                                              ebufP[ap % 3][:, half * 512:(half + 1) * 512],
                                              start=(i == 0), stop=(i == ni - 1), skip_group_check=True)
                        inc("TAV", instr)

                done = [0]
                for r in range(0, np2, 2):
                    emit_mms(r)
                    if r + 1 < np2:
                        emit_mms(r + 1)
                    for q in (r - 2, r - 1):
                        if 0 <= q:
                            emit_exp(q)
                            emit_dav(q)
                            done[0] = q + 1
                    emit_tanh(r)
                    if r + 1 < np2:
                        emit_tanh(r + 1)
                for q in range(done[0], np2):
                    emit_exp(q)
                    emit_dav(q)
                gpi[0] += np2

                @block.vector
                def _(v):
                    W(v, "TAV", gpi[0])
                    inc("RD", v.reciprocal(rdb[0:1, :], psd[0:1, :]))
                @block.tensor
                def _(te):
                    W(te, "RD", hn + 1)
                    instr = te.matmul(pssP[gpi[0] % 2][:, 0:512], onesrf[0:1, :], rdb[0:1, :],
                                      start=True, stop=True, skip_group_check=True)
                    inc("TB", instr)
                @block.vector
                def _(v):
                    W(v, "TB", hn + 1)
                    if n == 0 and h >= 2:
                        W(v, "AOS", 16 * (h - 1))
                    fence(v, v.tensor_copy(rbb[:], pssP[gpi[0] % 2][:, 0:512]))
                    instr = v.tensor_mul(ostg[hl][:, n * 512:(n + 1) * 512], pso[:], rbb[:])
                    inc("OE", instr)

            def emit_vload(h):
                @block.sync
                def _(sy):
                    if h >= 2:
                        W(sy, "VHU", h - 1)
                    dinc("VHL", sy.dma_start(
                        out=vhb[h % 2][:].rearrange("p (s c) -> p s c", s=32),
                        in_=vtok_h[:, h * 128:(h + 1) * 128].rearrange("(s p) c -> p s c", p=128)))

            tr_pend = []
            def push_prep(pr, t):
                emit_xload(pr, t)
                emit_prep_tile(pr, t)
                tr_pend.append((pr, t))
                while len(tr_pend) > 2:
                    emit_prep_tr(*tr_pend.pop(0))
            def flush_tr():
                while tr_pend:
                    emit_prep_tr(*tr_pend.pop(0))
            emit_loads(0)
            for t in range(ntt):
                push_prep(0, t)
            flush_tr()
            for pr in range(npr):
                if pr + 1 < npr:
                    emit_loads(pr + 1)
                natt = 0
                for hl in (range(2) if p2sub >= 2 else []):
                    h = 2 * pr + hl
                    emit_vload(h)
                    for n in range(NN):
                        emit_attn_tile(pr, hl, n)
                        if pr + 1 < npr:
                            for t in range(2 * natt, min(2 * natt + 2, ntt)):
                                push_prep(pr + 1, t)
                        natt += 1
                    @block.tensor
                    def _(te):
                        W(te, "OE", (h + 1) * NN)
                        inc("VHU", te.nop())
                    @block.scalar
                    def _(sc):
                        W(sc, "OE", (h + 1) * NN)
                        dinc("AOS", sc.dma_start(out=aoT_h[h * 128:(h + 1) * 128, :], in_=ostg[hl][:]))
                if p2sub < 2 and pr + 1 < npr:
                    for t in range(ntt):
                        push_prep(pr + 1, t)
                flush_tr()
            barrier()

    if phases >= 3:
        with ExitStack() as ps:
            psb = lambda name, shape, dty: ps.enter_context(nc.sbuf_tensor(name, shape, dty))
            wosb = psb("wosb", [128, NKB * D], BF16)
            aot = [psb(f"aot{i}", [128, D], BF16) for i in range(2)]
            xres = [psb(f"xres{i}", [128, D], F32) for i in range(2)]
            sres = psb("sres", [128, D], F32)
            junk3 = sres
            stats = psb("stats", [128, 8], F32)
            dscr = psb("dscr", [128, D], F32)
            h1t = [psb(f"h1t{i}", [128, D], F32) for i in range(2)]
            hib = psb("hib", [128, D], BF16)
            lob = psb("lob", [128, D], BF16)
            hTst0 = psb("hTst0", [128, NKB * 512], BF16)
            hTst = [hTst0, hTst0]
            loTst = psb("loTst", [128, NKB * 512], BF16)
            lgsb = psb("lgsb", [128, 512], F32)
            ltok = psb("ltok", [128, 16], F32)
            gsc = psb("gsc", [128, 64], F32)

            po3 = [ps.enter_context(nc.psum_tensor(f"p3o{i}", [128, 512], F32)) for i in range(2)]
            ptrh = ps.enter_context(nc.psum_tensor("p3trh", [128, 2048], BF16))
            ptrl = ps.enter_context(nc.psum_tensor("p3trl", [128, 2048], BF16))
            plg = ps.enter_context(nc.psum_tensor("p3lg", [128, 512], F32))
            ptt = ps.enter_context(nc.psum_tensor("p3tt", [128, 128], F32))

            @block.vector
            def _(v):
                inc("LGZ", v.memset(lgsb[:], 0.0))
            @block.sync
            def _(sy):
                W(sy, "PRES", 16 * 48)
                W(sy, "AOS", 16 * H)
                for c in range(4):
                    dinc("WOL", sy.dma_start(
                        out=wosb[:].rearrange("p (kb n) -> p kb n", kb=NKB)[:, c * 4:(c + 1) * 4, :],
                        in_=woB_h.rearrange("(kb p) n -> p kb n", p=128)[:, c * 4:(c + 1) * 4, :]))
                for t in range(nt3):
                    b = t % 2
                    if t >= 2:
                        W(sy, "P3MM", 4 * (t - 1))
                        W(sy, "P3SE", 4 * (t - 1))
                    dinc("P3AL", sy.dma_start(
                        out=aot[b][:].rearrange("p (kb c) -> p kb c", kb=NKB),
                        in_=aoT_h[:, t * 128:(t + 1) * 128].rearrange("(kb p) c -> p kb c", p=128)))
                    dinc("P3XL", sy.dma_start(out=xres[b][:], in_=x_h[t * 128:(t + 1) * 128, :]))

            for t in range(nt3):
                b = t % 2
                n = t // 4
                nb = n % 2
                @block.tensor
                def _(te):
                    W(te, "P3AL", 16 * (t + 1))
                    if t == 0:
                        W(te, "WOL", 16 * 4)
                    for d in range(4):
                        if 4 * t + d >= 2:
                            W(te, "P3SE", 4 * t + d - 1)
                        if p3x >= 2:
                            inc("P3MM", te.nop())
                            continue
                        for k in range(NKB):
                            instr = te.matmul(po3[d % 2][:],
                                              aot[b][:, k * 128:(k + 1) * 128],
                                              wosb[:, k * D + d * 512:k * D + d * 512 + 512],
                                              start=(k == 0), stop=(k == NKB - 1))
                        inc("P3MM", instr)
                @block.vector
                def _(v):
                    W(v, "P3XL", 16 * (t + 1))
                    for d in range(4):
                        W(v, "P3MM", 4 * t + d + 1)
                        if p3x >= 2:
                            inc("P3SE", v.tensor_copy(sres[:, d * 512:(d + 1) * 512],
                                                      xres[b][:, d * 512:(d + 1) * 512]))
                        else:
                            inc("P3SE", v.tensor_add(sres[:, d * 512:(d + 1) * 512],
                                                     xres[b][:, d * 512:(d + 1) * 512], po3[d % 2][:]))
                    if p3x >= 1:
                        inc("P3LN", v.tensor_copy(dscr[:], sres[:]))
                    else:
                        fence(v, v.tensor_reduce(stats[:, 0:1], sres[:], AX.X, ALU.add))
                        fence(v, v.tensor_scalar(stats[:, 1:2], stats[:, 0:1], 1.0 / D, None, ALU.mult))
                        inc("P3LN", v.tensor_scalar(dscr[:], sres[:], stats[:, 1:2], None, ALU.subtract))
                @block.scalar
                def _(sc):
                    W(sc, "P3LN", t + 1)
                    if p3x >= 1:
                        inc("P3SQ", sc.nop())
                    else:
                        inc("P3SQ", sc.activation(junk3[:], dscr[:], AF.Square, accum_out=stats[:, 2:3]))
                @block.vector
                def _(v):
                    W(v, "P3SQ", t + 1)
                    if p3x >= 1:
                        inc("P3RC", v.engine_nop())
                    else:
                        fence(v, v.tensor_scalar(stats[:, 3:4], stats[:, 2:3], float(D) * 1e-5, None, ALU.add))
                        inc("P3RC", v.reciprocal(stats[:, 4:5], stats[:, 3:4]))
                @block.scalar
                def _(sc):
                    W(sc, "P3RC", t + 1)
                    if p3x >= 1:
                        inc("P3SR", sc.nop())
                    else:
                        inc("P3SR", sc.activation(stats[:, 5:6], stats[:, 4:5], AF.Sqrt, scale=float(D)))
                @block.vector
                def _(v):
                    W(v, "P3SR", t + 1)
                    if t >= 2:
                        W(v, "P3H1S", 16 * (t - 1))
                    if p3x >= 1:
                        inc("P3HD", v.tensor_copy(h1t[b][:], dscr[:]))
                    else:
                        inc("P3HD", v.tensor_scalar(h1t[b][:], dscr[:], stats[:, 5:6], None, ALU.mult))
                    W(v, "P3HD", cur("P3HD"))
                    if t >= 1 and p3sub >= 2:
                        W(v, "P3TR", t)
                    fence(v, v.tensor_copy(hib[:], h1t[b][:]))
                    inc("P3HL", v.tensor_sub(lob[:], h1t[b][:], hib[:]))
                @block.scalar
                def _(sc):
                    W(sc, "P3HD", t + 1)
                    dinc("P3H1S", sc.dma_start(out=h1_h[t * 128:(t + 1) * 128, :], in_=h1t[b][:]))
                if p3sub < 2:
                    continue
                @block.tensor
                def _(te):
                    W(te, "P3HL", t + 1)
                    if t >= 1:
                        W(te, "P3TE", 2 * t)
                    for k in range(NKB):
                        te.transpose(ptrh[:, k * 128:(k + 1) * 128], hib[:, k * 128:(k + 1) * 128], idb[:])
                        instr = te.transpose(ptrl[:, k * 128:(k + 1) * 128], lob[:, k * 128:(k + 1) * 128], idb[:])
                    inc("P3TR", instr)
                @block.vector
                def _(v):
                    W(v, "P3TR", t + 1)
                    if t % 4 == 0 and n >= 1 and p3sub >= 3:
                        W(v, "P3GD", n)
                        W(v, "P3HTS", 16 * n)
                        W(v, "P3GM", n)
                    v.tensor_copy(
                        hTst[nb][:].rearrange("p (kb c) -> p kb c", kb=NKB)[:, :, (t % 4) * 128:(t % 4 + 1) * 128],
                        ptrh[:].rearrange("p (kb c) -> p kb c", kb=NKB))
                    instr = v.tensor_copy(
                        loTst[:].rearrange("p (kb c) -> p kb c", kb=NKB)[:, :, (t % 4) * 128:(t % 4 + 1) * 128],
                        ptrl[:].rearrange("p (kb c) -> p kb c", kb=NKB))
                    inc("P3TE", instr, 2)
                if t % 4 == 3 and p3sub >= 3:
                    @block.tensor
                    def _(te):
                        W(te, "P3TE", 2 * (t + 1))
                        if n == 0:
                            W(te, "SETV", cur("SETV"))
                        if n >= 1:
                            W(te, "P3GE", n)
                        nmm = 0
                        for k in range(NKB):
                            for (lw, rh) in [(ghi, hTst[nb]), (glo, hTst[nb]), (ghi, loTst)]:
                                nmm += 1
                                instr = te.matmul(plg[0:16, :], lw[:, k * 16:(k + 1) * 16],
                                                  rh[:, k * 512:(k + 1) * 512],
                                                  start=(nmm == 1), stop=(nmm == 3 * NKB))
                        inc("P3GM", instr)
                    @block.vector
                    def _(v):
                        W(v, "P3GM", n + 1)
                        W(v, "LGZ", 1)
                        if n >= 1:
                            W(v, "P3TT", 4 * n)
                        inc("P3GE", v.tensor_copy(lgsb[0:16, :], plg[0:16, :]))
                    for c in range(4):
                        @block.tensor
                        def _(te):
                            W(te, "P3GE", n + 1)
                            if 4 * n + c >= 1:
                                W(te, "P3TTE", 4 * n + c)
                            inc("P3TT", te.transpose(ptt[:], lgsb[:, c * 128:(c + 1) * 128], idf[:]))
                        @block.vector
                        def _(v):
                            W(v, "P3TT", 4 * n + c + 1)
                            tt = n * 4 + c
                            if tt >= 1:
                                W(v, "P3EX", tt)
                            inc("P3TTE", v.tensor_copy(ltok[:], ptt[:, 0:16]))
                            W(v, "P3TTE", cur("P3TTE"))
                            fence(v, v.max(gsc[:, 0:8], ltok[:]))
                            inc("P3M8", v.tensor_scalar(gsc[:, 8:24], ltok[:], gsc[:, 5:6], None, ALU.is_ge))
                        @block.scalar
                        def _(sc):
                            W(sc, "P3M8", 4 * n + c + 1)
                            inc("P3EX", sc.activation(gsc[:, 24:40], ltok[:], AF.Exp))
                        @block.vector
                        def _(v):
                            W(v, "P3EX", 4 * n + c + 1)
                            tt = n * 4 + c
                            fence(v, v.tensor_reduce(gsc[:, 40:41], gsc[:, 24:40], AX.X, ALU.add))
                            fence(v, v.reciprocal(gsc[:, 41:42], gsc[:, 40:41]))
                            fence(v, v.tensor_scalar(gsc[:, 42:58], gsc[:, 24:40], gsc[:, 41:42], None, ALU.mult))
                            instr = v.tensor_mul(cwsb[:, tt * 16:(tt + 1) * 16], gsc[:, 42:58], gsc[:, 8:24])
                            inc("P3CW", instr)
                            if c == 3:
                                inc("P3GD", v.engine_nop())
                    @block.scalar
                    def _(sc):
                        W(sc, "P3TE", 2 * (t + 1))
                        dinc("P3HTS", sc.dma_start(
                            out=hT_h[:, n * 512:(n + 1) * 512].rearrange("(kb p) c -> p kb c", p=128),
                            in_=hTst[nb][:].rearrange("p (kb c) -> p kb c", kb=NKB)))
            barrier()

    if phases >= 5:
        with ExitStack() as ps:
            psb = lambda name, shape, dty: ps.enter_context(nc.sbuf_tensor(name, shape, dty))
            wdsb = psb("wdsb", [128, NMB * D], BF16)
            wgsl = [psb(f"wgsl{i}", [128, D], BF16) for i in range(2)]
            wusl = [psb(f"wusl{i}", [128, D], BF16) for i in range(2)]
            hTn = [psb(f"hTn{i}", [128, NKB * 512], BF16) for i in range(2)]
            aTn = [psb(f"aTn{i}", [128, NMB * 512], BF16) for i in range(2)]
            sg5 = psb("sg5", [128, 512], F32)
            ysb = [psb(f"ysb{i}", [128, D], F32) for i in range(2)]

            pg5 = [ps.enter_context(nc.psum_tensor(f"p5g{i}", [128, 512], F32)) for i in range(2)]
            pu5 = [ps.enter_context(nc.psum_tensor(f"p5u{i}", [128, 512], F32)) for i in range(2)]
            py5 = [ps.enter_context(nc.psum_tensor(f"p5y{i}", [128, 512], F32)) for i in range(2)]

            nws = [0]
            nyt = [0]
            for e in range(E):
                @block.sync
                def _(sy):
                    if e == 0:
                        W(sy, "PRES", cur("PRES"))
                        W(sy, "P3HTS", 16 * NN)
                        W(sy, "P3GD", NN)
                        W(sy, "P3H1S", 16 * NT)
                    if e >= 1:
                        W(sy, "P5YM", e * NN * 16)
                    dinc("P5WD", sy.dma_start(
                        out=wdsb[:].rearrange("p (mb c) -> p mb c", mb=NMB),
                        in_=wdB_h[e].rearrange("(mb p) c -> p mb c", p=128)))
                for n in range(NN):
                    gn = e * NN + n
                    hb = gn % 2
                    @block.sync
                    def _(sy):
                        if gn >= 2:
                            W(sy, "P5GU", (gn - 1) * NMB)
                        dinc("P5HL", sy.dma_start(
                            out=hTn[hb][:].rearrange("p (kb c) -> p kb c", kb=NKB),
                            in_=hT_h[:, n * 512:(n + 1) * 512].rearrange("(kb p) c -> p kb c", p=128)))
                        for m in range(NMB):
                            wb = nws[0] % 2
                            if nws[0] >= 2:
                                W(sy, "P5GU", nws[0] - 1)
                            dinc("P5WL", sy.dma_start(out=wgsl[wb][:], in_=wgB_h[e, m]))
                            dinc("P5WL", sy.dma_start(out=wusl[wb][:], in_=wuB_h[e, m]))
                            nws[0] += 1
                    for m in range(NMB):
                        gm = gn * NMB + m
                        wb = gm % 2
                        gb = gm % 2
                        @block.tensor
                        def _(te):
                            W(te, "P5HL", 16 * (gn + 1))
                            W(te, "P5WL", 32 * (gm + 1))
                            if gm >= 2:
                                W(te, "P5AE", gm - 1)
                            for k in range(NKB):
                                te.matmul(pg5[gb][:], wgsl[wb][:, k * 128:(k + 1) * 128],
                                          hTn[hb][:, k * 512:(k + 1) * 512],
                                          start=(k == 0), stop=(k == NKB - 1),
                                          skip_group_check=True)
                                instr = te.matmul(pu5[gb][:], wusl[wb][:, k * 128:(k + 1) * 128],
                                                  hTn[hb][:, k * 512:(k + 1) * 512],
                                                  start=(k == 0), stop=(k == NKB - 1),
                                                  skip_group_check=True)
                            inc("P5GU", instr)
                        @block.scalar
                        def _(sc):
                            W(sc, "P5GU", gm + 1)
                            if gm >= 1:
                                W(sc, "P5AE", gm)
                            inc("P5SI", sc.activation(sg5[:], pg5[gb][:], AF.Silu))
                        @block.vector
                        def _(v):
                            W(v, "P5SI", gm + 1)
                            if gn >= 2 and m == 0:
                                W(v, "P5YM", (gn - 1) * 16)
                            inc("P5AE", v.tensor_mul(aTn[gn % 2][:, m * 512:(m + 1) * 512], sg5[:], pu5[gb][:]))
                    for tj in range(4):
                        yb = (4 * gn + tj) % 2
                        for dd in range(4):
                            yp = nyt[0] % 2
                            @block.tensor
                            def _(te):
                                W(te, "P5AE", (gn + 1) * NMB)
                                if n == 0 and tj == 0 and dd == 0:
                                    W(te, "P5WD", 16 * (e + 1))
                                if nyt[0] >= 2:
                                    W(te, "P5YE", nyt[0] - 1)
                                for m in range(NMB):
                                    instr = te.matmul(py5[yp][:],
                                                      aTn[gn % 2][:, m * 512 + tj * 128:m * 512 + (tj + 1) * 128],
                                                      wdsb[:, m * D + dd * 512:m * D + dd * 512 + 512],
                                                      start=(m == 0), stop=(m == NMB - 1))
                                inc("P5YM", instr)
                            @block.vector
                            def _(v):
                                W(v, "P5YM", nyt[0] + 1)
                                if nyt[0] >= 8:
                                    W(v, "P5YS", 16 * (nyt[0] // 4 - 1))
                                tt = n * 4 + tj
                                inc("P5YE", v.tensor_scalar(ysb[yb][:, dd * 512:(dd + 1) * 512], py5[yp][:],
                                                            cwsb[:, tt * 16 + e:tt * 16 + e + 1], None, ALU.mult))
                            nyt[0] += 1
                        @block.gpsimd
                        def _(g):
                            W(g, "P5YE", nyt[0])
                            tt = n * 4 + tj
                            if e == 0:
                                instr = g.dma_start(out=yac_h[tt * 128:(tt + 1) * 128, :], in_=ysb[yb][:])
                            else:
                                instr = g.dma_start(out=yac_h[tt * 128:(tt + 1) * 128, :], in_=ysb[yb][:],
                                                    accum_op=ALU.add)
                            dinc("P5YS", instr)
            barrier()

    if phases >= 6:
        with ExitStack() as ps:
            psb = lambda name, shape, dty: ps.enter_context(nc.sbuf_tensor(name, shape, dty))
            yld = [psb(f"yld{i}", [128, D], F32) for i in range(2)]
            hld = [psb(f"hld{i}", [128, D], F32) for i in range(2)]
            s6 = psb("s6", [128, D], F32)
            junk6 = psb("junk6", [128, D], F32)
            st6 = psb("st6", [128, 8], F32)
            dscr6 = psb("dscr6", [128, D], F32)
            ot6 = [psb(f"ot6{i}", [128, D], F32) for i in range(2)]
            @block.sync
            def _(sy):
                W(sy, "P5YS", 16 * NT * E)
                W(sy, "P3H1S", 16 * NT)
                for t in range(NT):
                    b = t % 2
                    if t >= 2:
                        W(sy, "P6A", t - 1)
                    dinc("P6L", sy.dma_start(out=yld[b][:], in_=yac_h[t * 128:(t + 1) * 128, :]))
                    dinc("P6L", sy.dma_start(out=hld[b][:], in_=h1_h[t * 128:(t + 1) * 128, :]))
            for t in range(NT):
                b = t % 2
                @block.vector
                def _(v):
                    W(v, "P6L", 32 * (t + 1))
                    inc("P6A", v.tensor_add(s6[:], yld[b][:], hld[b][:]))
                    W(v, "P6A", cur("P6A"))
                    fence(v, v.tensor_reduce(st6[:, 0:1], s6[:], AX.X, ALU.add))
                    fence(v, v.tensor_scalar(st6[:, 1:2], st6[:, 0:1], 1.0 / D, None, ALU.mult))
                    inc("P6D", v.tensor_scalar(dscr6[:], s6[:], st6[:, 1:2], None, ALU.subtract))
                @block.scalar
                def _(sc):
                    W(sc, "P6D", t + 1)
                    inc("P6SQ", sc.activation(junk6[:], dscr6[:], AF.Square, accum_out=st6[:, 2:3]))
                @block.vector
                def _(v):
                    W(v, "P6SQ", t + 1)
                    fence(v, v.tensor_scalar(st6[:, 3:4], st6[:, 2:3], float(D) * 1e-5, None, ALU.add))
                    inc("P6RC", v.reciprocal(st6[:, 4:5], st6[:, 3:4]))
                @block.scalar
                def _(sc):
                    W(sc, "P6RC", t + 1)
                    inc("P6SR", sc.activation(st6[:, 5:6], st6[:, 4:5], AF.Sqrt, scale=float(D)))
                @block.vector
                def _(v):
                    W(v, "P6SR", t + 1)
                    if t >= 2:
                        W(v, "P6S", 16 * (t - 1))
                    inc("P6N", v.tensor_scalar(ot6[b][:], dscr6[:], st6[:, 5:6], None, ALU.mult))
                @block.scalar
                def _(sc):
                    W(sc, "P6N", t + 1)
                    dinc("P6S", sc.dma_start(out=out_h[t * 128:(t + 1) * 128, :], in_=ot6[b][:]))
            @block.gpsimd
            def _(g):
                W(g, "P6S", 16 * NT)

    if dump:
        barrier()
        @block.sync
        def _(sy):
            pairs = {"d_xT": xT_h, "d_vtok": vtok_h, "d_aoT": aoT_h, "d_h1": h1_h,
                     "d_hT": hT_h, "d_yac": yac_h}
            for name, ap in dump_h.items():
                if name in pairs:
                    dinc("DMP", sy.dma_start(out=ap, in_=pairs[name]))
                elif name == "d_wgB":
                    dinc("DMP", sy.dma_start(out=ap, in_=wgB_h[0]))
        if "d_cw" in dump_h:
            @block.scalar
            def _(sc):
                dinc("DMP", sc.dma_start(out=dump_h["d_cw"], in_=cwsb[:]))
        @block.gpsimd
        def _(g):
            W(g, "DMP", cur("DMP"))

    for nm, val in waits_log:
        if val > cur(nm):
            raise RuntimeError(f"wait on {nm} for {val} but only {cur(nm)} increments emitted")
    es.close()
    return nc


def _host_prep(inp):
    f32 = np.float32
    x = np.ascontiguousarray(np.asarray(inp["L0_attn_x"], f32))
    wqT = np.ascontiguousarray(np.asarray(inp["L0_attn_wq"], f32).T)
    wkT = np.ascontiguousarray(np.asarray(inp["L0_attn_wk"], f32).T)
    wvT = np.ascontiguousarray(np.asarray(inp["L0_attn_wv"], f32).T)
    woT = np.ascontiguousarray(np.asarray(inp["L0_attn_wo"], f32).T)
    gwT = np.ascontiguousarray(np.asarray(inp["L0_ffn_gate_w"], f32).T)
    wgT = np.ascontiguousarray(np.transpose(np.asarray(inp["L0_ffn_w_gate"], f32), (0, 2, 1)))
    wuT = np.ascontiguousarray(np.transpose(np.asarray(inp["L0_ffn_w_up"], f32), (0, 2, 1)))
    wdT = np.ascontiguousarray(np.transpose(np.asarray(inp["L0_ffn_w_down"], f32), (0, 2, 1)))
    cos16 = np.asarray(inp["L0_attn_rope_cos"], f32)[:H]
    sin16 = np.asarray(inp["L0_attn_rope_sin"], f32)[:H]
    qnw = np.asarray(inp["L0_attn_q_norm_w"], f32)
    knw = np.asarray(inp["L0_attn_k_norm_w"], f32)

    cosE = np.repeat(cos16, 2, axis=1).reshape(-1)
    sinE = np.repeat(sin16, 2, axis=1).reshape(-1)
    sign = np.tile(np.array([-1.0, 1.0], f32), D // 2)
    swap = lambda w: np.ascontiguousarray(w.reshape(-1, 2)[:, ::-1]).reshape(-1)
    rows = np.stack([
        cosE, sinE, sign,
        np.tile(qnw, H), np.tile(swap(qnw), H),
        np.tile(knw, H), np.tile(swap(knw), H),
        np.ones(D, f32),
    ]).astype(f32)

    import ml_dtypes
    masks = np.zeros((128, 2048), np.float32)
    rr = np.arange(128)[:, None]
    cc = np.arange(512)[None, :]
    for oi, off in enumerate((0, 128, 256, 384)):
        masks[:, oi * 512:(oi + 1) * 512] = (cc >= rr + off)
    masks = masks.astype(ml_dtypes.bfloat16)
    identb = np.eye(128, dtype=np.float32).astype(ml_dtypes.bfloat16)
    identf = np.eye(128, dtype=np.float32)
    onescb = np.ones((128, 1), np.float32).astype(ml_dtypes.bfloat16)
    consts = np.tile(np.array([TANH_SC, EXP_SC, 128.0, float(D)], f32)[None, :], (128, 1))

    shared = dict(wqT=wqT, wkT=wkT, wvT=wvT, woT=woT, gwT=gwT,
                  wgT=wgT, wuT=wuT, wdT=wdT, rows=rows, masks=masks,
                  identb=identb, identf=identf, onescb=onescb, consts=consts)
    return x, shared


def kernel(**inp):
    x, shared = _host_prep(inp)
    nc = build(phases=6)
    in_maps = [dict(shared, x=np.ascontiguousarray(x[b])) for b in range(8)]
    res = run_bass_kernel_spmd(nc, in_maps, core_ids=list(range(8)))
    return np.stack([np.asarray(res.results[b]["out"]) for b in range(8)]).astype(np.float32)


# version 10 (final)
# speedup vs baseline: 1.2773x; 1.0561x over previous
import numpy as np
from contextlib import ExitStack
import concourse.bass as bass
import concourse.mybir as mybir
from concourse.bass_utils import run_bass_kernel_spmd

F32 = mybir.dt.float32
BF16 = mybir.dt.bfloat16
AF = mybir.ActivationFunctionType
ALU = mybir.AluOpType
AX = mybir.AxisListType

T, D, H, DH = 4096, 2048, 16, 128
E, MD = 16, 1408
NKB = D // 128
NMB = MD // 128
NT = T // 128
NN = T // 512
TANH_SC = float((1.0 / np.sqrt(128.0)) / 50.0)
EXP_SC = 50.0


def build(phases=6, dump=(), p2sub=2, npr=8, ntt=NT, p3sub=3, nt3=NT, p3x=0):
    nc = bass.Bass("TRN2", target_bir_lowering=False)
    dtn = nc.dram_tensor
    x_h = dtn("x", [T, D], F32, kind="ExternalInput").ap()
    wqT_h = dtn("wqT", [D, D], F32, kind="ExternalInput").ap()
    wkT_h = dtn("wkT", [D, D], F32, kind="ExternalInput").ap()
    wvT_h = dtn("wvT", [D, D], F32, kind="ExternalInput").ap()
    woT_h = dtn("woT", [D, D], F32, kind="ExternalInput").ap()
    gwT_h = dtn("gwT", [D, E], F32, kind="ExternalInput").ap()
    wgT_h = dtn("wgT", [E, D, MD], F32, kind="ExternalInput").ap()
    wuT_h = dtn("wuT", [E, D, MD], F32, kind="ExternalInput").ap()
    wdT_h = dtn("wdT", [E, MD, D], F32, kind="ExternalInput").ap()
    rows_h = dtn("rows", [8, D], F32, kind="ExternalInput").ap()
    masks_h = dtn("masks", [128, 2048], BF16, kind="ExternalInput").ap()
    identb_h = dtn("identb", [128, 128], BF16, kind="ExternalInput").ap()
    identf_h = dtn("identf", [128, 128], F32, kind="ExternalInput").ap()
    onescb_h = dtn("onescb", [128, 1], BF16, kind="ExternalInput").ap()
    consts_h = dtn("consts", [128, 4], F32, kind="ExternalInput").ap()

    xT_h = dtn("xTs", [D, T], BF16, kind="Internal").ap()
    vtok_h = dtn("vtoks", [T, D], BF16, kind="Internal").ap()
    aoT_h = dtn("aoTs", [D, T], BF16, kind="Internal").ap()
    h1_h = dtn("h1s", [T, D], F32, kind="Internal").ap()
    hT_h = dtn("hTs", [D, T], BF16, kind="Internal").ap()
    wqB_h = dtn("wqB", [D, D], BF16, kind="Internal").ap()
    wkB_h = dtn("wkB", [D, D], BF16, kind="Internal").ap()
    woB_h = dtn("woB", [D, D], BF16, kind="Internal").ap()
    wgB_h = dtn("wgB", [E, NMB, 128, D], BF16, kind="Internal").ap()
    wuB_h = dtn("wuB", [E, NMB, 128, D], BF16, kind="Internal").ap()
    wdB_h = dtn("wdB", [E, MD, D], BF16, kind="Internal").ap()
    yac_h = dtn("yac", [T, D], F32, kind="Internal").ap()
    rsc_h = dtn("ropescratch", [4, D], F32, kind="Internal").ap()
    out_h = dtn("out", [T, D], F32, kind="ExternalOutput").ap()

    dump_h = {}
    for name, shp, dty in [
        ("d_xT", [D, T], BF16), ("d_vtok", [T, D], BF16),
        ("d_aoT", [D, T], BF16), ("d_h1", [T, D], F32),
        ("d_hT", [D, T], BF16), ("d_cw", [128, NT * E], F32),
        ("d_yac", [T, D], F32), ("d_wgB", [NMB, 128, D], BF16),
        ("d_wvsb", [128, NKB * D], BF16),
        ("d_qT", [128, 2 * T], BF16), ("d_kT", [128, 2 * T], BF16),
        ("d_pqs", [128, 512], F32), ("d_qro", [128, 256], BF16),
        ("d_wqB", [D, D], BF16), ("d_xtc2", [128, D], BF16), ("d_wqsl", [128, NKB * 256], BF16),
        ("d_msd", [128, 16], F32),
    ]:
        if name in dump:
            dump_h[name] = dtn(name, shp, dty, kind="ExternalOutput").ap()

    es = ExitStack()
    sb = lambda name, shape, dty: es.enter_context(nc.sbuf_tensor(name, shape, dty))

    cst = sb("cst", [128, 4], F32)
    idb = sb("idb", [128, 128], BF16)
    idf = sb("idf", [128, 128], F32)
    maskt = sb("maskt", [128, 2048], BF16)
    onescb = sb("s_onescb", [128, 1], BF16)
    onesrf = sb("onesrf", [1, 128], F32)
    ropeCW = sb("ropeCW", [128, 2 * D], BF16)
    ropeSW = sb("ropeSW", [128, 2 * D], BF16)
    ghi = sb("ghi", [128, NKB * E], BF16)
    glo = sb("glo", [128, NKB * E], BF16)
    gwf = sb("gwf", [128, NKB * E], F32)
    glof = sb("glof", [128, NKB * E], F32)
    cwsb = sb("cwsb", [128, NT * E], F32)
    setup_ps = ExitStack()
    rowsb = setup_ps.enter_context(nc.sbuf_tensor("rowsb", [1, 8 * D], F32))
    rwork = setup_ps.enter_context(nc.sbuf_tensor("rwork", [1, 5 * D], F32))

    sems = {}
    dma_sems = set()
    def S(name):
        if name not in sems:
            sems[name] = [es.enter_context(nc.semaphore(name)), 0]
        return sems[name]
    def inc(name, instr, n=1):
        s = S(name)
        instr.then_inc(s[0], n)
        s[1] += n
        if n >= 16:
            dma_sems.add(name)
        return instr
    def cur(name):
        return S(name)[1]
    def dinc(name, instr):
        return inc(name, instr, 16)
    waits_log = []
    def W(eng, name, val):
        waits_log.append((name, val))
        eng.wait_ge(S(name)[0], val)
    def fence(eng, instr, name=None):
        if name is None:
            name = "FEN_" + type(eng).__name__
        inc(name, instr)
        W(eng, name, cur(name))
        return instr

    block = es.enter_context(nc.Block())
    nc.const_aps.register(F32, TANH_SC, cst[:, 0:1])
    nc.const_aps.register(F32, EXP_SC, cst[:, 1:2])
    nc.const_aps.register(F32, 128.0, cst[:, 2:3])
    nc.const_aps.register(F32, float(D), cst[:, 3:4])

    bar_n = [0]
    ENGINES = ["sync", "scalar", "vector", "tensor", "gpsimd"]
    def barrier():
        @block.sync
        def _(sy):
            for nm in sorted(dma_sems):
                W(sy, nm, cur(nm))
        bar_n[0] += 1
        tgt = 5 * 16 * bar_n[0]
        for en in ENGINES:
            @getattr(block, en)
            def _(eng):
                inc("BAR", eng.nop(), 16)
        for en in ENGINES:
            @getattr(block, en)
            def _(eng):
                W(eng, "BAR", tgt)

    @block.sync
    def _(sy):
        for dst, src in [
            (cst[:], consts_h), (idb[:], identb_h), (idf[:], identf_h),
            (maskt[:], masks_h), (onescb[:], onescb_h),
            (onesrf[0:1, :], rows_h[7:8, 0:128]),
            (rowsb[0:1, :], rows_h.rearrange("r d -> (r d)")[None, :]),
            (gwf[:].rearrange("p (kb e) -> p kb e", kb=NKB),
             gwT_h.rearrange("(kb p) e -> p kb e", p=128)),
        ]:
            dinc("SETL", sy.dma_start(out=dst, in_=src))
    @block.vector
    def _(v):
        W(v, "SETL", cur("SETL"))
        CE, SE, SG, WQ, WQS, WK, WKS = [rowsb[0:1, i * D:(i + 1) * D] for i in range(7)]
        TMP = rwork[0:1, 4 * D:5 * D]
        v.tensor_mul(rwork[0:1, 0:D], CE, WQ)
        fence(v, v.tensor_mul(TMP, SE, SG))
        v.tensor_mul(rwork[0:1, D:2 * D], TMP, WQS)
        v.tensor_mul(rwork[0:1, 2 * D:3 * D], CE, WK)
        v.tensor_mul(rwork[0:1, 3 * D:4 * D], TMP, WKS)
        fence(v, v.tensor_copy(ghi[:], gwf[:]))
        fence(v, v.tensor_sub(glof[:], gwf[:], ghi[:]))
        inc("SETV", v.tensor_copy(glo[:], glof[:]))
    @block.gpsimd
    def _(g):
        W(g, "SETV", cur("SETV"))
        dinc("SETG0", g.dma_start(out=rsc_h.rearrange("r d -> (r d)")[None, :], in_=rwork[0:1, 0:4 * D]))
        W(g, "SETG0", 16)
        for pr2 in range(8):
            for row, dst, off in ((0, ropeCW, 0), (2, ropeCW, 256), (1, ropeSW, 0), (3, ropeSW, 256)):
                dinc("SETG", g.dma_start(
                    out=dst[:, pr2 * 512 + off:pr2 * 512 + off + 256],
                    in_=rsc_h[row:row + 1, pr2 * 256:(pr2 + 1) * 256].to_broadcast([128, 256])))
    setup_ps.close()

    if phases >= 1:
        with ExitStack() as ps:
            psb = lambda name, shape, dty: ps.enter_context(nc.sbuf_tensor(name, shape, dty))
            wvsb = psb("wvsb", [128, NKB * D], BF16)
            xf = [psb(f"p1xf{i}", [128, D], F32) for i in range(2)]
            xb = [psb(f"p1xb{i}", [128, D], BF16) for i in range(2)]
            xtc = [psb(f"p1xtc{i}", [128, D], BF16) for i in range(2)]
            vst = [psb(f"p1vst{i}", [128, D], BF16) for i in range(2)]
            ptr = ps.enter_context(nc.psum_tensor("p1ptr", [128, 2048], BF16))
            pv = ps.enter_context(nc.psum_tensor("p1pv", [128, 2048], F32))

            @block.gpsimd
            def _(g):
                W(g, "SETL", cur("SETL"))
                for c in range(4):
                    dinc("WVL", g.dma_start(
                        out=wvsb[:].rearrange("p (kb n) -> p kb n", kb=NKB)[:, c * 4:(c + 1) * 4, :],
                        in_=wvT_h.rearrange("(kb p) n -> p kb n", p=128)[:, c * 4:(c + 1) * 4, :]))
                if "d_wvsb" in dump_h:
                    W(g, "P1SV", 16 * NT)
                    dinc("DMPW", g.dma_start(out=dump_h["d_wvsb"], in_=wvsb[:]))
                    W(g, "DMPW", 16)
            @block.sync
            def _(sy):
                W(sy, "SETG", 16 * 32)
                for t in range(NT):
                    if t >= 2:
                        W(sy, "P1XF", t - 1)
                    dinc("P1LX", sy.dma_start(out=xf[t % 2][:], in_=x_h[t * 128:(t + 1) * 128, :]))
            @block.vector
            def _(v):
                for t in range(NT):
                    b = t % 2
                    W(v, "P1LX", 16 * (t + 1))
                    if t >= 2:
                        W(v, "P1TR", t - 1)
                    inc("P1XF", v.tensor_copy(xb[b][:], xf[b][:]))
                    W(v, "P1TR", t + 1)
                    if t >= 2:
                        W(v, "P1SX", 16 * (t - 1))
                    inc("P1EV", v.tensor_copy(xtc[b][:], ptr[:]))
            @block.tensor
            def _(te):
                W(te, "WVL", 16 * 4)
                for t in range(NT):
                    b = t % 2
                    W(te, "P1XF", t + 1)
                    if t >= 1:
                        W(te, "P1EV", t)
                    for k in range(NKB):
                        instr = te.transpose(ptr[:, k * 128:(k + 1) * 128], xb[b][:, k * 128:(k + 1) * 128], idb[:])
                    inc("P1TR", instr)
                    W(te, "P1EV", t + 1)
                    if t >= 1:
                        W(te, "P1VE", t)
                    for d in range(4):
                        for k in range(NKB):
                            instr = te.matmul(pv[:, d * 512:(d + 1) * 512],
                                              xtc[b][:, k * 128:(k + 1) * 128],
                                              wvsb[:, k * D + d * 512:k * D + d * 512 + 512],
                                              start=(k == 0), stop=(k == NKB - 1),
                                              skip_group_check=True)
                    inc("P1MM", instr)
            @block.scalar
            def _(sc):
                for t in range(NT):
                    b = t % 2
                    W(sc, "P1MM", t + 1)
                    if t >= 2:
                        W(sc, "P1SV", 16 * (t - 1))
                    inc("P1VE", sc.copy(vst[b][:], pv[:]))
                    W(sc, "P1VE", t + 1)
                    dinc("P1SV", sc.dma_start(out=vtok_h[t * 128:(t + 1) * 128, :], in_=vst[b][:]))
                    W(sc, "P1EV", t + 1)
                    dinc("P1SX", sc.dma_start(
                        out=xT_h[:, t * 128:(t + 1) * 128].rearrange("(kb p) c -> p kb c", p=128),
                        in_=xtc[b][:].rearrange("p (kb c) -> p kb c", kb=NKB)))
            barrier()

    if phases >= 2:
        pre = [es.enter_context(nc.sbuf_tensor(f"preb{i}", [128, 2048], BF16)) for i in range(2)]
        @block.gpsimd
        def _(g):
            W(g, "SETL", cur("SETL"))
            nbuf = [0]
            def stage(dst_ap, src_ap, nelem, view=None):
                b = nbuf[0] % 2
                if nbuf[0] >= 2:
                    W(g, "PRES", 16 * (nbuf[0] - 1))
                dinc("PREL", g.dma_start(out=pre[b][:, 0:nelem], in_=src_ap))
                W(g, "PREL", cur("PREL"))
                buf = pre[b][:, 0:nelem] if view is None else view(pre[b][:, 0:nelem])
                dinc("PRES", g.dma_start(out=dst_ap, in_=buf))
                nbuf[0] += 1
            for W_h, WB_h in [(wqT_h, wqB_h), (wkT_h, wkB_h), (woT_h, woB_h)]:
                for c in range(16):
                    stage(WB_h.rearrange("(kb p) n -> p kb n", p=128)[:, c:c + 1, :],
                          W_h.rearrange("(kb p) n -> p kb n", p=128)[:, c:c + 1, :],
                          D)
            for W_h, WB_h in [(wgT_h, wgB_h), (wuT_h, wuB_h)]:
                for e in range(E):
                    for kb in range(NKB):
                        stage(WB_h[e, :, :, kb * 128:(kb + 1) * 128].rearrange("m p c -> p m c"),
                              W_h[e, kb * 128:(kb + 1) * 128, :],
                              MD,
                              view=lambda bb: bb.rearrange("p (m c) -> p m c", m=NMB))
            for e in range(E):
                for m in range(NMB):
                    stage(wdB_h[e, m * 128:(m + 1) * 128, :],
                          wdT_h[e, m * 128:(m + 1) * 128, :],
                          D)

    if phases >= 2 and p2sub >= 1:
        with ExitStack() as ps:
            psb = lambda name, shape, dty: ps.enter_context(nc.sbuf_tensor(name, shape, dty))
            wq_sl = [psb(f"wqsl{i}", [128, NKB * 256], BF16) for i in range(2)]
            wk_sl = [psb(f"wksl{i}", [128, NKB * 256], BF16) for i in range(2)]
            xtc2 = [psb(f"p2xtc{i}", [128, D], BF16) for i in range(2)]
            qkTp = [psb(f"qkTp{i}", [128, 4 * T], BF16) for i in range(2)]
            vhb = [psb(f"vhb{i}", [128, T], BF16) for i in range(2)]
            ostg = [psb(f"ostg{i}", [128, T], BF16) for i in range(2)]
            pqs = [psb(f"pqs{i}", [128, 512], F32) for i in range(2)]
            qrs = psb("qrs", [128, 512], F32)
            jro = psb("jro", [128, 512], F32)
            sqv = jro
            jro2 = psb("jro2", [128, 512], F32)
            qbb = [psb(f"qbb{i}", [128, 512], BF16) for i in range(2)]
            msd = psb("msd", [128, 16], F32)
            ts1P = [psb(f"ts1{i}", [128, 1024], F32) for i in range(2)]
            ebufP = [psb(f"ebuf{i}", [128, 1024], BF16) for i in range(3)]
            etmpP = psb("etmp", [128, 1024], BF16)
            rdb = psb("rdb", [1, 512], F32)
            rbb = psb("rbb", [128, 512], F32)

            pqk = ps.enter_context(nc.psum_tensor("p2qk", [128, 512], F32))
            pq = pqk[:, 0:256]
            pk = pqk[:, 256:512]
            ptr2_full = ps.enter_context(nc.psum_tensor("p2tr", [128, 1024], BF16))
            ptr2 = ptr2_full[:, 0:512]
            pssP = [ps.enter_context(nc.psum_tensor(f"p2sp{i}", [128, 1024], F32)) for i in range(2)]
            pso = ps.enter_context(nc.psum_tensor("p2o", [128, 512], F32))
            psd = ps.enter_context(nc.psum_tensor("p2d", [128, 512], F32))

            gpi = [0]

            def emit_loads(pr):
                pb = pr % 2
                @block.sync
                def _(sy):
                    if pr == 0:
                        W(sy, "PRES", 16 * 32)
                    if pr >= 2:
                        W(sy, "P2TE", ntt * (pr - 1))
                    dinc("WQL", sy.dma_start(
                        out=wq_sl[pb][:].rearrange("p (kb c) -> p kb c", kb=NKB),
                        in_=wqB_h[:, pr * 256:(pr + 1) * 256].rearrange("(kb p) c -> p kb c", p=128)))
                    dinc("WQL", sy.dma_start(
                        out=wk_sl[pb][:].rearrange("p (kb c) -> p kb c", kb=NKB),
                        in_=wkB_h[:, pr * 256:(pr + 1) * 256].rearrange("(kb p) c -> p kb c", p=128)))

            def emit_xload(pr, t):
                @block.sync
                def _(sy):
                    need = pr * ntt + t
                    if pr == 0:
                        W(sy, "P1SX", 16 * (t + 1))
                    if need >= 2:
                        W(sy, "P2MMQ", need - 1)
                    dinc("P2XL", sy.dma_start(
                        out=xtc2[need % 2][:].rearrange("p (kb c) -> p kb c", kb=NKB),
                        in_=xT_h[:, t * 128:(t + 1) * 128].rearrange("(kb p) c -> p kb c", p=128)))

            def emit_prep_tile(pr, t):
                pb = pr % 2
                gi = pr * ntt + t
                xb3 = gi % 2
                @block.tensor
                def _(te):
                    W(te, "P2XL", 16 * (gi + 1))
                    W(te, "WQL", 32 * (pr + 1))
                    if gi >= 1:
                        W(te, "P2CP", gi)
                    for k in range(NKB):
                        te.matmul(pq, xtc2[xb3][:, k * 128:(k + 1) * 128],
                                  wq_sl[pb][:, k * 256:(k + 1) * 256],
                                  start=(k == 0), stop=(k == NKB - 1), skip_group_check=True)
                    for k in range(NKB):
                        instr = te.matmul(pk, xtc2[xb3][:, k * 128:(k + 1) * 128],
                                          wk_sl[pb][:, k * 256:(k + 1) * 256],
                                          start=(k == 0), stop=(k == NKB - 1), skip_group_check=True)
                    inc("P2MMQ", instr)
                @block.vector
                def _(v):
                    W(v, "P2MMQ", gi + 1)
                    if gi >= 2:
                        W(v, "P2RC", gi - 1)
                    inc("P2CP", v.tensor_copy(pqs[gi % 2][:], pqk[:]))
                    W(v, "P2CP", cur("P2CP"))
                    fence(v, v.tensor_mul(sqv[:], pqs[gi % 2][:], pqs[gi % 2][:]))
                    for s2 in range(4):
                        instr = v.tensor_reduce(msd[:, s2:s2 + 1], sqv[:, s2 * 128:(s2 + 1) * 128],
                                                AX.X, ALU.add)
                    fence(v, instr)
                    fence(v, v.tensor_scalar(msd[:, 4:8], msd[:, 0:4], 128.0 * 1e-6, None, ALU.add))
                    inc("P2MS", v.reciprocal(msd[:, 8:12], msd[:, 4:8]))
                @block.scalar
                def _(sc):
                    W(sc, "P2MS", gi + 1)
                    if gi >= 1:
                        W(sc, "P2RC", gi)
                    inc("P2RS", sc.activation(msd[:, 12:16], msd[:, 8:12], AF.Sqrt, scale=128.0))
                @block.vector
                def _(v):
                    W(v, "P2RS", gi + 1)
                    if gi >= 2:
                        W(v, "P2TR", gi - 1)
                    for s2 in range(4):
                        instr = v.tensor_scalar(qrs[:, s2 * 128:(s2 + 1) * 128],
                                                pqs[gi % 2][:, s2 * 128:(s2 + 1) * 128],
                                                msd[:, 12 + s2:13 + s2], None, ALU.mult)
                    fence(v, instr)
                    CW = ropeCW[:, pr * 512:(pr + 1) * 512]
                    SW = ropeSW[:, pr * 512:(pr + 1) * 512]
                    q3 = qrs[:].rearrange("p (n two) -> p n two", two=2)
                    j3 = jro2[:].rearrange("p (n two) -> p n two", two=2)
                    s3 = SW.rearrange("p (n two) -> p n two", two=2)
                    v.tensor_mul(jro[:], qrs[:], CW)
                    v.tensor_mul(j3[:, :, 0:1], q3[:, :, 1:2], s3[:, :, 0:1])
                    fence(v, v.tensor_mul(j3[:, :, 1:2], q3[:, :, 0:1], s3[:, :, 1:2]))
                    inc("P2RC", v.tensor_add(qbb[gi % 2][:], jro[:], jro2[:]))

            def emit_prep_tr(pr, t):
                pb = pr % 2
                gi = pr * ntt + t
                @block.tensor
                def _(te):
                    W(te, "P2RC", gi + 1)
                    if gi >= 1:
                        W(te, "P2TE", gi)
                    for s2 in range(4):
                        instr = te.transpose(ptr2_full[:, s2 * 128:(s2 + 1) * 128],
                                             qbb[gi % 2][:, s2 * 128:(s2 + 1) * 128], idb[:])
                    inc("P2TR", instr)
                @block.vector
                def _(v):
                    W(v, "P2TR", gi + 1)
                    if t == 0 and pr >= 2 and p2sub >= 2:
                        W(v, "TAV", (2 * pr - 2) * 72)
                    instr = v.tensor_copy(
                        qkTp[pb][:].rearrange("p (g tt) -> p g tt", g=4)[:, :, t * 128:(t + 1) * 128],
                        ptr2_full[:, 0:512].rearrange("p (g c) -> p g c", g=4))
                    inc("P2TE", instr)

            def emit_attn_tile(pr, hl, n):
                pb = pr % 2
                h = 2 * pr + hl
                QOF = hl * T
                KOF = 2 * T + hl * T
                ni = 4 * n + 4
                np2 = ni // 2
                hn = h * NN + n
                ap0 = gpi[0]
                tanh_fence = {}

                def emit_mms(pi):
                    ap = ap0 + pi
                    @block.tensor
                    def _(te):
                        if pi == 0:
                            W(te, "P2TE", ntt * (pr + 1))
                            W(te, "VHL", 16 * (h + 1))
                            if hn >= 1:
                                W(te, "OE", hn)
                        if ap >= 2:
                            W(te, "FEN_SC_TANH", ap - 1)
                        te.matmul(pssP[ap % 2][:, 0:512],
                                  qkTp[pb][:, KOF + 2 * pi * 128:KOF + (2 * pi + 1) * 128],
                                  qkTp[pb][:, QOF + n * 512:QOF + (n + 1) * 512],
                                  start=True, stop=True, skip_group_check=True)
                        instr = te.matmul(pssP[ap % 2][:, 512:1024],
                                          qkTp[pb][:, KOF + (2 * pi + 1) * 128:KOF + (2 * pi + 2) * 128],
                                          qkTp[pb][:, QOF + n * 512:QOF + (n + 1) * 512],
                                          start=True, stop=True, skip_group_check=True)
                        inc("TS", instr)

                def emit_tanh(pi):
                    ap = ap0 + pi
                    @block.scalar
                    def _(sc):
                        W(sc, "TS", ap + 1)
                        instr = sc.activation(ts1P[ap % 2][:], pssP[ap % 2][:], AF.Tanh, scale=TANH_SC)
                        inc("FEN_SC_TANH", instr)
                        tanh_fence[pi] = cur("FEN_SC_TANH")

                def emit_exp(pi):
                    ap = ap0 + pi
                    diag = pi >= 2 * n
                    @block.scalar
                    def _(sc):
                        W(sc, "FEN_SC_TANH", tanh_fence[pi])
                        if ap >= 3:
                            W(sc, "TAV", ap - 2)
                        if diag:
                            W(sc, "VM", cur("VM"))
                        instr = sc.activation(etmpP[:] if diag else ebufP[ap % 3][:],
                                              ts1P[ap % 2][:], AF.Exp, scale=EXP_SC)
                        inc("TH", instr)
                    if diag:
                        @block.vector
                        def _(v):
                            W(v, "TH", ap + 1)
                            for half in range(2):
                                off_idx = 2 * pi - 4 * n + half
                                inc("VM", v.tensor_mul(
                                    ebufP[ap % 3][:, half * 512:(half + 1) * 512],
                                    etmpP[:, half * 512:(half + 1) * 512],
                                    maskt[:, off_idx * 512:(off_idx + 1) * 512]))

                def emit_dav(pi):
                    ap = ap0 + pi
                    diag = pi >= 2 * n
                    @block.tensor
                    def _(te):
                        W(te, "TH", ap + 1)
                        if diag:
                            W(te, "VM", cur("VM"))
                        for half in range(2):
                            i = 2 * pi + half
                            te.matmul(psd[0:1, :], onescb[:],
                                      ebufP[ap % 3][:, half * 512:(half + 1) * 512],
                                      start=(i == 0), stop=(i == ni - 1), skip_group_check=True)
                            instr = te.matmul(pso[:],
                                              vhb[h % 2][:, i * 128:(i + 1) * 128],
                                              ebufP[ap % 3][:, half * 512:(half + 1) * 512],
                                              start=(i == 0), stop=(i == ni - 1), skip_group_check=True)
                        inc("TAV", instr)

                done = [0]
                for r in range(0, np2, 2):
                    emit_mms(r)
                    if r + 1 < np2:
                        emit_mms(r + 1)
                    for q in (r - 2, r - 1):
                        if 0 <= q:
                            emit_exp(q)
                            emit_dav(q)
                            done[0] = q + 1
                    emit_tanh(r)
                    if r + 1 < np2:
                        emit_tanh(r + 1)
                for q in range(done[0], np2):
                    emit_exp(q)
                    emit_dav(q)
                gpi[0] += np2

                @block.vector
                def _(v):
                    W(v, "TAV", gpi[0])
                    inc("RD", v.reciprocal(rdb[0:1, :], psd[0:1, :]))
                @block.tensor
                def _(te):
                    W(te, "RD", hn + 1)
                    instr = te.matmul(pssP[gpi[0] % 2][:, 0:512], onesrf[0:1, :], rdb[0:1, :],
                                      start=True, stop=True, skip_group_check=True)
                    inc("TB", instr)
                @block.vector
                def _(v):
                    W(v, "TB", hn + 1)
                    if n == 0 and h >= 2:
                        W(v, "AOS", 16 * (h - 1))
                    fence(v, v.tensor_copy(rbb[:], pssP[gpi[0] % 2][:, 0:512]))
                    instr = v.tensor_mul(ostg[hl][:, n * 512:(n + 1) * 512], pso[:], rbb[:])
                    inc("OE", instr)

            def emit_vload(h):
                @block.sync
                def _(sy):
                    if h == 0:
                        W(sy, "P1SV", 16 * NT)
                    if h >= 2:
                        W(sy, "VHU", h - 1)
                    dinc("VHL", sy.dma_start(
                        out=vhb[h % 2][:].rearrange("p (s c) -> p s c", s=32),
                        in_=vtok_h[:, h * 128:(h + 1) * 128].rearrange("(s p) c -> p s c", p=128)))

            tr_pend = []
            def push_prep(pr, t):
                emit_xload(pr, t)
                emit_prep_tile(pr, t)
                tr_pend.append((pr, t))
                while len(tr_pend) > 2:
                    emit_prep_tr(*tr_pend.pop(0))
            def flush_tr():
                while tr_pend:
                    emit_prep_tr(*tr_pend.pop(0))
            emit_loads(0)
            for t in range(ntt):
                push_prep(0, t)
            flush_tr()
            for pr in range(npr):
                if pr + 1 < npr:
                    emit_loads(pr + 1)
                natt = 0
                for hl in (range(2) if p2sub >= 2 else []):
                    h = 2 * pr + hl
                    emit_vload(h)
                    for n in range(NN):
                        emit_attn_tile(pr, hl, n)
                        if pr + 1 < npr:
                            for t in range(2 * natt, min(2 * natt + 2, ntt)):
                                push_prep(pr + 1, t)
                        natt += 1
                    @block.tensor
                    def _(te):
                        W(te, "OE", (h + 1) * NN)
                        inc("VHU", te.nop())
                    @block.scalar
                    def _(sc):
                        W(sc, "OE", (h + 1) * NN)
                        dinc("AOS", sc.dma_start(out=aoT_h[h * 128:(h + 1) * 128, :], in_=ostg[hl][:]))
                if p2sub < 2 and pr + 1 < npr:
                    for t in range(ntt):
                        push_prep(pr + 1, t)
                flush_tr()
            barrier()

    if phases >= 3:
        with ExitStack() as ps:
            psb = lambda name, shape, dty: ps.enter_context(nc.sbuf_tensor(name, shape, dty))
            wosb = psb("wosb", [128, NKB * D], BF16)
            aot = [psb(f"aot{i}", [128, D], BF16) for i in range(2)]
            xres = [psb(f"xres{i}", [128, D], F32) for i in range(2)]
            sres = psb("sres", [128, D], F32)
            junk3 = sres
            stats = psb("stats", [128, 8], F32)
            dscr = psb("dscr", [128, D], F32)
            h1t = [psb(f"h1t{i}", [128, D], F32) for i in range(2)]
            hib = psb("hib", [128, D], BF16)
            lob = psb("lob", [128, D], BF16)
            hTst0 = psb("hTst0", [128, NKB * 512], BF16)
            hTst = [hTst0, hTst0]
            loTst = psb("loTst", [128, NKB * 512], BF16)
            lgsb = psb("lgsb", [128, 512], F32)
            ltok = psb("ltok", [128, 16], F32)
            gsc = psb("gsc", [128, 64], F32)

            po3 = [ps.enter_context(nc.psum_tensor(f"p3o{i}", [128, 512], F32)) for i in range(2)]
            ptrh = ps.enter_context(nc.psum_tensor("p3trh", [128, 2048], BF16))
            ptrl = ps.enter_context(nc.psum_tensor("p3trl", [128, 2048], BF16))
            plg = ps.enter_context(nc.psum_tensor("p3lg", [128, 512], F32))
            ptt = ps.enter_context(nc.psum_tensor("p3tt", [128, 128], F32))

            @block.vector
            def _(v):
                inc("LGZ", v.memset(lgsb[:], 0.0))
            @block.sync
            def _(sy):
                W(sy, "PRES", 16 * 48)
                W(sy, "AOS", 16 * H)
                for c in range(4):
                    dinc("WOL", sy.dma_start(
                        out=wosb[:].rearrange("p (kb n) -> p kb n", kb=NKB)[:, c * 4:(c + 1) * 4, :],
                        in_=woB_h.rearrange("(kb p) n -> p kb n", p=128)[:, c * 4:(c + 1) * 4, :]))
                for t in range(nt3):
                    b = t % 2
                    if t >= 2:
                        W(sy, "P3MM", 4 * (t - 1))
                        W(sy, "P3SE", 4 * (t - 1))
                    dinc("P3AL", sy.dma_start(
                        out=aot[b][:].rearrange("p (kb c) -> p kb c", kb=NKB),
                        in_=aoT_h[:, t * 128:(t + 1) * 128].rearrange("(kb p) c -> p kb c", p=128)))
                    dinc("P3XL", sy.dma_start(out=xres[b][:], in_=x_h[t * 128:(t + 1) * 128, :]))

            for t in range(nt3):
                b = t % 2
                n = t // 4
                nb = n % 2
                @block.tensor
                def _(te):
                    W(te, "P3AL", 16 * (t + 1))
                    if t == 0:
                        W(te, "WOL", 16 * 4)
                    for d in range(4):
                        if 4 * t + d >= 2:
                            W(te, "P3SE", 4 * t + d - 1)
                        if p3x >= 2:
                            inc("P3MM", te.nop())
                            continue
                        for k in range(NKB):
                            instr = te.matmul(po3[d % 2][:],
                                              aot[b][:, k * 128:(k + 1) * 128],
                                              wosb[:, k * D + d * 512:k * D + d * 512 + 512],
                                              start=(k == 0), stop=(k == NKB - 1))
                        inc("P3MM", instr)
                @block.vector
                def _(v):
                    W(v, "P3XL", 16 * (t + 1))
                    for d in range(4):
                        W(v, "P3MM", 4 * t + d + 1)
                        if p3x >= 2:
                            inc("P3SE", v.tensor_copy(sres[:, d * 512:(d + 1) * 512],
                                                      xres[b][:, d * 512:(d + 1) * 512]))
                        else:
                            inc("P3SE", v.tensor_add(sres[:, d * 512:(d + 1) * 512],
                                                     xres[b][:, d * 512:(d + 1) * 512], po3[d % 2][:]))
                    if p3x >= 1:
                        inc("P3LN", v.tensor_copy(dscr[:], sres[:]))
                    else:
                        fence(v, v.tensor_reduce(stats[:, 0:1], sres[:], AX.X, ALU.add))
                        fence(v, v.tensor_scalar(stats[:, 1:2], stats[:, 0:1], 1.0 / D, None, ALU.mult))
                        inc("P3LN", v.tensor_scalar(dscr[:], sres[:], stats[:, 1:2], None, ALU.subtract))
                @block.scalar
                def _(sc):
                    W(sc, "P3LN", t + 1)
                    if p3x >= 1:
                        inc("P3SQ", sc.nop())
                    else:
                        inc("P3SQ", sc.activation(junk3[:], dscr[:], AF.Square, accum_out=stats[:, 2:3]))
                @block.vector
                def _(v):
                    W(v, "P3SQ", t + 1)
                    if p3x >= 1:
                        inc("P3RC", v.engine_nop())
                    else:
                        fence(v, v.tensor_scalar(stats[:, 3:4], stats[:, 2:3], float(D) * 1e-5, None, ALU.add))
                        inc("P3RC", v.reciprocal(stats[:, 4:5], stats[:, 3:4]))
                @block.scalar
                def _(sc):
                    W(sc, "P3RC", t + 1)
                    if p3x >= 1:
                        inc("P3SR", sc.nop())
                    else:
                        inc("P3SR", sc.activation(stats[:, 5:6], stats[:, 4:5], AF.Sqrt, scale=float(D)))
                @block.vector
                def _(v):
                    W(v, "P3SR", t + 1)
                    if t >= 2:
                        W(v, "P3H1S", 16 * (t - 1))
                    if p3x >= 1:
                        inc("P3HD", v.tensor_copy(h1t[b][:], dscr[:]))
                    else:
                        inc("P3HD", v.tensor_scalar(h1t[b][:], dscr[:], stats[:, 5:6], None, ALU.mult))
                    W(v, "P3HD", cur("P3HD"))
                    if t >= 1 and p3sub >= 2:
                        W(v, "P3TR", t)
                    fence(v, v.tensor_copy(hib[:], h1t[b][:]))
                    inc("P3HL", v.tensor_sub(lob[:], h1t[b][:], hib[:]))
                @block.scalar
                def _(sc):
                    W(sc, "P3HD", t + 1)
                    dinc("P3H1S", sc.dma_start(out=h1_h[t * 128:(t + 1) * 128, :], in_=h1t[b][:]))
                if p3sub < 2:
                    continue
                @block.tensor
                def _(te):
                    W(te, "P3HL", t + 1)
                    if t >= 1:
                        W(te, "P3TE", 2 * t)
                    for k in range(NKB):
                        te.transpose(ptrh[:, k * 128:(k + 1) * 128], hib[:, k * 128:(k + 1) * 128], idb[:])
                        instr = te.transpose(ptrl[:, k * 128:(k + 1) * 128], lob[:, k * 128:(k + 1) * 128], idb[:])
                    inc("P3TR", instr)
                @block.vector
                def _(v):
                    W(v, "P3TR", t + 1)
                    if t % 4 == 0 and n >= 1 and p3sub >= 3:
                        W(v, "P3GD", n)
                        W(v, "P3HTS", 16 * n)
                        W(v, "P3GM", n)
                    v.tensor_copy(
                        hTst[nb][:].rearrange("p (kb c) -> p kb c", kb=NKB)[:, :, (t % 4) * 128:(t % 4 + 1) * 128],
                        ptrh[:].rearrange("p (kb c) -> p kb c", kb=NKB))
                    instr = v.tensor_copy(
                        loTst[:].rearrange("p (kb c) -> p kb c", kb=NKB)[:, :, (t % 4) * 128:(t % 4 + 1) * 128],
                        ptrl[:].rearrange("p (kb c) -> p kb c", kb=NKB))
                    inc("P3TE", instr, 2)
                if t % 4 == 3 and p3sub >= 3:
                    @block.tensor
                    def _(te):
                        W(te, "P3TE", 2 * (t + 1))
                        if n == 0:
                            W(te, "SETV", cur("SETV"))
                        if n >= 1:
                            W(te, "P3GE", n)
                        nmm = 0
                        for k in range(NKB):
                            for (lw, rh) in [(ghi, hTst[nb]), (glo, hTst[nb]), (ghi, loTst)]:
                                nmm += 1
                                instr = te.matmul(plg[0:16, :], lw[:, k * 16:(k + 1) * 16],
                                                  rh[:, k * 512:(k + 1) * 512],
                                                  start=(nmm == 1), stop=(nmm == 3 * NKB))
                        inc("P3GM", instr)
                    @block.vector
                    def _(v):
                        W(v, "P3GM", n + 1)
                        W(v, "LGZ", 1)
                        if n >= 1:
                            W(v, "P3TT", 4 * n)
                        inc("P3GE", v.tensor_copy(lgsb[0:16, :], plg[0:16, :]))
                    for c in range(4):
                        @block.tensor
                        def _(te):
                            W(te, "P3GE", n + 1)
                            if 4 * n + c >= 1:
                                W(te, "P3TTE", 4 * n + c)
                            inc("P3TT", te.transpose(ptt[:], lgsb[:, c * 128:(c + 1) * 128], idf[:]))
                        @block.vector
                        def _(v):
                            W(v, "P3TT", 4 * n + c + 1)
                            tt = n * 4 + c
                            if tt >= 1:
                                W(v, "P3EX", tt)
                            inc("P3TTE", v.tensor_copy(ltok[:], ptt[:, 0:16]))
                            W(v, "P3TTE", cur("P3TTE"))
                            fence(v, v.max(gsc[:, 0:8], ltok[:]))
                            inc("P3M8", v.tensor_scalar(gsc[:, 8:24], ltok[:], gsc[:, 5:6], None, ALU.is_ge))
                        @block.scalar
                        def _(sc):
                            W(sc, "P3M8", 4 * n + c + 1)
                            inc("P3EX", sc.activation(gsc[:, 24:40], ltok[:], AF.Exp))
                        @block.vector
                        def _(v):
                            W(v, "P3EX", 4 * n + c + 1)
                            tt = n * 4 + c
                            fence(v, v.tensor_reduce(gsc[:, 40:41], gsc[:, 24:40], AX.X, ALU.add))
                            fence(v, v.reciprocal(gsc[:, 41:42], gsc[:, 40:41]))
                            fence(v, v.tensor_scalar(gsc[:, 42:58], gsc[:, 24:40], gsc[:, 41:42], None, ALU.mult))
                            instr = v.tensor_mul(cwsb[:, tt * 16:(tt + 1) * 16], gsc[:, 42:58], gsc[:, 8:24])
                            inc("P3CW", instr)
                            if c == 3:
                                inc("P3GD", v.engine_nop())
                    @block.scalar
                    def _(sc):
                        W(sc, "P3TE", 2 * (t + 1))
                        dinc("P3HTS", sc.dma_start(
                            out=hT_h[:, n * 512:(n + 1) * 512].rearrange("(kb p) c -> p kb c", p=128),
                            in_=hTst[nb][:].rearrange("p (kb c) -> p kb c", kb=NKB)))
            barrier()

    if phases >= 5:
        with ExitStack() as ps:
            psb = lambda name, shape, dty: ps.enter_context(nc.sbuf_tensor(name, shape, dty))
            wdsb = psb("wdsb", [128, NMB * D], BF16)
            wgsl = [psb(f"wgsl{i}", [128, D], BF16) for i in range(2)]
            wusl = [psb(f"wusl{i}", [128, D], BF16) for i in range(2)]
            hTn = [psb(f"hTn{i}", [128, NKB * 512], BF16) for i in range(2)]
            aTn = [psb(f"aTn{i}", [128, NMB * 512], BF16) for i in range(2)]
            sg5 = psb("sg5", [128, 512], F32)
            ysb = [psb(f"ysb{i}", [128, D], F32) for i in range(2)]

            pg5 = [ps.enter_context(nc.psum_tensor(f"p5g{i}", [128, 512], F32)) for i in range(2)]
            pu5 = [ps.enter_context(nc.psum_tensor(f"p5u{i}", [128, 512], F32)) for i in range(2)]
            py5 = [ps.enter_context(nc.psum_tensor(f"p5y{i}", [128, 512], F32)) for i in range(2)]

            nws = [0]
            nyt = [0]
            for e in range(E):
                @block.sync
                def _(sy):
                    if e == 0:
                        W(sy, "PRES", cur("PRES"))
                        W(sy, "P3HTS", 16 * NN)
                        W(sy, "P3GD", NN)
                        W(sy, "P3H1S", 16 * NT)
                    if e >= 1:
                        W(sy, "P5YM", e * NN * 16)
                    dinc("P5WD", sy.dma_start(
                        out=wdsb[:].rearrange("p (mb c) -> p mb c", mb=NMB),
                        in_=wdB_h[e].rearrange("(mb p) c -> p mb c", p=128)))
                for n in range(NN):
                    gn = e * NN + n
                    hb = gn % 2
                    @block.sync
                    def _(sy):
                        if gn >= 2:
                            W(sy, "P5GU", (gn - 1) * NMB)
                        dinc("P5HL", sy.dma_start(
                            out=hTn[hb][:].rearrange("p (kb c) -> p kb c", kb=NKB),
                            in_=hT_h[:, n * 512:(n + 1) * 512].rearrange("(kb p) c -> p kb c", p=128)))
                        for m in range(NMB):
                            wb = nws[0] % 2
                            if nws[0] >= 2:
                                W(sy, "P5GU", nws[0] - 1)
                            dinc("P5WL", sy.dma_start(out=wgsl[wb][:], in_=wgB_h[e, m]))
                            dinc("P5WL", sy.dma_start(out=wusl[wb][:], in_=wuB_h[e, m]))
                            nws[0] += 1
                    for m in range(NMB):
                        gm = gn * NMB + m
                        wb = gm % 2
                        gb = gm % 2
                        @block.tensor
                        def _(te):
                            W(te, "P5HL", 16 * (gn + 1))
                            W(te, "P5WL", 32 * (gm + 1))
                            if gm >= 2:
                                W(te, "P5AE", gm - 1)
                            for k in range(NKB):
                                te.matmul(pg5[gb][:], wgsl[wb][:, k * 128:(k + 1) * 128],
                                          hTn[hb][:, k * 512:(k + 1) * 512],
                                          start=(k == 0), stop=(k == NKB - 1),
                                          skip_group_check=True)
                                instr = te.matmul(pu5[gb][:], wusl[wb][:, k * 128:(k + 1) * 128],
                                                  hTn[hb][:, k * 512:(k + 1) * 512],
                                                  start=(k == 0), stop=(k == NKB - 1),
                                                  skip_group_check=True)
                            inc("P5GU", instr)
                        @block.scalar
                        def _(sc):
                            W(sc, "P5GU", gm + 1)
                            if gm >= 1:
                                W(sc, "P5AE", gm)
                            inc("P5SI", sc.activation(sg5[:], pg5[gb][:], AF.Silu))
                        @block.vector
                        def _(v):
                            W(v, "P5SI", gm + 1)
                            if gn >= 2 and m == 0:
                                W(v, "P5YM", (gn - 1) * 16)
                            inc("P5AE", v.tensor_mul(aTn[gn % 2][:, m * 512:(m + 1) * 512], sg5[:], pu5[gb][:]))
                    for tj in range(4):
                        yb = (4 * gn + tj) % 2
                        for dd in range(4):
                            yp = nyt[0] % 2
                            @block.tensor
                            def _(te):
                                W(te, "P5AE", (gn + 1) * NMB)
                                if n == 0 and tj == 0 and dd == 0:
                                    W(te, "P5WD", 16 * (e + 1))
                                if nyt[0] >= 2:
                                    W(te, "P5YE", nyt[0] - 1)
                                for m in range(NMB):
                                    instr = te.matmul(py5[yp][:],
                                                      aTn[gn % 2][:, m * 512 + tj * 128:m * 512 + (tj + 1) * 128],
                                                      wdsb[:, m * D + dd * 512:m * D + dd * 512 + 512],
                                                      start=(m == 0), stop=(m == NMB - 1))
                                inc("P5YM", instr)
                            @block.vector
                            def _(v):
                                W(v, "P5YM", nyt[0] + 1)
                                if nyt[0] >= 8:
                                    W(v, "P5YS", 16 * (nyt[0] // 4 - 1))
                                tt = n * 4 + tj
                                inc("P5YE", v.tensor_scalar(ysb[yb][:, dd * 512:(dd + 1) * 512], py5[yp][:],
                                                            cwsb[:, tt * 16 + e:tt * 16 + e + 1], None, ALU.mult))
                            nyt[0] += 1
                        @block.gpsimd
                        def _(g):
                            W(g, "P5YE", nyt[0])
                            tt = n * 4 + tj
                            if e == 0:
                                instr = g.dma_start(out=yac_h[tt * 128:(tt + 1) * 128, :], in_=ysb[yb][:])
                            else:
                                instr = g.dma_start(out=yac_h[tt * 128:(tt + 1) * 128, :], in_=ysb[yb][:],
                                                    accum_op=ALU.add)
                            dinc("P5YS", instr)
            barrier()

    if phases >= 6:
        with ExitStack() as ps:
            psb = lambda name, shape, dty: ps.enter_context(nc.sbuf_tensor(name, shape, dty))
            yld = [psb(f"yld{i}", [128, D], F32) for i in range(2)]
            hld = [psb(f"hld{i}", [128, D], F32) for i in range(2)]
            s6 = psb("s6", [128, D], F32)
            junk6 = psb("junk6", [128, D], F32)
            st6 = psb("st6", [128, 8], F32)
            dscr6 = psb("dscr6", [128, D], F32)
            ot6 = [psb(f"ot6{i}", [128, D], F32) for i in range(2)]
            @block.sync
            def _(sy):
                W(sy, "P5YS", 16 * NT * E)
                W(sy, "P3H1S", 16 * NT)
                for t in range(NT):
                    b = t % 2
                    if t >= 2:
                        W(sy, "P6A", t - 1)
                    dinc("P6L", sy.dma_start(out=yld[b][:], in_=yac_h[t * 128:(t + 1) * 128, :]))
                    dinc("P6L", sy.dma_start(out=hld[b][:], in_=h1_h[t * 128:(t + 1) * 128, :]))
            for t in range(NT):
                b = t % 2
                @block.vector
                def _(v):
                    W(v, "P6L", 32 * (t + 1))
                    inc("P6A", v.tensor_add(s6[:], yld[b][:], hld[b][:]))
                    W(v, "P6A", cur("P6A"))
                    fence(v, v.tensor_reduce(st6[:, 0:1], s6[:], AX.X, ALU.add))
                    fence(v, v.tensor_scalar(st6[:, 1:2], st6[:, 0:1], 1.0 / D, None, ALU.mult))
                    inc("P6D", v.tensor_scalar(dscr6[:], s6[:], st6[:, 1:2], None, ALU.subtract))
                @block.scalar
                def _(sc):
                    W(sc, "P6D", t + 1)
                    inc("P6SQ", sc.activation(junk6[:], dscr6[:], AF.Square, accum_out=st6[:, 2:3]))
                @block.vector
                def _(v):
                    W(v, "P6SQ", t + 1)
                    fence(v, v.tensor_scalar(st6[:, 3:4], st6[:, 2:3], float(D) * 1e-5, None, ALU.add))
                    inc("P6RC", v.reciprocal(st6[:, 4:5], st6[:, 3:4]))
                @block.scalar
                def _(sc):
                    W(sc, "P6RC", t + 1)
                    inc("P6SR", sc.activation(st6[:, 5:6], st6[:, 4:5], AF.Sqrt, scale=float(D)))
                @block.vector
                def _(v):
                    W(v, "P6SR", t + 1)
                    if t >= 2:
                        W(v, "P6S", 16 * (t - 1))
                    inc("P6N", v.tensor_scalar(ot6[b][:], dscr6[:], st6[:, 5:6], None, ALU.mult))
                @block.scalar
                def _(sc):
                    W(sc, "P6N", t + 1)
                    dinc("P6S", sc.dma_start(out=out_h[t * 128:(t + 1) * 128, :], in_=ot6[b][:]))
            @block.gpsimd
            def _(g):
                W(g, "P6S", 16 * NT)

    if dump:
        barrier()
        @block.sync
        def _(sy):
            pairs = {"d_xT": xT_h, "d_vtok": vtok_h, "d_aoT": aoT_h, "d_h1": h1_h,
                     "d_hT": hT_h, "d_yac": yac_h}
            for name, ap in dump_h.items():
                if name in pairs:
                    dinc("DMP", sy.dma_start(out=ap, in_=pairs[name]))
                elif name == "d_wgB":
                    dinc("DMP", sy.dma_start(out=ap, in_=wgB_h[0]))
        if "d_cw" in dump_h:
            @block.scalar
            def _(sc):
                dinc("DMP", sc.dma_start(out=dump_h["d_cw"], in_=cwsb[:]))
        @block.gpsimd
        def _(g):
            W(g, "DMP", cur("DMP"))

    for nm, val in waits_log:
        if val > cur(nm):
            raise RuntimeError(f"wait on {nm} for {val} but only {cur(nm)} increments emitted")
    es.close()
    return nc


def _host_prep(inp):
    f32 = np.float32
    x = np.ascontiguousarray(np.asarray(inp["L0_attn_x"], f32))
    wqT = np.ascontiguousarray(np.asarray(inp["L0_attn_wq"], f32).T)
    wkT = np.ascontiguousarray(np.asarray(inp["L0_attn_wk"], f32).T)
    wvT = np.ascontiguousarray(np.asarray(inp["L0_attn_wv"], f32).T)
    woT = np.ascontiguousarray(np.asarray(inp["L0_attn_wo"], f32).T)
    gwT = np.ascontiguousarray(np.asarray(inp["L0_ffn_gate_w"], f32).T)
    wgT = np.ascontiguousarray(np.transpose(np.asarray(inp["L0_ffn_w_gate"], f32), (0, 2, 1)))
    wuT = np.ascontiguousarray(np.transpose(np.asarray(inp["L0_ffn_w_up"], f32), (0, 2, 1)))
    wdT = np.ascontiguousarray(np.transpose(np.asarray(inp["L0_ffn_w_down"], f32), (0, 2, 1)))
    cos16 = np.asarray(inp["L0_attn_rope_cos"], f32)[:H]
    sin16 = np.asarray(inp["L0_attn_rope_sin"], f32)[:H]
    qnw = np.asarray(inp["L0_attn_q_norm_w"], f32)
    knw = np.asarray(inp["L0_attn_k_norm_w"], f32)

    cosE = np.repeat(cos16, 2, axis=1).reshape(-1)
    sinE = np.repeat(sin16, 2, axis=1).reshape(-1)
    sign = np.tile(np.array([-1.0, 1.0], f32), D // 2)
    swap = lambda w: np.ascontiguousarray(w.reshape(-1, 2)[:, ::-1]).reshape(-1)
    rows = np.stack([
        cosE, sinE, sign,
        np.tile(qnw, H), np.tile(swap(qnw), H),
        np.tile(knw, H), np.tile(swap(knw), H),
        np.ones(D, f32),
    ]).astype(f32)

    import ml_dtypes
    masks = np.zeros((128, 2048), np.float32)
    rr = np.arange(128)[:, None]
    cc = np.arange(512)[None, :]
    for oi, off in enumerate((0, 128, 256, 384)):
        masks[:, oi * 512:(oi + 1) * 512] = (cc >= rr + off)
    masks = masks.astype(ml_dtypes.bfloat16)
    identb = np.eye(128, dtype=np.float32).astype(ml_dtypes.bfloat16)
    identf = np.eye(128, dtype=np.float32)
    onescb = np.ones((128, 1), np.float32).astype(ml_dtypes.bfloat16)
    consts = np.tile(np.array([TANH_SC, EXP_SC, 128.0, float(D)], f32)[None, :], (128, 1))

    shared = dict(wqT=wqT, wkT=wkT, wvT=wvT, woT=woT, gwT=gwT,
                  wgT=wgT, wuT=wuT, wdT=wdT, rows=rows, masks=masks,
                  identb=identb, identf=identf, onescb=onescb, consts=consts)
    return x, shared


def kernel(**inp):
    x, shared = _host_prep(inp)
    nc = build(phases=6)
    in_maps = [dict(shared, x=np.ascontiguousarray(x[b])) for b in range(8)]
    res = run_bass_kernel_spmd(nc, in_maps, core_ids=list(range(8)))
    return np.stack([np.asarray(res.results[b]["out"]) for b in range(8)]).astype(np.float32)
